# version 06 (final)
# speedup vs baseline: 1.2620x; 1.0800x over previous
from contextlib import ExitStack

import numpy as np

import concourse.bass as bass
import concourse.mybir as mybir
from concourse.bass_utils import run_bass_kernel_spmd
from concourse import library_config

F32 = mybir.dt.float32
BF16 = mybir.dt.bfloat16
I32 = mybir.dt.int32
AF = mybir.ActivationFunctionType
ALU = mybir.AluOpType
AX = mybir.AxisListType

TOK = 4096
D = 1024
NT = TOK // 128
FF = 2816
LN_EPS = 1e-5
A_SLABS = 8


def ns_schedule(l0, tol=1e-15, max_iter=24):
    gs = []
    l, u = float(l0), 1.0
    for _ in range(max_iter):
        g = 2.0 / (l + u)
        ylo = g * l * (2 - g * l)
        yhi = g * u * (2 - g * u)
        lo = min(ylo, yhi)
        hi = 1.0 if (g * l <= 1.0 <= g * u) else max(ylo, yhi)
        l, u = lo, hi
        gs.append(g)
        if 1 - l <= tol and u - 1 <= tol:
            break
    return gs


NS64_GAMMAS = ns_schedule(1e-3)
NS8_GAMMAS = ns_schedule(2e-3)

ENGINES = ("sync", "tensor", "vector", "scalar", "gpsimd")


class Prog:
    def __init__(self):
        self.streams = {e: [] for e in ENGINES}

    def em(self, eng, fn):
        self.streams[eng].append(fn)

    def play(self, block):
        def run(fns):
            return lambda e: [f(e) for f in fns]
        block.sync(run(self.streams["sync"]))
        block.tensor(run(self.streams["tensor"]))
        block.vector(run(self.streams["vector"]))
        block.scalar(run(self.streams["scalar"]))
        block.gpsimd(run(self.streams["gpsimd"]))


class Sem:
    def __init__(self, es, nc, name):
        self.h = es.enter_context(nc.semaphore(name))
        self.n = 0

    def bump(self, k=1):
        self.n += k
        return self.n


class K:
    def __init__(self, stages=("A", "B", "C", "D"), debug_taps=False):
        self.stages = stages
        self.debug_taps = debug_taps
        self.nc = bass.Bass("TRN2", target_bir_lowering=False,
                            detect_race_conditions=False)
        self.es = ExitStack()
        self.prog = Prog()
        self.sems = {}
        self._arena_off = None
        self._arena_end = None
        self._alloc_io()
        self._alloc_fixed()

    def sem(self, name):
        if name not in self.sems:
            self.sems[name] = Sem(self.es, self.nc, name)
        return self.sems[name]

    def _dram(self, name, shape, dtype, kind):
        return self.nc.dram_tensor(name, shape, dtype, kind=kind).ap()

    def _alloc_io(self):
        st = self.stages
        ext = lambda n, s, d=F32: self._dram(n, s, d, "ExternalInput")
        first = st[0]
        self.d_x = ext("x", [TOK, D]) if "A" in st else None
        self.d_ln_g = ext("ln_g", [D]) if "A" in st else None
        self.d_ln_b = ext("ln_b", [D]) if "A" in st else None
        for nm, shape in (("Wq", [D, D]), ("Wk", [D, D]), ("Wv", [D, D]),
                          ("Wo", [D, D])):
            setattr(self, "d_" + nm, ext(nm, shape) if "A" in st else None)
        for nm, shape in (("w_gate", [FF, D]), ("w_up", [FF, D]),
                          ("w_down", [D, FF])):
            setattr(self, "d_" + nm, ext(nm, shape) if "B" in st else None)
        for nm, shape in (("Mq", [D, 512]), ("Mk", [D, 512]), ("Mv", [D, D])):
            setattr(self, "d_" + nm, ext(nm, shape) if "C" in st else None)
        self.d_rhs = ext("rhs", [D, D]) if "D" in st else None
        self.d_out = self._dram("out", [TOK, D], F32, "ExternalOutput")
        def scratch(name, dtype, needed_by, produced_by):
            if needed_by not in st and produced_by not in st:
                return None
            if produced_by in st:
                kind = "Internal" if needed_by in st else "ExternalOutput"
            else:
                kind = "ExternalInput"
            return self._dram(name, [TOK, D], dtype, kind)
        self.d_xpl = self._dram("xpl_s", [TOK, D], F32, "Internal") if "A" in st else None
        self.d_x1 = scratch("x1_s", F32, "B", "A")
        self.d_x2 = scratch("x2_s", F32, "CD", "B")
        if self.d_x2 is None and ("C" in st or "D" in st):
            self.d_x2 = self._dram("x2_s", [TOK, D], F32,
                                   "Internal" if "B" in st else "ExternalInput")
        self.d_ln2 = self._dram("ln2_s", [D, TOK], BF16, "Internal") if "C" in st else None
        if "B" in st:
            self.d_wgq = self._dram("wgq_s", [FF, D], BF16, "Internal")
            self.d_wuq = self._dram("wuq_s", [FF, D], BF16, "Internal")
            self.d_wdq = self._dram("wdq_s", [D, FF], BF16, "Internal")
        self.d_attn = scratch("attn_s", BF16, "D", "C")

    def _alloc_fixed(self):
        nc = self.nc
        self.eye128 = nc.alloc_sbuf_tensor("eye128", [128, 128], F32)
        self.eyebf = nc.alloc_sbuf_tensor("eyebf", [128, 128], BF16)
        self.i64b = nc.alloc_sbuf_tensor("i64b", [128, 512], F32)
        self.i16b = nc.alloc_sbuf_tensor("i16b", [16, 128], F32)
        self.sel16 = nc.alloc_sbuf_tensor("sel16", [16, 2], F32)
        self.ones = nc.alloc_sbuf_tensor("ones", [128, 64], F32)
        self.gb = nc.alloc_sbuf_tensor("gb", [128, D], F32)
        self.bb = nc.alloc_sbuf_tensor("bb", [128, D], F32)
        self.gbrow = nc.alloc_sbuf_tensor("gbrow", [1, D], F32)
        self.bbrow = nc.alloc_sbuf_tensor("bbrow", [1, D], F32)
        self.onesrow = nc.alloc_sbuf_tensor("onesrow", [1, 128], F32)
        self.fence_t = nc.alloc_sbuf_tensor("fence_t", [128, 8], F32)
        self.half2 = nc.alloc_sbuf_tensor("half2", [128, 1], F32)
        self.sel2x16 = nc.alloc_sbuf_tensor("sel2x16", [2, 16], F32)
        self.sel64 = nc.alloc_sbuf_tensor("sel64", [128, 2], F32)
        self.sel2x128 = nc.alloc_sbuf_tensor("sel2x128", [2, 128], F32)
        self.pbh = [nc.alloc_psum_tensor(f"pb{i}", [128, 1024], F32)
                    for i in range(4)]
        left = nc._sbuf_addr_for_side("left")
        right = nc._sbuf_addr_for_side("right")
        self._arena_off = (left + 127) // 128 * 128
        self._arena_end = right

    def pbank(self, i, r0=0, r1=128, c0=0, c1=512):
        h = self.pbh[i // 2]
        base = 512 * (i % 2)
        return h[r0:r1, base + c0:base + c1]

    def vfence(self, e):
        e.tensor_scalar_add(self.fence_t[:, :], self.fence_t[:, :], 0.0)

    def at(self, name, shape, dtype, off):
        nbytes = shape[-1] * mybir.dt.size(dtype)
        assert self._arena_off + off + nbytes <= self._arena_end, \
            (name, self._arena_off, off, nbytes, self._arena_end)
        return self.nc.alloc_sbuf_tensor_at(name, shape, dtype,
                                            offset=self._arena_off + off)

    def emit_init(self):
        p = self.prog
        s_ld = self.sem("init_ld")
        s_gp = self.sem("init_gp")
        s_v = self.sem("init_v")
        need_gb = "A" in self.stages

        def sy(e):
            if need_gb:
                e.dma_start(out=self.gbrow[0:1, :],
                            in_=self.d_ln_g[None, :]).then_inc(s_ld.h, 16)
                e.dma_start(out=self.bbrow[0:1, :],
                            in_=self.d_ln_b[None, :]).then_inc(s_ld.h, 16)
        p.em("sync", sy)
        if need_gb:
            s_ld.bump(32)
        ld_snap = s_ld.n

        def gp(e):
            e.memset(self.onesrow[0:1, :], 1.0)
            e.iota(self.eye128[:, :], pattern=[[1, 128]], base=0,
                   channel_multiplier=-1,
                   allow_small_or_imprecise_dtypes=True).then_inc(s_gp.h, 1)
        p.em("gpsimd", gp)
        s_gp.bump(1)
        gp1 = 1
        if need_gb:
            s_mm0 = self.sem("init_mm")

            def te0(e):
                e.wait_ge(s_ld.h, ld_snap)
                e.wait_ge(s_gp.h, 1)
                last = None
                for half, (row, dst) in enumerate(
                        [(self.gbrow, 0), (self.bbrow, 1)]):
                    for c in range(2):
                        last = e.matmul(self.pbank(2 * half + c),
                                        self.onesrow[0:1, :],
                                        row[0:1, 512 * c:512 * c + 512],
                                        start=True, stop=True)
                last.then_inc(s_mm0.h, 1)
            p.em("tensor", te0)
            s_mm0.bump(1)

            def ve0(e):
                e.wait_ge(s_mm0.h, 1)
                ins = None
                for c in range(2):
                    e.tensor_copy(self.gb[:, 512 * c:512 * c + 512],
                                  self.pbank(2 * 0 + c))
                    ins = e.tensor_copy(self.bb[:, 512 * c:512 * c + 512],
                                        self.pbank(2 * 1 + c))
                ins.then_inc(s_gp.h, 2)
            p.em("vector", ve0)
            s_gp.bump(2)

        def ve(e):
            e.wait_ge(s_gp.h, gp1)
            e.memset(self.fence_t[:, :], 0.0)
            e.memset(self.half2[:, :], 2.0)
            e.tensor_single_scalar(self.eye128[:, :], self.eye128[:, :], 0.0,
                                   ALU.is_equal)
            e.tensor_copy(self.eyebf[:, :], self.eye128[:, :])
            for half in range(2):
                src = self.eye128[64 * half:64 * half + 64,
                                  64 * half:64 * half + 64]
                dst = self.i64b[64 * half:64 * half + 64, :].rearrange(
                    "p (s j) -> p s j", j=64)
                e.tensor_copy(dst, src.unsqueeze(1).broadcast_to((64, 8, 64)))
            e.memset(self.i16b[:, :], 0.0)
            e.tensor_copy(
                self.i16b[0:16, :].rearrange("p (s j) -> p s j", j=16),
                self.eye128[0:16, 0:16].unsqueeze(1).broadcast_to((16, 8, 16)))
            e.reduce_sum(self.sel16[0:16, 0:1], self.eye128[0:16, 0:8], AX.X)
            e.reduce_sum(self.sel16[0:16, 1:2], self.eye128[0:16, 8:16], AX.X)
            e.reduce_sum(self.sel64[:, 0:1], self.eye128[:, 0:64], AX.X)
            e.reduce_sum(self.sel64[:, 1:2], self.eye128[:, 64:128], AX.X)
            e.memset(self.ones[:, :], 1.0).then_inc(s_v.h, 1)
        p.em("vector", ve)
        s_v.bump(1)
        iv0 = s_v.n
        if "A" in self.stages:
            s_m0 = self.sem("init_mm2")

            def te1(e, iv0=iv0):
                e.wait_ge(s_v.h, iv0)
                e.transpose(self.pbank(6, 0, 2, 0, 16), self.sel16[0:16, 0:2],
                            self.eye128[0:16, 0:16])
                e.transpose(self.pbank(7, 0, 2, 0, 128), self.sel64[:, 0:2],
                            self.eye128[:, :]).then_inc(s_m0.h, 1)
            p.em("tensor", te1)
            s_m0.bump(1)

            def ve1(e):
                e.wait_ge(s_m0.h, 1)
                e.tensor_copy(self.sel2x16[0:2, 0:16],
                              self.pbank(6, 0, 2, 0, 16))
                e.tensor_copy(self.sel2x128[0:2, 0:128],
                              self.pbank(7, 0, 2, 0, 128)).then_inc(s_v.h, 1)
            p.em("vector", ve1)
            s_v.bump(1)
        self.init_v = s_v.n

    def emit_ln(self, x_ap, out_ap, stat, sq_tmp, sv, ss, fin_ap=None):
        p = self.prog
        n = x_ap.shape[-1]

        def v1(e):
            e.reduce_sum(stat[:, 0:1], x_ap, AX.X)
            self.vfence(e)
            e.tensor_scalar_mul(stat[:, 0:1], stat[:, 0:1], 1.0 / n)
            self.vfence(e)
            e.tensor_scalar(out_ap, x_ap, stat[:, 0:1], None,
                            ALU.subtract).then_inc(sv.h, 1)
        p.em("vector", v1)
        sv.bump(1)
        v1n = sv.n

        def s1(e):
            e.wait_ge(sv.h, v1n)
            e.activation(sq_tmp, out_ap, AF.Square,
                         accum_out=stat[:, 1:2]).then_inc(ss.h, 1)
        p.em("scalar", s1)
        ss.bump(1)
        s1n = ss.n

        def v2(e):
            e.wait_ge(ss.h, s1n)
            e.tensor_scalar(stat[:, 2:3], stat[:, 1:2], 1.0 / n, LN_EPS,
                            ALU.mult, ALU.add)
            self.vfence(e)
            e.tensor_scalar_add(self.fence_t[:, :], self.fence_t[:, :],
                                0.0).then_inc(sv.h, 1)
        p.em("vector", v2)
        sv.bump(1)
        v2n = sv.n

        def s2(e):
            e.wait_ge(sv.h, v2n)
            e.activation(stat[:, 3:4], stat[:, 2:3], AF.Sqrt).then_inc(ss.h, 1)
        p.em("scalar", s2)
        ss.bump(1)
        s2n = ss.n

        def v3(e):
            e.wait_ge(ss.h, s2n)
            self.vfence(e)
            e.reciprocal(stat[:, 3:4], stat[:, 3:4])
            self.vfence(e)
            e.tensor_mul(stat[:, 4:5], stat[:, 3:4], stat[:, 3:4])
            self.vfence(e)
            e.tensor_mul(stat[:, 4:5], stat[:, 4:5], stat[:, 2:3])
            self.vfence(e)
            e.tensor_scalar(stat[:, 4:5], stat[:, 4:5], -0.5, 1.5,
                            ALU.mult, ALU.add)
            self.vfence(e)
            e.tensor_mul(stat[:, 3:4], stat[:, 3:4], stat[:, 4:5])
            self.vfence(e)
            e.tensor_scalar(fin_ap if fin_ap is not None else out_ap,
                            out_ap, stat[:, 3:4], None,
                            ALU.mult).then_inc(sv.h, 1)
        p.em("vector", v3)
        sv.bump(1)
        return sv.n

    def emit_floor_pos(self, e, y_ap, i_tmp, f_tmp, out_ap):
        e.tensor_copy(i_tmp, y_ap)
        e.tensor_copy(f_tmp, i_tmp)
        e.tensor_tensor(out_ap, f_tmp, y_ap, ALU.is_gt)
        e.tensor_sub(out_ap, f_tmp, out_ap)

    def emit_round_i8(self, x_ap, inv_ap, sgn_tmp, f_tmp, i_tmp, y_tmp, q_out,
                      sv, ss, wait_v=None):
        p = self.prog

        def s1(e):
            if wait_v is not None:
                e.wait_ge(sv.h, wait_v)
            e.sign(sgn_tmp, x_ap).then_inc(ss.h, 1)
        p.em("scalar", s1)
        ss.bump(1)
        s1n = ss.n

        def v2(e):
            e.wait_ge(ss.h, s1n)
            e.tensor_mul(y_tmp, x_ap, sgn_tmp)
            e.tensor_scalar(y_tmp, y_tmp, inv_ap, 0.5, ALU.mult, ALU.add)
            self.emit_floor_pos(e, y_tmp, i_tmp, f_tmp, q_out)
        p.em("vector", v2)

    def emit_stage_C(self):
        p = self.prog
        at = self.at
        o = 0
        def nxt(name, shape, dtype):
            nonlocal o
            t = at(name, shape, dtype, o)
            sz = shape[-1] * mybir.dt.size(dtype)
            o += (sz + 31) // 32 * 32
            return t
        Mq_sb = nxt("c_Mq", [128, 8 * 512], BF16)
        Mk_sb = nxt("c_Mk", [128, 8 * 512], BF16)
        Mv_sb = nxt("c_Mv", [128, 8 * 1024], BF16)
        lnw = nxt("c_lnw", [128, 2 * 8 * 512], BF16)
        qT = nxt("c_qT", [128, 4096], BF16)
        kT = nxt("c_kT", [128, 4096], BF16)
        vP = nxt("c_vP", [128, 32 * 256], BF16)
        EB_off = o
        EB = nxt("c_EB", [128, 8 * 4096], BF16)
        oT = nxt("c_oT", [128, 2 * 4096], F32)
        vp = nxt("c_vp", [128, 8 * 128], BF16)
        zac = nxt("c_zac", [128, 16], F32)
        stat = nxt("c_stat", [128, 16], F32)
        oTb = at("c_oTb", [128, 4096], BF16, EB_off + 20480)
        atst = at("c_atst", [128, 8 * 128], BF16, EB_off + 28672)
        x2t = at("c_x2t", [128, 2 * 1024], F32, EB_off)
        ln2f = at("c_ln2f", [128, 1024], F32, EB_off + 8192)
        ln2b = at("c_ln2b", [128, 2 * 1024], BF16, EB_off + 12288)
        sqt = at("c_sqt", [128, 1024], F32, EB_off + 16384)
        tT = at("c_tT", [128, 2 * 1024], BF16, EB_off + 20480)

        s_ld = self.sem("C_ld")
        s_tp = self.sem("C_tp")
        s_v = self.sem("C_v")
        s_s = self.sem("C_s")
        s_mm = self.sem("C_mm")
        s_st = self.sem("C_st")
        s_gld = self.sem("C_gld")
        pb = self.pbank
        pbh = self.pbh
        deps = getattr(self, "dep_C", [])

        c0_v = {}
        c0_tr = {}
        c0_ev = {}
        for i in range(NT):
            par = i % 2
            x2a = x2t[:, par * 1024:par * 1024 + 1024]
            lb = ln2b[:, par * 1024:par * 1024 + 1024]
            tTa = tT[:, par * 1024:par * 1024 + 1024]

            def sy(e, i=i, x2a=x2a):
                if i == 0:
                    for (ds, dv) in deps:
                        e.wait_ge(ds.h, dv)
                if i >= 2:
                    e.wait_ge(s_v.h, c0_v[i - 2])
                e.dma_start(out=x2a, in_=self.d_x2[128 * i:128 * i + 128, :]
                            ).then_inc(s_ld.h, 16)
            p.em("sync", sy)
            s_ld.bump(16)
            ld_snap = s_ld.n

            def vw(e, ld_snap=ld_snap, trp=c0_tr.get(i - 2)):
                e.wait_ge(s_ld.h, ld_snap)
                if trp is not None:
                    e.wait_ge(s_mm.h, trp)
            p.em("vector", vw)
            vend = self.emit_ln(x2a, ln2f[:, :], stat, sqt[:, :], s_v, s_s,
                                fin_ap=lb)
            c0_v[i] = vend

            def tr0(e, i=i, lb=lb, vend=vend, evp=c0_ev.get(i - 2)):
                e.wait_ge(s_v.h, vend)
                if evp is not None:
                    e.wait_ge(s_s.h, evp)
                last = None
                for c in range(8):
                    last = e.matmul(pbh[2 + i % 2][:, 128 * c:128 * c + 128],
                                    lb[:, 128 * c:128 * c + 128],
                                    self.eyebf[:, :], start=True, stop=True)
                last.then_inc(s_mm.h, 1)
            p.em("tensor", tr0)
            s_mm.bump(1)
            c0_tr[i] = s_mm.n

            def ev0(e, i=i, tTa=tTa, trn=c0_tr[i]):
                e.wait_ge(s_mm.h, trn)
                if i >= 2:
                    e.wait_ge(s_st.h, 128 * (i - 1))
                e.activation(tTa, pbh[2 + i % 2][:, :],
                             AF.Copy).then_inc(s_s.h, 1)
            p.em("scalar", ev0)
            s_s.bump(1)
            c0_ev[i] = s_s.n

            def sy2(e, i=i, tTa=tTa, evn=c0_ev[i]):
                e.wait_ge(s_s.h, evn)
                for c in range(8):
                    e.dma_start(
                        out=self.d_ln2[128 * c:128 * c + 128,
                                       128 * i:128 * i + 128],
                        in_=tTa[:, 128 * c:128 * c + 128]).then_inc(s_st.h, 16)
            p.em("sync", sy2)
            s_st.bump(16 * 8)
        c0_done = s_st.n

        def gld(e):
            for (ds, dv) in deps:
                e.wait_ge(ds.h, dv)
            e.dma_start(out=Mq_sb[:, :].rearrange("p (c f) -> p c f", f=512),
                        in_=self.d_Mq.rearrange("(c p) f -> p c f", p=128)
                        ).then_inc(s_gld.h, 16)
            e.dma_start(out=Mk_sb[:, :].rearrange("p (c f) -> p c f", f=512),
                        in_=self.d_Mk.rearrange("(c p) f -> p c f", p=128)
                        ).then_inc(s_gld.h, 16)
            e.dma_start(out=Mv_sb[:, :].rearrange("p (c f) -> p c f", f=1024),
                        in_=self.d_Mv.rearrange("(c p) f -> p c f", p=128)
                        ).then_inc(s_gld.h, 16)
        p.em("gpsimd", gld)
        s_gld.bump(48)
        m_ready = s_gld.n

        cw_mm = {}
        cw_ev = {}
        pair_done = {-1: 0}
        gseq_s = {}
        out_mm = {}
        out_v = {}
        vz_v = {}
        add_v = {}
        store_n = {}

        for pr in range(4):
            for w in range(8):
                wpar = w % 2
                lw = lnw[:, wpar * 4096:wpar * 4096 + 4096]

                def syw(e, pr=pr, w=w, lw=lw):
                    if pr == 0 and w == 0:
                        e.wait_ge(s_st.h, c0_done)
                    idx = pr * 8 + w
                    if idx >= 2:
                        e.wait_ge(s_mm.h, cw_mm[idx - 2])
                    for c in range(8):
                        e.dma_start(
                            out=lw[:, 512 * c:512 * c + 512],
                            in_=self.d_ln2[128 * c:128 * c + 128,
                                           512 * w:512 * w + 512]
                            ).then_inc(s_tp.h, 16)
                p.em("sync", syw)
                s_tp.bump(16 * 8)
                tp_snap = s_tp.n

                def tew(e, pr=pr, w=w, lw=lw, tp_snap=tp_snap):
                    e.wait_ge(s_tp.h, tp_snap)
                    if pr == 0 and w == 0:
                        e.wait_ge(s_gld.h, m_ready)
                    if pr > 0 and w == 0:
                        e.wait_ge(s_st.h, pair_done[pr - 1])
                    idx = pr * 8 + w
                    if idx >= 1:
                        e.wait_ge(s_v.h, cw_ev[idx - 1])
                    for c in range(8):
                        e.matmul(pb(0), Mq_sb[:, 512 * c + 128 * pr:512 * c + 128 * pr + 128],
                                 lw[:, 512 * c:512 * c + 512],
                                 start=(c == 0), stop=(c == 7))
                    for c in range(8):
                        e.matmul(pb(1), Mk_sb[:, 512 * c + 128 * pr:512 * c + 128 * pr + 128],
                                 lw[:, 512 * c:512 * c + 512],
                                 start=(c == 0), stop=(c == 7))
                    last = None
                    for j in range(4):
                        for c in range(8):
                            last = e.matmul(
                                pb(2 + j // 2, 0, 128, 256 * (j % 2),
                                   256 * (j % 2) + 256),
                                lw[:, 512 * c + 128 * j:512 * c + 128 * j + 128],
                                Mv_sb[:, 1024 * c + 256 * pr:1024 * c + 256 * pr + 256],
                                start=(c == 0), stop=(c == 7))
                    last.then_inc(s_mm.h, 1)
                p.em("tensor", tew)
                s_mm.bump(1)
                cw_mm[pr * 8 + w] = s_mm.n
                mmw = s_mm.n

                def vew(e, w=w, mmw=mmw):
                    e.wait_ge(s_mm.h, mmw)
                    e.tensor_copy(qT[:, 512 * w:512 * w + 512], pb(0))
                    e.tensor_copy(kT[:, 512 * w:512 * w + 512], pb(1))
                    ins = e.tensor_copy(vP[:, 1024 * w:1024 * w + 1024],
                                        pbh[1][:, :])
                    ins.then_inc(s_v.h, 1)
                p.em("vector", vew)
                s_v.bump(1)
                cw_ev[pr * 8 + w] = s_v.n
            qkv_v = cw_ev[pr * 8 + 7]

            gseq = 0
            for jb in range(16):
                ebpar = jb % 2
                for t2 in range(2):
                    jt = 2 * jb + t2
                    for g in range(4):
                        myseq = gseq

                        def tes(e, jt=jt, g=g, myseq=myseq, jb=jb, t2=t2,
                                qkv_v=qkv_v, pr=pr):
                            if myseq == 0:
                                e.wait_ge(s_v.h, qkv_v)
                            if myseq >= 2:
                                e.wait_ge(s_s.h, gseq_s[(pr, myseq - 2)])
                            if t2 == 0 and g == 0 and jb >= 1:
                                e.wait_ge(s_v.h, out_v[(pr, jb - 1)])
                            last = None
                            for hh in range(2):
                                psb = pbh[(myseq % 2) + 2 * hh]
                                for n2 in range(2):
                                    last = e.matmul(
                                        psb[:, 512 * n2:512 * n2 + 512],
                                        kT[64 * hh:64 * hh + 64,
                                           128 * jt:128 * jt + 128],
                                        qT[64 * hh:64 * hh + 64,
                                           1024 * g + 512 * n2:1024 * g + 512 * n2 + 512],
                                        start=True, stop=True,
                                        tile_position=(64 * hh, 0))
                            last.then_inc(s_mm.h, 1)
                        p.em("tensor", tes)
                        s_mm.bump(1)
                        mseq = s_mm.n

                        def sce(e, g=g, myseq=myseq, jb=jb, t2=t2,
                                mseq=mseq, ebpar=ebpar, pr=pr):
                            e.wait_ge(s_mm.h, mseq)
                            if t2 == 0 and g == 0:
                                if jb >= 2:
                                    e.wait_ge(s_mm.h, out_mm[(pr, jb - 2)])
                                if jb >= 1:
                                    e.wait_ge(s_v.h, vz_v[(pr, jb - 1)])
                            last = None
                            for hh in range(2):
                                slot = ((ebpar * 2 + t2) * 2 + hh) * 4096
                                last = e.activation(
                                    EB[:, slot + 1024 * g:slot + 1024 * g + 1024],
                                    pbh[(myseq % 2) + 2 * hh][:, :], AF.Exp,
                                    scale=0.125,
                                    accum_out=zac[:, (t2 * 2 + hh) * 4 + g:
                                                  (t2 * 2 + hh) * 4 + g + 1])
                            last.then_inc(s_s.h, 1)
                        p.em("scalar", sce)
                        s_s.bump(1)
                        gseq_s[(pr, myseq)] = s_s.n
                        gseq += 1
                sexp = s_s.n

                def vz(e, jb=jb, ebpar=ebpar, sexp=sexp, pr=pr):
                    e.wait_ge(s_s.h, sexp)
                    if jb >= 2:
                        e.wait_ge(s_mm.h, out_mm[(pr, jb - 2)])
                    ins = None
                    for t2 in range(2):
                        jt = 2 * jb + t2
                        for hh in range(2):
                            zi = (t2 * 2 + hh) * 4
                            e.reduce_sum(stat[:, 0:1], zac[:, zi:zi + 4], AX.X)
                            self.vfence(e)
                            e.reciprocal(stat[:, 1:2], stat[:, 0:1])
                            self.vfence(e)
                            slot = ((ebpar * 2 + t2) * 2 + hh) * 128
                            ins = e.tensor_scalar(
                                vp[:, slot:slot + 128],
                                vP[:, 256 * jt + 128 * hh:256 * jt + 128 * hh + 128],
                                stat[:, 1:2], None, ALU.mult)
                    ins.then_inc(s_v.h, 1)
                p.em("vector", vz)
                s_v.bump(1)
                vz_v[(pr, jb)] = s_v.n
                vpn = s_v.n

                for hh in range(2):
                    for it in range(8):
                        seqi = hh * 8 + it

                        def teo(e, hh=hh, it=it, seqi=seqi, ebpar=ebpar,
                                vpn=vpn, pr=pr, jb=jb):
                            if seqi == 0:
                                e.wait_ge(s_v.h, vpn)
                            if seqi >= 2:
                                e.wait_ge(s_v.h, add_v[(pr, jb, seqi - 2)])
                            last = None
                            for t2 in range(2):
                                slotv = ((ebpar * 2 + t2) * 2 + hh) * 128
                                slote = ((ebpar * 2 + t2) * 2 + hh) * 4096
                                last = e.matmul(
                                    pbh[seqi % 2][:, 0:512],
                                    vp[:, slotv:slotv + 128],
                                    EB[:, slote + 512 * it:slote + 512 * it + 512],
                                    start=(t2 == 0), stop=(t2 == 1))
                            last.then_inc(s_mm.h, 1)
                        p.em("tensor", teo)
                        s_mm.bump(1)
                        mo = s_mm.n

                        def veo(e, jb=jb, hh=hh, it=it, seqi=seqi, mo=mo):
                            e.wait_ge(s_mm.h, mo)
                            dst = oT[:, 4096 * hh + 512 * it:
                                     4096 * hh + 512 * it + 512]
                            if jb == 0:
                                ins = e.tensor_copy(dst, pbh[seqi % 2][:, 0:512])
                            else:
                                ins = e.tensor_add(dst, dst,
                                                   pbh[seqi % 2][:, 0:512])
                            ins.then_inc(s_v.h, 1)
                        p.em("vector", veo)
                        s_v.bump(1)
                        add_v[(pr, jb, seqi)] = s_v.n
                out_mm[(pr, jb)] = s_mm.n
                out_v[(pr, jb)] = s_v.n

            for hh in range(2):
                def vcast(e, hh=hh, trp=store_n.get((pr, hh - 1, "tr"))):
                    if trp is not None:
                        e.wait_ge(s_mm.h, trp)
                    ins = e.tensor_copy(oTb[:, :],
                                        oT[:, 4096 * hh:4096 * hh + 4096])
                    ins.then_inc(s_v.h, 1)
                p.em("vector", vcast)
                s_v.bump(1)
                cv = s_v.n
                for b in range(8):
                    if b >= 2:
                        kprev = (pr, hh, b - 2)
                    elif hh == 1:
                        kprev = (pr, 0, 6 + b)
                    else:
                        kprev = (pr - 1, 1, 6 + b)

                    def trh(e, b=b, cv=cv,
                            evp=store_n.get((kprev[0], kprev[1], "ev",
                                             kprev[2]))):
                        if b == 0:
                            e.wait_ge(s_v.h, cv)
                        if evp is not None:
                            e.wait_ge(s_s.h, evp)
                        last = None
                        for t in range(4):
                            last = e.matmul(
                                pbh[2 + b % 2][:, 128 * t:128 * t + 128],
                                oTb[:, 128 * (4 * b + t):128 * (4 * b + t) + 128],
                                self.eyebf[:, :], start=True, stop=True)
                        last.then_inc(s_mm.h, 1)
                    p.em("tensor", trh)
                    s_mm.bump(1)
                    trn = s_mm.n

                    def evh(e, b=b, trn=trn,
                            stp=store_n.get((kprev[0], kprev[1], "st",
                                             kprev[2]))):
                        e.wait_ge(s_mm.h, trn)
                        if stp is not None:
                            e.wait_ge(s_st.h, stp)
                        e.activation(atst[:, 512 * (b % 2):512 * (b % 2) + 512],
                                     pbh[2 + b % 2][:, 0:512],
                                     AF.Copy).then_inc(s_s.h, 1)
                    p.em("scalar", evh)
                    s_s.bump(1)
                    store_n[(pr, hh, "ev", b)] = s_s.n

                    def sth(e, b=b, pr=pr, hh=hh, evn=s_s.n):
                        e.wait_ge(s_s.h, evn)
                        col = 128 * (2 * pr + hh)
                        for t in range(4):
                            i = 4 * b + t
                            e.dma_start(
                                out=self.d_attn[128 * i:128 * i + 128,
                                                col:col + 128],
                                in_=atst[:, 512 * (b % 2) + 128 * t:
                                         512 * (b % 2) + 128 * t + 128]
                                ).then_inc(s_st.h, 16)
                    p.em("sync", sth)
                    s_st.bump(16 * 4)
                    store_n[(pr, hh, "st", b)] = s_st.n
                store_n[(pr, hh, "tr")] = s_mm.n
            pair_done[pr] = s_st.n

        def syend(e):
            e.wait_ge(s_st.h, s_st.n)
        p.em("sync", syend)
        self.C_done = (s_st, s_st.n)
        if self.debug_taps:
            for nm, t in (("qT", qT), ("kT", kT), ("vP", vP), ("oT", oT),
                          ("zac", zac), ("vp", vp)):
                self.tap(nm, t[tuple(slice(None) for _ in t.shape)],
                         wait_sems=(s_v, s_st, s_tp, s_s))

    def emit_stage_A(self):
        p = self.prog
        at = self.at
        o = 0
        offs = {}
        def nxt(name, shape, dtype):
            nonlocal o
            offs[name] = o
            t = at(name, shape, dtype, o)
            sz = shape[-1] * mybir.dt.size(dtype)
            o += (sz + 31) // 32 * 32
            return t
        Wq_bf = nxt("a_Wq", [128, 8 * 1024], BF16)
        Wk_bf = nxt("a_Wk", [128, 8 * 1024], BF16)
        Wv_bf = nxt("a_Wv", [128, 8 * 1024], BF16)
        Wo_bf = nxt("a_Wo", [128, 8 * 1024], BF16)
        xnT = nxt("a_xnT", [128, 8 * 512], BF16)
        qTs = nxt("a_qTs", [128, 2 * 4096], BF16)
        kS = nxt("a_kS", [128, 2 * 4096], BF16)
        vS = nxt("a_vS", [128, 2 * 4096], BF16)
        Sb = nxt("a_Sb", [128, 512], F32)
        Gb = nxt("a_Gb", [128, 512], F32)
        Hb = nxt("a_Hb", [128, 512], F32)
        Xb = nxt("a_Xb", [128, 512], F32)
        T2b = nxt("a_T2b", [128, 512], F32)
        Wb = nxt("a_Wb", [128, 512], F32)
        WTb = T2b
        Sbf = nxt("a_Sbf", [128, 512], BF16)
        dcol = nxt("a_dcol", [128, 8], F32)
        Bbd = nxt("a_Bbd", [128, 128], F32)
        N16b = nxt("a_N16b", [16, 128], F32)
        X8b = nxt("a_X8b", [16, 128], F32)
        T8b = nxt("a_T8b", [16, 128], F32)
        d16 = nxt("a_d16", [16, 8], F32)
        C1b = nxt("a_C1b", [16, 512], F32)
        D1b = nxt("a_D1b", [16, 512], F32)
        BTb = at("a_BTb", [16, 1024], F32, offs["a_Gb"])
        alph = nxt("a_alph", [2, 8], F32)
        al8 = alph
        oTb = nxt("a_oTb", [128, 1024], BF16)
        xt = nxt("a_xt", [128, 2 * 1024], F32)
        ln0 = nxt("a_ln0", [128, 1024], F32)
        xnf = nxt("a_xnf", [128, 1024], F32)
        sq = nxt("a_sq", [128, 1024], F32)
        xnb = nxt("a_xnb", [128, 1024], BF16)
        xplo = nxt("a_xplo", [128, 1024], F32)
        xplt = nxt("a_xplt", [128, 4 * 1024], F32)
        x1o = nxt("a_x1o", [128, 1024], F32)
        stat = dcol

        s_ld = self.sem("A_ld")
        s_sp = self.sem("A_sp")
        s_xp = self.sem("A_xp")
        s_tp = self.sem("A_tp")
        s_v = self.sem("A_v")
        s_s = self.sem("A_s")
        s_mm = self.sem("A_mm")
        s_st = self.sem("A_st")
        s_gld = self.sem("A_gld")
        pb = self.pbank
        pbh = self.pbh
        iv = self.init_v
        s_iv = self.sem("init_v")

        G64 = [float(g) for g in NS64_GAMMAS] + [1.0]
        G8 = [float(g) for g in ns_schedule(5e-3)] + [1.0]

        def gld(e):
            for w_d, w_sb in ((self.d_Wq, Wq_bf), (self.d_Wk, Wk_bf),
                              (self.d_Wv, Wv_bf), (self.d_Wo, Wo_bf)):
                e.dma_start(out=w_sb[:, :].rearrange("p (c f) -> p c f", f=1024),
                            in_=w_d.rearrange("(c p) f -> p c f", p=128)
                            ).then_inc(s_gld.h, 16)
        p.em("gpsimd", gld)
        s_gld.bump(64)
        w_ready = s_gld.n

        def v0(e, iv=iv):
            e.wait_ge(s_iv.h, iv)
            e.memset(Sb[:, :], 0.0).then_inc(s_v.h, 1)
        p.em("vector", v0)
        s_v.bump(1)
        consts_v = s_v.n

        a0_v = {}
        a1_mm = {}
        a1_ev = {}
        scan_mm = {}
        x1_v = {}

        for s in range(A_SLABS):
            for j in range(4):
                i = 4 * s + j
                par = i % 2
                xta = xt[:, 1024 * par:1024 * par + 1024]
                xpa = xplo[:, 0:1024]

                def syx(e, i=i, xta=xta):
                    if i >= 2:
                        e.wait_ge(s_sp.h, 16 * (i - 1))
                    e.dma_start(out=xta, in_=self.d_x[128 * i:128 * i + 128, :]
                                ).then_inc(s_ld.h, 16)
                p.em("sync", syx)
                s_ld.bump(16)
                lds = s_ld.n

                def vx(e, lds=lds, i=i):
                    e.wait_ge(s_ld.h, lds)
                    if i == 0:
                        e.wait_ge(s_v.h, consts_v)
                p.em("vector", vx)
                self.emit_ln(xta, ln0[:, :], stat, sq[:, :], s_v, s_s)
                self.emit_ln(ln0[:, :], xnf[:, :], stat, sq[:, :], s_v, s_s)

                stp = s_sp.n

                def vaff(e, xta=xta, xpa=xpa, i=i, stp=stp):
                    if i >= 1:
                        e.wait_ge(s_sp.h, stp)
                    e.tensor_mul(xnf[:, :], xnf[:, :], self.gb[:, :])
                    e.tensor_add(xnb[:, :], xnf[:, :], self.bb[:, :])
                    ins = e.tensor_add(xpa, xta, ln0[:, :])
                    ins.then_inc(s_v.h, 1)
                p.em("vector", vaff)
                s_v.bump(1)
                a0_v[i] = s_v.n

                def syp(e, i=i, xpa=xpa, av=a0_v[i]):
                    e.wait_ge(s_v.h, av)
                    e.dma_start(out=self.d_xpl[128 * i:128 * i + 128, :],
                                in_=xpa).then_inc(s_sp.h, 16)
                p.em("sync", syp)
                s_sp.bump(16)

                def syt(e, j=j, av=a0_v[i], s=s, i=i):
                    e.wait_ge(s_v.h, av)
                    if s >= 1 and j == 0:
                        e.wait_ge(s_mm.h, a1_mm[s - 1])
                    for c in range(8):
                        e.dma_start_transpose(
                            xnT[:, 512 * c + 128 * j:512 * c + 128 * j + 128],
                            xnb[:, 128 * c:128 * c + 128]).then_inc(s_tp.h, 16)
                p.em("scalar", syt)
                s_tp.bump(16 * 8)
            tpA = s_tp.n
            spar = s % 2

            def teA(e, s=s, spar=spar, tpA=tpA):
                e.wait_ge(s_tp.h, tpA)
                if s == 0:
                    e.wait_ge(s_gld.h, w_ready)
                if s >= 2:
                    e.wait_ge(s_mm.h, scan_mm[4 * (s - 1) - 1 + 4])
                last = None
                for dt in range(8):
                    bk = 6 + dt % 2
                    if dt >= 2:
                        e.wait_ge(s_s.h, a1_ev.get((s, dt - 2), 0))
                    for c in range(8):
                        last = e.matmul(pb(bk),
                                        Wq_bf[:, 1024 * c + 128 * dt:1024 * c + 128 * dt + 128],
                                        xnT[:, 512 * c:512 * c + 512],
                                        start=(c == 0), stop=(c == 7))
                    last.then_inc(s_mm.h, 1)
            for dt in range(8):
                def teQ(e, s=s, dt=dt, tpA=tpA, spar=spar):
                    if dt == 0:
                        e.wait_ge(s_tp.h, tpA)
                        if s == 0:
                            e.wait_ge(s_gld.h, w_ready)
                        if s >= 2:
                            e.wait_ge(s_v.h, x1_v[4 * s - 5])
                    if dt >= 2:
                        e.wait_ge(s_s.h, a1_ev[(s, dt - 2)])
                    bk = 6 + dt % 2
                    last = None
                    for c in range(8):
                        last = e.matmul(pb(bk),
                                        Wq_bf[:, 1024 * c + 128 * dt:1024 * c + 128 * dt + 128],
                                        xnT[:, 512 * c:512 * c + 512],
                                        start=(c == 0), stop=(c == 7))
                    last.then_inc(s_mm.h, 1)
                p.em("tensor", teQ)
                s_mm.bump(1)
                mq = s_mm.n

                def scQ(e, dt=dt, spar=spar, mq=mq):
                    e.wait_ge(s_mm.h, mq)
                    bk = 6 + dt % 2
                    e.activation(qTs[:, 4096 * spar + 512 * dt:4096 * spar + 512 * dt + 512],
                                 pb(bk), AF.Copy, scale=0.125).then_inc(s_s.h, 1)
                p.em("scalar", scQ)
                s_s.bump(1)
                a1_ev[(s, dt)] = s_s.n
            for kv, (Wsb, dst) in enumerate(((Wk_bf, kS), (Wv_bf, vS))):
                for j in range(4):
                    for half in range(2):
                        idx = 8 + kv * 8 + j * 2 + half

                        def teK(e, j=j, half=half, Wsb=Wsb, spar=spar,
                                kv=kv, s=s, idx=idx):
                            bk = 6 + idx % 2
                            e.wait_ge(s_s.h, a1_ev[(s, idx - 2)])
                            last = None
                            for c in range(8):
                                last = e.matmul(
                                    pb(bk),
                                    xnT[:, 512 * c + 128 * j:512 * c + 128 * j + 128],
                                    Wsb[:, 1024 * c + 512 * half:1024 * c + 512 * half + 512],
                                    start=(c == 0), stop=(c == 7))
                            last.then_inc(s_mm.h, 1)
                        p.em("tensor", teK)
                        s_mm.bump(1)
                        mk = s_mm.n

                        def scK(e, j=j, half=half, dst=dst, spar=spar,
                                mk=mk, idx=idx):
                            e.wait_ge(s_mm.h, mk)
                            bk = 6 + idx % 2
                            e.activation(
                                dst[:, 4096 * spar + 1024 * j + 512 * half:
                                    4096 * spar + 1024 * j + 512 * half + 512],
                                pb(bk), AF.Copy).then_inc(s_s.h, 1)
                        p.em("scalar", scK)
                        s_s.bump(1)
                        a1_ev[(s, idx)] = s_s.n
            a1_mm[s] = s_mm.n
            a1_done_s = a1_ev[(s, 23)]

            for j in range(4):
                i = 4 * s + j

                def syxp(e, i=i, j=j):
                    if i >= 4:
                        e.wait_ge(s_v.h, x1_v.get(i - 4, 0))
                    e.wait_ge(s_sp.h, 16 * (i + 1))
                    e.dma_start(out=xplt[:, 1024 * (i % 4):1024 * (i % 4) + 1024],
                                in_=self.d_xpl[128 * i:128 * i + 128, :]
                                ).then_inc(s_xp.h, 16)
                p.em("sync", syxp)
                s_xp.bump(16)
                x1_v[("xp", i)] = s_xp.n

            for j in range(4):
                t = 4 * s + j
                self._emit_scan_step(
                    t, s, j, spar, a1_done_s, G64, G8,
                    dict(qTs=qTs, kS=kS, vS=vS, Sb=Sb, Gb=Gb, Hb=Hb, Xb=Xb,
                         T2b=T2b, Wb=Wb, WTb=WTb, Sbf=Sbf,
                         dcol=dcol, Bbd=Bbd, N16b=N16b, X8b=X8b, T8b=T8b,
                         d16=d16, C1b=C1b, D1b=D1b, BTb=BTb, alph=alph,
                         al8=al8, oTb=oTb, Wo_bf=Wo_bf, xplt=xplt, x1o=x1o),
                    (s_v, s_s, s_mm, s_st, s_xp), scan_mm, x1_v)

        def syend(e):
            e.wait_ge(s_st.h, s_st.n)
        p.em("sync", syend)
        self.A_done = (s_st, s_st.n)
        if self.debug_taps:
            for nm, tt in (("Sb", Sb), ("Gb", Gb), ("Xb", Xb), ("Wb", Wb),
                           ("Bbd", Bbd), ("N16b", N16b), ("X8b", X8b),
                           ("C1b", C1b), ("alph", alph), ("al8", al8),
                           ("qTs", qTs), ("kS", kS), ("vS", vS), ("oTb", oTb)):
                self.tap(nm, tt[tuple(slice(None) for _ in tt.shape)],
                         wait_sems=(s_v, s_st, s_s))

    def _emit_scan_step(self, t, s, j, spar, a1_done, G64, G8, B, sems,
                        scan_mm, x1_v):
        p = self.prog
        s_v, s_s, s_mm, s_st, s_xp = sems
        pb = self.pbank
        pbh = self.pbh
        qTs, kS, vS = B["qTs"], B["kS"], B["vS"]
        Sb, Gb, Hb, Xb, T2b = B["Sb"], B["Gb"], B["Hb"], B["Xb"], B["T2b"]
        Wb, WTb, Sbf = B["Wb"], B["WTb"], B["Sbf"]
        dcol, Bbd, N16b, X8b = B["dcol"], B["Bbd"], B["N16b"], B["X8b"]
        T8b, d16, C1b, D1b, BTb = B["T8b"], B["d16"], B["C1b"], B["D1b"], B["BTb"]
        alph, al8, oTb, Wo_bf = B["alph"], B["al8"], B["oTb"], B["Wo_bf"]
        xplt, x1o = B["xplt"], B["x1o"]

        def lane(r, c):
            return (64 * r, 64 * c)

        def Kc(h):
            return kS[64 * 0:128, 4096 * spar + 1024 * j + 64 * h:
                      4096 * spar + 1024 * j + 64 * h + 64]

        def Vc(h):
            return vS[:, 4096 * spar + 1024 * j + 64 * h:
                      4096 * spar + 1024 * j + 64 * h + 64]

        def QcT(h):
            r, c = h % 2, h // 2
            return qTs[64 * r:64 * r + 64,
                       4096 * spar + 512 * c + 128 * j:
                       4096 * spar + 512 * c + 128 * j + 128]

        def lslice(buf, h, w=64):
            r, c = h % 2, h // 2
            return buf[64 * r:64 * r + 64, w * c:w * c + w]

        def TEphase(name, fn, wait_v=None, wait_s=None, extra=None):
            def te(e, fn=fn, wait_v=wait_v, wait_s=wait_s, extra=extra):
                if wait_v is not None:
                    e.wait_ge(s_v.h, wait_v)
                if wait_s is not None:
                    e.wait_ge(s_s.h, wait_s)
                if extra is not None:
                    extra(e)
                last = fn(e)
                last.then_inc(s_mm.h, 1)
            p.em("tensor", te)
            s_mm.bump(1)
            return s_mm.n

        def p1(e):
            last = None
            for h in range(16):
                r, c = h % 2, h // 2
                e.matmul(pb(0, 64 * r, 64 * r + 64, 64 * c, 64 * c + 64),
                         Kc(h), Kc(h), start=True, stop=True,
                         tile_position=(0, 64 * r))
            for h in range(16):
                r, c = h % 2, h // 2
                last = e.matmul(pb(1, 64 * r, 64 * r + 64, 64 * c, 64 * c + 64),
                                Kc(h), Vc(h), start=True, stop=True,
                                tile_position=(0, 64 * r))
            return last
        def p1_extra(e, t=t):
            e.wait_ge(s_s.h, a1_done)
            if t >= 1:
                e.wait_ge(s_v.h, x1_v[t - 1])
        m1 = TEphase("GH", p1, extra=p1_extra)

        def v2(e, m1=m1):
            e.wait_ge(s_mm.h, m1)
            e.scalar_tensor_tensor(Gb[:, :], self.i64b[:, :], 1e-4,
                                   pb(0, 0, 128, 0, 512), ALU.mult,
                                   ALU.add).then_inc(s_v.h, 1)
        p.em("vector", v2)
        s_v.bump(1)
        v2n = s_v.n

        def s2(e, m1=m1):
            e.wait_ge(s_mm.h, m1)
            e.activation(Hb[:, :], pb(1, 0, 128, 0, 512), AF.Copy).then_inc(s_s.h, 1)
        p.em("scalar", s2)
        s_s.bump(1)
        s2n = s_s.n

        def p3(e):
            last = None
            for h in range(16):
                r, c = h % 2, h // 2
                last = e.matmul(pb(2, 64 * r, 64 * r + 64, 64 * c, 64 * c + 64),
                                lslice(Gb, h), lslice(Gb, h), start=True,
                                stop=True, tile_position=(64 * r, 64 * r))
            return last
        m3 = TEphase("G2", p3, wait_v=v2n)

        def v4(e, m3=m3):
            e.wait_ge(s_mm.h, m3)
            e.tensor_mul(T2b[:, :], pb(2, 0, 128, 0, 512), self.i64b[:, :])
            e.tensor_reduce(dcol[:, 0:8],
                            T2b[:, :].rearrange("p (c f) -> p c f", f=64),
                            AX.X, ALU.add)
            self.vfence(e)
            e.tensor_scalar_add(self.fence_t[:, :], self.fence_t[:, :],
                                0.0).then_inc(s_v.h, 1)
        p.em("vector", v4)
        s_v.bump(1)
        v4n = s_v.n

        def p5(e):
            return e.matmul(pb(3, 0, 2, 0, 8), self.sel64[:, 0:2],
                            dcol[:, 0:8], start=True, stop=True)
        m5 = TEphase("tr", p5, wait_v=v4n)

        def s6(e, m5=m5):
            e.wait_ge(s_mm.h, m5)
            e.activation(alph[0:2, 0:8], pb(3, 0, 2, 0, 8), AF.Sqrt,
                         scale=1.0101).then_inc(s_s.h, 1)
        p.em("scalar", s6)
        s_s.bump(1)
        s6n = s_s.n

        def v7(e, s6n=s6n):
            e.wait_ge(s_s.h, s6n)
            self.vfence(e)
            e.reciprocal(alph[0:2, 0:8], alph[0:2, 0:8])
            self.vfence(e)
            e.tensor_scalar_add(self.fence_t[:, :], self.fence_t[:, :],
                                0.0).then_inc(s_v.h, 1)
        p.em("vector", v7)
        s_v.bump(1)
        v7n = s_v.n

        def p8(e):
            return e.matmul(pb(3, 0, 128, 0, 512), self.sel2x128[0:2, 0:128],
                            alph[0:2, 0:8].unsqueeze(-1).broadcast_to((2, 8, 64)),
                            start=True, stop=True)
        m8 = TEphase("ab", p8, wait_v=v7n)

        def v9(e, m8=m8, g0=G64[0]):
            e.wait_ge(s_mm.h, m8)
            e.tensor_mul(Gb[:, :], Gb[:, :], pb(3, 0, 128, 0, 512))
            e.tensor_mul(Hb[:, :], Hb[:, :], pb(3, 0, 128, 0, 512))
            e.tensor_scalar_mul(Xb[:, :], Gb[:, :], -g0 * g0)
            e.scalar_tensor_tensor(Xb[:, :], self.i64b[:, :], 2.0 * g0,
                                   Xb[:, :], ALU.mult, ALU.add).then_inc(s_v.h, 1)
        p.em("vector", v9)
        s_v.bump(1)
        lastv = s_v.n
        lasts = s2n

        for g in G64[1:]:
            def pa(e):
                last = None
                for h in range(16):
                    r, c = h % 2, h // 2
                    last = e.matmul(pb(2, 64 * r, 64 * r + 64, 64 * c, 64 * c + 64),
                                    lslice(Gb, h), lslice(Xb, h), start=True,
                                    stop=True, tile_position=(64 * r, 64 * r))
                return last
            ma = TEphase("nsT", pa, wait_v=lastv, wait_s=lasts)

            def vb(e, ma=ma, g=g):
                e.wait_ge(s_mm.h, ma)
                e.tensor_scalar_mul(T2b[:, :], pb(2, 0, 128, 0, 512), -g * g)
                e.scalar_tensor_tensor(T2b[:, :], self.i64b[:, :], 2.0 * g,
                                       T2b[:, :], ALU.mult, ALU.add).then_inc(s_v.h, 1)
            p.em("vector", vb)
            s_v.bump(1)
            vbn = s_v.n

            def pc(e):
                last = None
                for h in range(16):
                    r, c = h % 2, h // 2
                    last = e.matmul(pb(3, 64 * r, 64 * r + 64, 64 * c, 64 * c + 64),
                                    lslice(Xb, h), lslice(T2b, h), start=True,
                                    stop=True, tile_position=(64 * r, 64 * r))
                return last
            mc = TEphase("nsX", pc, wait_v=vbn)

            def sd(e, mc=mc):
                e.wait_ge(s_mm.h, mc)
                e.activation(Xb[:, :], pb(3, 0, 128, 0, 512),
                             AF.Copy).then_inc(s_s.h, 1)
            p.em("scalar", sd)
            s_s.bump(1)
            lasts = s_s.n
            lastv = vbn

        def p11(e):
            last = None
            for h in range(16):
                r, c = h % 2, h // 2
                last = e.matmul(pb(2, 64 * r, 64 * r + 64, 64 * c, 64 * c + 64),
                                lslice(Xb, h), lslice(Hb, h), start=True,
                                stop=True, tile_position=(64 * r, 64 * r))
            return last
        m11 = TEphase("W", p11, wait_s=lasts)

        def s12(e, m11=m11):
            e.wait_ge(s_mm.h, m11)
            e.activation(Wb[:, :], pb(2, 0, 128, 0, 512), AF.Copy).then_inc(s_s.h, 1)
        p.em("scalar", s12)
        s_s.bump(1)
        s12n = s_s.n

        def p13(e):
            last = None
            for h in range(16):
                r, c = h % 2, h // 2
                last = e.matmul(pb(3, 64 * r, 64 * r + 64, 64 * c, 64 * c + 64),
                                lslice(Hb, h), lslice(Xb, h), start=True,
                                stop=True, tile_position=(64 * r, 64 * r))
            return last
        m13 = TEphase("WT", p13, wait_s=s12n)

        def s14(e, m13=m13):
            e.wait_ge(s_mm.h, m13)
            e.activation(WTb[:, :], pb(3, 0, 128, 0, 512), AF.Copy).then_inc(s_s.h, 1)
        p.em("scalar", s14)
        s_s.bump(1)
        s14n = s_s.n

        def p15(e):
            last = None
            for h in range(16):
                r, c = h % 2, h // 2
                last = e.matmul(pb(0, 64 * r, 64 * r + 64, 8 * c, 8 * c + 8),
                                lslice(WTb, h),
                                Wb[64 * r:64 * r + 64, 64 * c:64 * c + 8],
                                start=True, stop=True,
                                tile_position=(64 * r, 64 * r))
            return last
        m15 = TEphase("B", p15, wait_s=s14n)

        def v16(e, m15=m15):
            e.wait_ge(s_mm.h, m15)
            e.memset(Bbd[:, :], 0.0)
            for r in range(2):
                e.tensor_copy(
                    Bbd[64 * r:64 * r + 64, 8 * r:8 * r + 8].rearrange(
                        "p f -> p 1 f").broadcast_to((64, 1, 8))
                    if False else
                    Bbd[64 * r:64 * r + 64, :].rearrange(
                        "p (pr f) -> p pr f", f=16)[:, :, 8 * r:8 * r + 8],
                    pb(0, 64 * r, 64 * r + 64, 0, 64).rearrange(
                        "p (pr f) -> p pr f", f=8))
            e.tensor_scalar_add(self.fence_t[:, :], self.fence_t[:, :],
                                0.0).then_inc(s_v.h, 1)
        p.em("vector", v16)
        s_v.bump(1)
        v16n = s_v.n

        def p17(e):
            last = None
            for pr in range(8):
                last = e.matmul(pb(1, 0, 16, 16 * pr, 16 * pr + 16),
                                Bbd[:, 16 * pr:16 * pr + 16],
                                Bbd[:, 16 * pr:16 * pr + 16],
                                start=True, stop=True)
            return last
        m17 = TEphase("N16", p17, wait_v=v16n)

        def v18(e, m17=m17):
            e.wait_ge(s_mm.h, m17)
            e.tensor_copy(N16b[0:16, :], pb(1, 0, 16, 0, 128)).then_inc(s_v.h, 1)
        p.em("vector", v18)
        s_v.bump(1)
        v18a = s_v.n

        def p17b(e):
            last = None
            for pr in range(8):
                last = e.matmul(pb(0, 0, 16, 16 * pr, 16 * pr + 16),
                                N16b[0:16, 16 * pr:16 * pr + 16],
                                N16b[0:16, 16 * pr:16 * pr + 16],
                                start=True, stop=True)
            return last
        m17b = TEphase("N16sq", p17b, wait_v=v18a)

        def v18b(e, m17b=m17b):
            e.wait_ge(s_mm.h, m17b)
            e.tensor_mul(T8b[0:16, :], pb(0, 0, 16, 0, 128), self.i16b[0:16, :])
            e.tensor_reduce(d16[0:16, 0:8],
                            T8b[0:16, :].rearrange("p (c f) -> p c f", f=16),
                            AX.X, ALU.add)
            self.vfence(e)
            e.tensor_scalar_add(self.fence_t[:, :], self.fence_t[:, :],
                                0.0).then_inc(s_v.h, 1)
        p.em("vector", v18b)
        s_v.bump(1)
        v18n = s_v.n

        def p19(e):
            return e.matmul(pb(2, 0, 2, 0, 8), self.sel16[0:16, 0:2],
                            d16[0:16, 0:8], start=True, stop=True)
        m19 = TEphase("tr16", p19, wait_v=v18n)

        def s20(e, m19=m19):
            e.wait_ge(s_mm.h, m19)
            e.activation(al8[0:2, 0:8], pb(2, 0, 2, 0, 8), AF.Sqrt,
                         scale=1.0101).then_inc(s_s.h, 1)
        p.em("scalar", s20)
        s_s.bump(1)
        s20n = s_s.n

        def v21(e, s20n=s20n):
            e.wait_ge(s_s.h, s20n)
            self.vfence(e)
            e.reciprocal(al8[0:2, 0:8], al8[0:2, 0:8])
            self.vfence(e)
            e.tensor_scalar_add(self.fence_t[:, :], self.fence_t[:, :],
                                0.0).then_inc(s_v.h, 1)
        p.em("vector", v21)
        s_v.bump(1)
        v21n = s_v.n

        def p22(e):
            e.matmul(pb(2, 0, 16, 0, 128), self.sel2x16[0:2, 0:16],
                     al8[0:2, 0:8].unsqueeze(-1).broadcast_to((2, 8, 16)),
                     start=True, stop=True)
            return e.matmul(pb(3, 0, 16, 0, 512), self.sel2x16[0:2, 0:16],
                            al8[0:2, 0:8].unsqueeze(-1).broadcast_to((2, 8, 64)),
                            start=True, stop=True)
        m22 = TEphase("ab16", p22, wait_v=v21n)

        def v23(e, m22=m22, h0=G8[0]):
            e.wait_ge(s_mm.h, m22)
            e.tensor_mul(N16b[0:16, :], N16b[0:16, :], pb(2, 0, 16, 0, 128))
            e.tensor_scalar_mul(X8b[0:16, :], N16b[0:16, :], -h0 * h0)
            e.scalar_tensor_tensor(X8b[0:16, :], self.i16b[0:16, :], 2.0 * h0,
                                   X8b[0:16, :], ALU.mult, ALU.add).then_inc(s_v.h, 1)
        p.em("vector", v23)
        s_v.bump(1)
        lastv8 = s_v.n
        lasts8 = None

        for g in G8[1:]:
            def pa8(e):
                last = None
                for pr in range(8):
                    last = e.matmul(pb(0, 0, 16, 16 * pr, 16 * pr + 16),
                                    N16b[0:16, 16 * pr:16 * pr + 16],
                                    X8b[0:16, 16 * pr:16 * pr + 16],
                                    start=True, stop=True)
                return last
            ma8 = TEphase("ns8T", pa8, wait_v=lastv8, wait_s=lasts8)

            def vb8(e, ma8=ma8, g=g):
                e.wait_ge(s_mm.h, ma8)
                e.tensor_scalar_mul(T8b[0:16, :], pb(0, 0, 16, 0, 128), -g * g)
                e.scalar_tensor_tensor(T8b[0:16, :], self.i16b[0:16, :],
                                       2.0 * g, T8b[0:16, :], ALU.mult,
                                       ALU.add).then_inc(s_v.h, 1)
            p.em("vector", vb8)
            s_v.bump(1)
            vb8n = s_v.n

            def pc8(e):
                last = None
                for pr in range(8):
                    last = e.matmul(pb(1, 0, 16, 16 * pr, 16 * pr + 16),
                                    X8b[0:16, 16 * pr:16 * pr + 16],
                                    T8b[0:16, 16 * pr:16 * pr + 16],
                                    start=True, stop=True)
                return last
            mc8 = TEphase("ns8X", pc8, wait_v=vb8n)

            def sd8(e, mc8=mc8):
                e.wait_ge(s_mm.h, mc8)
                e.activation(X8b[0:16, :], pb(1, 0, 16, 0, 128),
                             AF.Copy).then_inc(s_s.h, 1)
            p.em("scalar", sd8)
            s_s.bump(1)
            lasts8 = s_s.n
            lastv8 = vb8n

        def p25(e):
            last = None
            for pr in range(8):
                last = e.matmul(pb(0, 0, 16, 64 * pr, 64 * pr + 64),
                                Bbd[:, 16 * pr:16 * pr + 16],
                                Wb[:, 64 * pr:64 * pr + 64],
                                start=True, stop=True)
            return last
        m25 = TEphase("C1", p25, wait_s=lasts8)

        def s26(e, m25=m25):
            e.wait_ge(s_mm.h, m25)
            e.activation(C1b[0:16, :], pb(0, 0, 16, 0, 512),
                         AF.Copy).then_inc(s_s.h, 1)
        p.em("scalar", s26)
        s_s.bump(1)
        s26n = s_s.n

        def v26(e, s26n=s26n):
            e.wait_ge(s_s.h, s26n)
            e.tensor_mul(C1b[0:16, :], C1b[0:16, :],
                         pb(3, 0, 16, 0, 512)).then_inc(s_v.h, 1)
        p.em("vector", v26)
        s_v.bump(1)
        v26n = s_v.n

        def p27(e):
            last = None
            for pr in range(8):
                last = e.matmul(pb(1, 0, 16, 64 * pr, 64 * pr + 64),
                                X8b[0:16, 16 * pr:16 * pr + 16],
                                C1b[0:16, 64 * pr:64 * pr + 64],
                                start=True, stop=True)
            return last
        m27 = TEphase("D1", p27, wait_v=v26n)

        def s28(e, m27=m27):
            e.wait_ge(s_mm.h, m27)
            e.activation(D1b[0:16, :], pb(1, 0, 16, 0, 512),
                         AF.Copy).then_inc(s_s.h, 1)
        p.em("scalar", s28)
        s_s.bump(1)
        s28n = s_s.n

        def p29(e):
            last = None
            for pr in range(8):
                last = e.transpose(pbh[1][0:16, 128 * pr:128 * pr + 128],
                                   Bbd[:, 16 * pr:16 * pr + 16],
                                   self.eye128[:, :])
            return last
        m29 = TEphase("BT", p29, wait_s=s28n)

        def s30(e, m29=m29):
            e.wait_ge(s_mm.h, m29)
            e.activation(BTb[0:16, :], pbh[1][0:16, 0:1024],
                         AF.Copy).then_inc(s_s.h, 1)
        p.em("scalar", s30)
        s_s.bump(1)
        s30n = s_s.n

        def p31(e):
            last = None
            for pr in range(8):
                last = e.matmul(pb(0, 0, 128, 64 * pr, 64 * pr + 64),
                                BTb[0:16, 128 * pr:128 * pr + 128],
                                D1b[0:16, 64 * pr:64 * pr + 64],
                                start=True, stop=True)
            return last
        m31 = TEphase("Wr", p31, wait_s=s30n)

        def v32(e, m31=m31):
            e.wait_ge(s_mm.h, m31)
            e.scalar_tensor_tensor(Sb[:, :], Sb[:, :], 0.9,
                                   pb(0, 0, 128, 0, 512), ALU.mult, ALU.add)
            e.tensor_copy(Sbf[:, :], Sb[:, :]).then_inc(s_v.h, 1)
        p.em("vector", v32)
        s_v.bump(1)
        v32n = s_v.n

        def p33(e):
            last = None
            for h in range(16):
                r, c = h % 2, h // 2
                last = e.matmul(pbh[1][64 * r:64 * r + 64, 128 * c:128 * c + 128],
                                lslice(Sbf, h), QcT(h), start=True, stop=True,
                                tile_position=(64 * r, 64 * r))
            return last
        m33 = TEphase("oT", p33, wait_v=v32n)

        def s34(e, m33=m33):
            e.wait_ge(s_mm.h, m33)
            e.activation(oTb[:, :], pbh[1][:, 0:1024], AF.Copy).then_inc(s_s.h, 1)
        p.em("scalar", s34)
        s_s.bump(1)
        s34n = s_s.n

        def p35(e):
            last = None
            for d2 in range(2):
                for c in range(8):
                    last = e.matmul(pb(4 + d2),
                                    oTb[:, 128 * c:128 * c + 128],
                                    Wo_bf[:, 1024 * c + 512 * d2:
                                          1024 * c + 512 * d2 + 512],
                                    start=(c == 0), stop=(c == 7))
            return last
        m35 = TEphase("Wo", p35, wait_s=s34n)
        scan_mm[t] = s_mm.n

        par = t % 2
        xpv = x1_v[("xp", t)]
        stsnap = s_st.n

        def v36(e, m35=m35, par=par, xpv=xpv, t=t, stsnap=stsnap):
            e.wait_ge(s_mm.h, m35)
            e.wait_ge(s_xp.h, xpv)
            if t >= 1:
                e.wait_ge(s_st.h, stsnap)
            for d2 in range(2):
                ins = e.tensor_add(
                    x1o[:, 512 * d2:512 * d2 + 512],
                    xplt[:, 1024 * (t % 4) + 512 * d2:
                         1024 * (t % 4) + 512 * d2 + 512],
                    pb(4 + d2))
            ins.then_inc(s_v.h, 1)
        p.em("vector", v36)
        s_v.bump(1)
        x1_v[t] = s_v.n

        def sy36(e, t=t, par=par, xv=x1_v[t]):
            e.wait_ge(s_v.h, xv)
            e.dma_start(out=self.d_x1[128 * t:128 * t + 128, :],
                        in_=x1o[:, 0:1024]).then_inc(s_st.h, 16)
        p.em("sync", sy36)
        s_st.bump(16)

    def emit_fp4(self, x_ap, out_bf, T, sv, ss, nblk):
        p = self.prog
        N = nblk * 16
        qam, qsc, qinv = T["qam"], T["qsc"], T["qinv"]
        qv = T["qv"][:, 0:N]
        qu2 = T["qu2"][:, 0:N]
        qf2 = T["qf2"][:, 0:N]
        qc1 = T["qc1"][:, 0:N]
        qr1 = T["qr1"][:, 0:N]
        qi = T["qi"][:, 0:N]
        qsg = T["qsg"][:, 0:N]
        am = qam[:, 0:nblk]
        sc = qsc[:, 0:nblk]
        inv = qinv[:, 0:nblk]
        xb = x_ap.rearrange("p (b k) -> p b k", k=16)
        invb = inv.unsqueeze(-1).broadcast_to((128, nblk, 16))
        scb = sc.unsqueeze(-1).broadcast_to((128, nblk, 16))

        def d1(e):
            e.tensor_reduce(am, xb, AX.X, ALU.max, apply_absolute_value=True)
            self.vfence(e)
            e.tensor_scalar(sc, am, 1.0 / 6.0, 1e-12, ALU.mult, ALU.max)
            self.vfence(e)
            e.reciprocal(inv, sc)
            self.vfence(e)
            e.tensor_tensor(qv.rearrange("p (b k) -> p b k", k=16), xb, invb,
                            ALU.mult).then_inc(sv.h, 1)
        p.em("vector", d1)
        sv.bump(1)
        v1 = sv.n

        def a1(e):
            e.wait_ge(sv.h, v1)
            e.activation(qu2, qv, AF.Abs, scale=2.0)
            e.sign(qsg, qv).then_inc(ss.h, 1)
        p.em("scalar", a1)
        ss.bump(1)
        s1 = ss.n

        def d2(e):
            e.wait_ge(ss.h, s1)
            e.tensor_scalar_add(qi, qu2, 0.5)
            e.tensor_copy(qf2, qi)
            e.scalar_tensor_tensor(qv, qf2, -0.5, qu2, ALU.add, ALU.is_gt)
            e.tensor_sub(qf2, qf2, qv)
            e.tensor_single_scalar(qc1, qu2, 5.0, ALU.is_gt).then_inc(sv.h, 1)
        p.em("vector", d2)
        sv.bump(1)
        v2 = sv.n

        def a2(e):
            e.wait_ge(sv.h, v2)
            e.activation(qr1, qf2, AF.Relu, scale=-0.5,
                         bias=self.half2[:, 0:1]).then_inc(ss.h, 1)
        p.em("scalar", a2)
        ss.bump(1)
        s2 = ss.n

        def d3(e):
            e.wait_ge(ss.h, s2)
            e.tensor_sub(qc1, qc1, qr1)
            e.scalar_tensor_tensor(qc1, qu2, 7.0, qc1, ALU.is_gt, ALU.add)
            e.tensor_scalar(qr1, qu2, 10.0, 2.0, ALU.is_gt, ALU.mult)
            e.tensor_add(qc1, qc1, qr1)
            e.tensor_tensor(qr1.rearrange("p (b k) -> p b k", k=16),
                            qsg.rearrange("p (b k) -> p b k", k=16), scb,
                            ALU.mult)
            e.scalar_tensor_tensor(out_bf, qc1, 2.0, qr1, ALU.add,
                                   ALU.mult).then_inc(sv.h, 1)
        p.em("vector", d3)
        sv.bump(1)
        return sv.n

    def emit_fp4_dve(self, x_ap, out_bf, T, sv, nblk):
        p = self.prog
        N = nblk * 16
        qam, qsc, qinv = T["qam"], T["qsc"], T["qinv"]
        qv = T["qv"][:, 0:N]
        qu2 = T["qu2"][:, 0:N]
        qf2 = T["qf2"][:, 0:N]
        qc1 = T["qc1"][:, 0:N]
        qr1 = T["qr1"][:, 0:N]
        qi = T["qi"][:, 0:N]
        am = qam[:, 0:nblk]
        sc = qsc[:, 0:nblk]
        inv = qinv[:, 0:nblk]
        xb = x_ap.rearrange("p (b k) -> p b k", k=16)
        invb = inv.unsqueeze(-1).broadcast_to((128, nblk, 16))
        scb = sc.unsqueeze(-1).broadcast_to((128, nblk, 16))

        def dq(e):
            e.tensor_reduce(am, xb, AX.X, ALU.max, apply_absolute_value=True)
            self.vfence(e)
            e.tensor_scalar(sc, am, 1.0 / 6.0, 1e-12, ALU.mult, ALU.max)
            self.vfence(e)
            e.reciprocal(inv, sc)
            self.vfence(e)
            e.tensor_tensor(qv.rearrange("p (b k) -> p b k", k=16), xb, invb,
                            ALU.mult)
            e.tensor_scalar_mul(qf2, qv, -1.0)
            e.tensor_tensor(qu2, qv, qf2, ALU.max)
            e.tensor_scalar(qr1, qu2, 2.0, 0.5, ALU.mult, ALU.add)
            e.tensor_copy(qi, qr1)
            e.tensor_copy(qf2, qi)
            e.tensor_tensor(qc1, qf2, qr1, ALU.is_gt)
            e.tensor_sub(qf2, qf2, qc1)
            e.tensor_scalar(qc1, qf2, 0.5, 2.0, ALU.mult, ALU.min)
            e.scalar_tensor_tensor(qc1, qu2, 2.5, qc1, ALU.is_gt, ALU.add)
            e.scalar_tensor_tensor(qc1, qu2, 3.5, qc1, ALU.is_gt, ALU.add)
            e.tensor_scalar(qr1, qu2, 5.0, 2.0, ALU.is_gt, ALU.mult)
            e.tensor_add(qc1, qc1, qr1)
            e.tensor_single_scalar(qr1, qv, 0.0, ALU.is_gt)
            e.tensor_scalar(qr1, qr1, 2.0, -1.0, ALU.mult, ALU.add)
            e.tensor_tensor(qf2.rearrange("p (b k) -> p b k", k=16),
                            qr1.rearrange("p (b k) -> p b k", k=16), scb,
                            ALU.mult)
            e.tensor_tensor(out_bf, qc1, qf2, ALU.mult).then_inc(sv.h, 1)
        p.em("vector", dq)
        sv.bump(1)
        return sv.n

    def emit_stage_B(self):
        p = self.prog
        at = self.at
        o = 0
        def nxt(name, shape, dtype):
            nonlocal o
            t = at(name, shape, dtype, o)
            sz = shape[-1] * mybir.dt.size(dtype)
            o += (sz + 31) // 32 * 32
            return t
        wgqT = nxt("b_wgqT", [128, 8 * FF], BF16)
        wuqT = nxt("b_wuqT", [128, 8 * FF], BF16)
        wdqT = nxt("b_wdqT", [128, 22 * 1024], BF16)
        T = {}
        T["qv"] = nxt("b_qv", [128, 512], F32)
        T["qu2"] = nxt("b_qu2", [128, 512], F32)
        T["qf2"] = nxt("b_qf2", [128, 512], F32)
        T["qc1"] = nxt("b_qc1", [128, 512], F32)
        T["qr1"] = nxt("b_qr1", [128, 512], F32)
        T["qi"] = nxt("b_qi", [128, 512], I32)
        T["qsg"] = nxt("b_qsg", [128, 512], BF16)
        T["qam"] = nxt("b_qam", [128, 32], F32)
        T["qsc"] = nxt("b_qsc", [128, 32], F32)
        T["qinv"] = nxt("b_qinv", [128, 32], F32)
        hq = nxt("b_hq", [128, FF], BF16)
        hqT = nxt("b_hqT", [128, 22 * 128], BF16)
        hbuf = T["qf2"]
        qsil = nxt("b_qsil", [128, 512], F32)
        x1t = nxt("b_x1t", [128, 2 * 1024], F32)
        ln1 = nxt("b_ln1", [128, 1024], F32)
        sq = nxt("b_sq", [128, 1024], F32)
        xq = nxt("b_xq", [128, 1024], BF16)
        xqT = nxt("b_xqT", [128, 8 * 128], BF16)
        x2o = nxt("b_x2o", [128, 1024], F32)
        stat = nxt("b_stat", [128, 16], F32)
        wld = nxt("b_wld", [128, 3 * 256], F32)
        wqb = nxt("b_wqb", [128, 3 * 256], BF16)

        s_ld = self.sem("B_ld")
        s_tp = self.sem("B_tp")
        s_tq = self.sem("B_tq")
        s_v = self.sem("B_v")
        s_s = self.sem("B_s")
        s_mm = self.sem("B_mm")
        s_st = self.sem("B_st")
        s_pld = self.sem("B_pld")
        s_pst = self.sem("B_pst")
        s_vd = self.sem("B_vd")
        pb = self.pbank
        deps = getattr(self, "dep_B", [])

        subs = []
        for d in range(8):
            for g in range(11):
                subs.append((self.d_w_down, self.d_wdq, d, 256 * g, 256, "d"))
        n_down = len(subs)
        for m in range(22):
            for wn in ("g", "u"):
                src = self.d_w_gate if wn == "g" else self.d_w_up
                dst = self.d_wgq if wn == "g" else self.d_wuq
                for g in range(4):
                    subs.append((src, dst, m, 256 * g, 256, wn))

        vqk = {}
        pstk = {}
        U = {}
        wdq_tq = 0
        for k, (src, dst, r, off, gw, wn) in enumerate(subs):
            lb = 256 * (k % 3)

            def gl(e, src=src, r=r, off=off, gw=gw, k=k, lb=lb,
                   vprev=vqk.get(k - 3)):
                if k == 0:
                    for (ds, dv) in deps:
                        e.wait_ge(ds.h, dv)
                if vprev is not None:
                    e.wait_ge(s_v.h, vprev)
                e.dma_start(out=wld[:, lb:lb + gw],
                            in_=src[128 * r:128 * r + 128, off:off + gw]
                            ).then_inc(s_pld.h, 16)
            p.em("gpsimd", gl)
            s_pld.bump(16)
            pl = s_pld.n

            def vpre(e, pl=pl, sprev=pstk.get(k - 3)):
                e.wait_ge(s_pld.h, pl)
                if sprev is not None:
                    e.wait_ge(s_pst.h, sprev)
            p.em("vector", vpre)
            vq = self.emit_fp4_dve(wld[:, lb:lb + gw], wqb[:, lb:lb + gw],
                                   T, s_v, gw // 16)
            vqk[k] = vq

            def gs(e, dst=dst, r=r, off=off, gw=gw, lb=lb, vq=vq):
                e.wait_ge(s_v.h, vq)
                e.dma_start(out=dst[128 * r:128 * r + 128, off:off + gw],
                            in_=wqb[:, lb:lb + gw]).then_inc(s_pst.h, 16)
            p.em("sync", gs)
            s_pst.bump(16)
            pstk[k] = s_pst.n

            blks = tuple(range(off // 128, (off + gw) // 128))
            if wn == "d":
                q = "sync" if r % 2 == 0 else "scalar"
                sem = s_tp if q == "sync" else s_tq

                def tpd(e, r=r, blks=blks, ps=pstk[k], sem=sem):
                    e.wait_ge(s_pst.h, ps)
                    for m in blks:
                        e.dma_start_transpose(
                            wdqT[:, 1024 * m + 128 * r:1024 * m + 128 * r + 128],
                            self.d_wdq[128 * r:128 * r + 128,
                                       128 * m:128 * m + 128]
                            ).then_inc(sem.h, 16)
                p.em(q, tpd)
                sem.bump(16 * len(blks))
            else:
                q = "sync" if wn == "g" else "scalar"
                sem = s_tp if wn == "g" else s_tq
                wT = wgqT if wn == "g" else wuqT
                dq = self.d_wgq if wn == "g" else self.d_wuq

                def tpg(e, m=r, blks=blks, ps=pstk[k], wT=wT, dq=dq, sem=sem):
                    e.wait_ge(s_pst.h, ps)
                    for c in blks:
                        e.dma_start_transpose(
                            wT[:, FF * c + 128 * m:FF * c + 128 * m + 128],
                            dq[128 * m:128 * m + 128, 128 * c:128 * c + 128]
                            ).then_inc(sem.h, 16)
                p.em(q, tpg)
                sem.bump(16 * len(blks))
                if wn == "u" and off == 768:
                    for gm in range(6):
                        if (gm < 5 and r == 4 * gm + 3) or (gm == 5 and r == 21):
                            U[gm] = s_tq.n
            if k == n_down - 1:
                wdq_tq = s_tq.n

        GW = [512, 512, 512, 512, 512, 256]
        GOF = [0, 512, 1024, 1536, 2048, 2560]
        BT = [(0, 4, 6), (4, 4, 7), (8, 4, 6), (12, 4, 7), (16, 4, 6), (20, 2, 7)]
        tpq = {}
        gu_mm = {}
        gu_v = {}
        vh_v = {}
        tr_n = {}
        evac = {}
        md = {}
        vdns = {}
        trd = {}
        pend = []
        for i in range(NT):
            while len(pend) > 1:
                pend.pop(0)()
            bx = x1t[:, 1024 * (i % 2):1024 * (i % 2) + 1024]
            bxq = xq[:, :]
            bhq = hq[:, :]
            bx2 = x2o[:, :]

            def syx(e, i=i, bx=bx):
                if i == 0:
                    for (ds, dv) in deps:
                        e.wait_ge(ds.h, dv)
                if i >= 2:
                    e.wait_ge(s_vd.h, i - 1)
                e.dma_start(out=bx, in_=self.d_x1[128 * i:128 * i + 128, :]
                            ).then_inc(s_ld.h, 16)
            p.em("sync", syx)
            s_ld.bump(16)
            lds = s_ld.n

            def vx(e, lds=lds):
                e.wait_ge(s_ld.h, lds)
            p.em("vector", vx)
            self.emit_ln(bx, ln1[:, :], stat, sq[:, :], s_v, s_s)

            def vx2(e, tp=tpq.get(i - 1)):
                if tp is not None:
                    e.wait_ge(s_tp.h, tp)
            p.em("vector", vx2)
            vq = 0
            for g in range(2):
                vq = self.emit_fp4(ln1[:, 512 * g:512 * g + 512],
                                   bxq[:, 512 * g:512 * g + 512], T, s_v, s_s, 32)

            def syq(e, i=i, vq=vq, bxq=bxq, mmp=gu_mm.get((i - 1, 5))):
                e.wait_ge(s_v.h, vq)
                if mmp is not None:
                    e.wait_ge(s_mm.h, mmp)
                for k in range(8):
                    e.dma_start_transpose(xqT[:, 128 * k:128 * k + 128],
                                          bxq[:, 128 * k:128 * k + 128]
                                          ).then_inc(s_tp.h, 16)
            p.em("sync", syq)
            s_tp.bump(16 * 8)
            tpq[i] = s_tp.n

            for g in range(6):
                gw, gof = GW[g], GOF[g]
                bga = pb(0 + 2 * (g % 2), 0, 128, 0, gw)
                bup = pb(1 + 2 * (g % 2), 0, 128, 0, gw)
                key = (i, g - 2) if g >= 2 else (i - 1, g + 4)

                def teg(e, i=i, g=g, gof=gof, gw=gw, bga=bga, bup=bup,
                        tp=tpq[i], pv=gu_v.get(key)):
                    if g == 0:
                        e.wait_ge(s_tp.h, tp)
                    if i == 0:
                        e.wait_ge(s_tq.h, U[g])
                    if pv is not None:
                        e.wait_ge(s_v.h, pv)
                    last = None
                    for c in range(8):
                        e.matmul(bga, xqT[:, 128 * c:128 * c + 128],
                                 wgqT[:, FF * c + gof:FF * c + gof + gw],
                                 start=(c == 0), stop=(c == 7))
                    for c in range(8):
                        last = e.matmul(bup, xqT[:, 128 * c:128 * c + 128],
                                        wuqT[:, FF * c + gof:FF * c + gof + gw],
                                        start=(c == 0), stop=(c == 7))
                    last.then_inc(s_mm.h, 1)
                p.em("tensor", teg)
                s_mm.bump(1)
                gu_mm[(i, g)] = s_mm.n
                mg = s_mm.n

                vkey = (i, g - 1) if g >= 1 else (i - 1, 5)

                def sgl(e, mg=mg, gw=gw, bga=bga, vp=vh_v.get(vkey)):
                    e.wait_ge(s_mm.h, mg)
                    if vp is not None:
                        e.wait_ge(s_v.h, vp)
                    e.activation(qsil[:, 0:gw], bga, AF.Silu).then_inc(s_s.h, 1)
                p.em("scalar", sgl)
                s_s.bump(1)
                sg = s_s.n

                def vh(e, sg=sg, gw=gw, g=g, bup=bup, trp=trd.get(i - 1)):
                    e.wait_ge(s_s.h, sg)
                    if g == 0 and trp is not None:
                        e.wait_ge(s_mm.h, trp)
                    e.tensor_mul(hbuf[:, 0:gw], qsil[:, 0:gw],
                                 bup).then_inc(s_v.h, 1)
                p.em("vector", vh)
                s_v.bump(1)
                vh_v[(i, g)] = s_v.n
                hq_v = self.emit_fp4(hbuf[:, 0:gw], bhq[:, gof:gof + gw],
                                     T, s_v, s_s, gw // 16)
                gu_v[(i, g)] = hq_v

            for b, (m0, nm, bank) in enumerate(BT):
                ev_guard = evac.get((i, b - 2)) if b >= 2 else \
                    evac.get((i - 1, 4 + b))

                def tr(e, b=b, m0=m0, nm=nm, bank=bank, bhq=bhq,
                       hv=gu_v[(i, 5)], eg=ev_guard):
                    if b == 0:
                        e.wait_ge(s_v.h, hv)
                    if eg is not None:
                        e.wait_ge(s_s.h, eg)
                    last = None
                    for t in range(nm):
                        last = e.matmul(
                            pb(bank, 0, 128, 128 * t, 128 * t + 128),
                            bhq[:, 128 * (m0 + t):128 * (m0 + t) + 128],
                            self.eyebf[:, :], start=True, stop=True)
                    last.then_inc(s_mm.h, 1)
                p.em("tensor", tr)
                s_mm.bump(1)
                tr_n[(i, b)] = s_mm.n

                def ev(e, b=b, m0=m0, nm=nm, bank=bank, tn=tr_n[(i, b)]):
                    e.wait_ge(s_mm.h, tn)
                    e.activation(hqT[:, 128 * m0:128 * (m0 + nm)],
                                 pb(bank, 0, 128, 0, 128 * nm),
                                 AF.Copy).then_inc(s_s.h, 1)
                p.em("scalar", ev)
                s_s.bump(1)
                evac[(i, b)] = s_s.n
            trd[i] = s_mm.n

            def ted(e, i=i, evl=evac[(i, 5)]):
                e.wait_ge(s_s.h, evl)
                if i == 0:
                    e.wait_ge(s_tq.h, wdq_tq)
                if i >= 1:
                    e.wait_ge(s_vd.h, i)
                last = None
                for h in range(2):
                    for m in range(22):
                        last = e.matmul(pb(4 + h),
                                        hqT[:, 128 * m:128 * m + 128],
                                        wdqT[:, 1024 * m + 512 * h:
                                             1024 * m + 512 * h + 512],
                                        start=(m == 0), stop=(m == 21))
                last.then_inc(s_mm.h, 1)
            p.em("tensor", ted)
            s_mm.bump(1)
            md[i] = s_mm.n

            def phase2(i=i, md=md[i], bx=bx, bx2=bx2):
                def vd(e, i=i, md=md, bx=bx, bx2=bx2):
                    e.wait_ge(s_mm.h, md)
                    if i >= 1:
                        e.wait_ge(s_st.h, 16 * i)
                    for h in range(2):
                        ins = e.tensor_add(bx2[:, 512 * h:512 * h + 512],
                                           bx[:, 512 * h:512 * h + 512],
                                           pb(4 + h))
                    ins.then_inc(s_vd.h, 1)
                p.em("vector", vd)
                s_vd.bump(1)

                def sy2(e, i=i, bx2=bx2):
                    e.wait_ge(s_vd.h, i + 1)
                    e.dma_start(out=self.d_x2[128 * i:128 * i + 128, :],
                                in_=bx2).then_inc(s_st.h, 16)
                p.em("sync", sy2)
                s_st.bump(16)
            pend.append(phase2)

        for f in pend:
            f()

        def syend(e):
            e.wait_ge(s_st.h, s_st.n)
        p.em("sync", syend)
        self.B_done = (s_st, s_st.n)

    def emit_stage_D(self):
        p = self.prog
        at = self.at
        o = 0
        def nxt(name, shape, dtype):
            nonlocal o
            t = at(name, shape, dtype, o)
            sz = shape[-1] * mybir.dt.size(dtype)
            o += (sz + 31) // 32 * 32
            return t
        qrT = nxt("d_qrT", [128, 8 * 1024], BF16)
        rload = nxt("d_rload", [128, 2 * 1024], F32)
        rhsT = nxt("d_rhsT", [128, 1024], F32)
        rq = nxt("d_rq", [128, 1024], BF16)
        rtmp = nxt("d_rtmp", [128, 1024], F32)
        rsgn = nxt("d_rsgn", [128, 1024], F32)
        ri = nxt("d_ri", [128, 1024], I32)
        rf2 = nxt("d_rf2", [128, 1024], F32)
        ry = nxt("d_ry", [128, 1024], F32)
        rqm = nxt("d_rqm", [128, 1024], F32)
        rstat = nxt("d_rstat", [128, 16], F32)
        x2t = nxt("d_x2t", [128, 2 * 1024], F32)
        att = nxt("d_att", [128, 2 * 1024], BF16)
        x3t = nxt("d_x3t", [128, 2 * 1024], F32)
        ln3 = nxt("d_ln3", [128, 1024], F32)
        stat = nxt("d_stat", [128, 16], F32)
        lsc = nxt("d_lsc", [128, 4], F32)
        qsgn = nxt("d_qsgn", [128, 1024], F32)
        qtmp = nxt("d_qtmp", [128, 1024], F32)
        qi = nxt("d_qi", [128, 1024], I32)
        qf2 = nxt("d_qf2", [128, 1024], F32)
        qy = nxt("d_qy", [128, 1024], F32)
        qm = nxt("d_qm", [128, 1024], F32)
        qb = nxt("d_qb", [128, 1024], BF16)
        qT = nxt("d_qT", [128, 2 * 1024], BF16)
        outt = nxt("d_outt", [128, 2 * 1024], F32)

        s_ld = self.sem("D_ld")
        s_tp = self.sem("D_tp")
        s_v = self.sem("D_v")
        s_s = self.sem("D_s")
        s_mm = self.sem("D_mm")
        s_st = self.sem("D_st")
        s_pe = self.sem("D_pe")
        pb = self.pbank
        init_v = self.init_v
        s_iv = self.sem("init_v")

        for c in range(8):
            def sy(e, c=c):
                if c == 0:
                    for (ds, dv) in getattr(self, "dep_D", []):
                        e.wait_ge(ds.h, dv)
                if c >= 2:
                    e.wait_ge(s_pe.h, 2 * (c - 1))
                e.dma_start(
                    out=rload[:, (c % 2) * 1024:(c % 2) * 1024 + 1024]
                    .rearrange("p (k f) -> p k f", f=128),
                    in_=self.d_rhs[:, 128 * c:128 * c + 128]
                    .rearrange("(k p) f -> p k f", p=128)).then_inc(s_ld.h, 16)
            p.em("sync", sy)
            s_ld.bump(16)
            ld_snap = s_ld.n

            def te1(e, c=c, ld_snap=ld_snap):
                e.wait_ge(s_ld.h, ld_snap)
                e.wait_ge(s_iv.h, init_v)
                if c > 0:
                    e.wait_ge(s_pe.h, 2 * c)
                buf = rload[:, (c % 2) * 1024:(c % 2) * 1024 + 1024]
                last = None
                for k in range(4):
                    last = e.transpose(pb(k, 0, 128, 0, 128),
                                       buf[:, 128 * k:128 * k + 128],
                                       self.eye128[:, :])
                last.then_inc(s_mm.h, 1)
            p.em("tensor", te1)
            s_mm.bump(1)
            mm1 = s_mm.n

            def te2(e, c=c):
                e.wait_ge(s_pe.h, 2 * c + 1)
                buf = rload[:, (c % 2) * 1024:(c % 2) * 1024 + 1024]
                last = None
                for k in range(4):
                    last = e.transpose(pb(4 + k, 0, 128, 0, 128),
                                       buf[:, 128 * (4 + k):128 * (4 + k) + 128],
                                       self.eye128[:, :])
                last.then_inc(s_mm.h, 1)
            p.em("tensor", te2)
            s_mm.bump(1)
            mm2 = s_mm.n

            def ve(e, mm1=mm1, mm2=mm2):
                e.wait_ge(s_mm.h, mm1)
                for k in range(4):
                    ins = e.tensor_copy(rhsT[:, 128 * k:128 * k + 128],
                                        pb(k, 0, 128, 0, 128))
                ins.then_inc(s_pe.h, 1)
                e.wait_ge(s_mm.h, mm2)
                for k in range(4):
                    ins = e.tensor_copy(rhsT[:, 128 * (4 + k):128 * (4 + k) + 128],
                                        pb(4 + k, 0, 128, 0, 128))
                ins.then_inc(s_pe.h, 1)
                e.tensor_reduce(rstat[:, 0:1], rhsT[:, :], AX.X, ALU.max,
                                apply_absolute_value=True)
                self.vfence(e)
                e.tensor_scalar_max(rstat[:, 1:2], rstat[:, 0:1], 1e-12)
                self.vfence(e)
                e.tensor_scalar_mul(rstat[:, 2:3], rstat[:, 1:2], 1.0 / 127.0)
                self.vfence(e)
                e.reciprocal(rstat[:, 3:4], rstat[:, 2:3])
                self.vfence(e)
                e.tensor_scalar_add(self.fence_t[:, :], self.fence_t[:, :],
                                    0.0).then_inc(s_v.h, 1)
            p.em("vector", ve)
            s_pe.bump(2)
            s_v.bump(1)

            self.emit_round_i8(rhsT[:, :], rstat[:, 3:4], rsgn[:, :],
                               rf2[:, :], ri[:, :], ry[:, :], rqm[:, :],
                               s_v, s_s, wait_v=s_v.n)
            tp_prev = s_tp.n

            def vfold(e, tp_prev=tp_prev):
                e.wait_ge(s_tp.h, tp_prev)
                e.tensor_scalar(rsgn[:, :], rsgn[:, :], rstat[:, 2:3], None,
                                ALU.mult)
                e.tensor_mul(rq[:, :], rqm[:, :], rsgn[:, :]).then_inc(s_v.h, 1)
            p.em("vector", vfold)
            s_v.bump(1)
            vf = s_v.n

            def sy2(e, c=c, vf=vf):
                e.wait_ge(s_v.h, vf)
                for k in range(8):
                    e.dma_start_transpose(
                        qrT[:, 1024 * k + 128 * c:1024 * k + 128 * c + 128],
                        rq[:, 128 * k:128 * k + 128]).then_inc(s_tp.h, 16)
            p.em("scalar", sy2)
            s_tp.bump(16 * 8)

        rhs_ready = s_tp.n
        s_v5 = self.sem("D_v5")

        pend = []
        for i in range(NT):
            while len(pend) > 1:
                pend.pop(0)()
            par = i % 2
            x2a = x2t[:, par * 1024:par * 1024 + 1024]
            atta = att[:, par * 1024:par * 1024 + 1024]
            x3a = x3t[:, par * 1024:par * 1024 + 1024]
            lsa = lsc[:, par:par + 1]
            qTa = qT[:, par * 1024:par * 1024 + 1024]
            outta = outt[:, par * 1024:par * 1024 + 1024]

            def sy(e, i=i, x2a=x2a, atta=atta):
                if i == 0:
                    for (ds, dv) in getattr(self, "dep_D", []):
                        e.wait_ge(ds.h, dv)
                if i >= 2:
                    e.wait_ge(s_st.h, 16 * (i - 1))
                e.dma_start(out=x2a, in_=self.d_x2[128 * i:128 * i + 128, :]
                            ).then_inc(s_ld.h, 16)
                e.dma_start(out=atta, in_=self.d_attn[128 * i:128 * i + 128, :]
                            ).then_inc(s_ld.h, 16)
            p.em("sync", sy)
            s_ld.bump(32)
            ld_snap = s_ld.n

            def ve(e, x2a=x2a, atta=atta, x3a=x3a, ld_snap=ld_snap):
                e.wait_ge(s_ld.h, ld_snap)
                e.tensor_add(x3a, x2a, atta)
            p.em("vector", ve)

            self.emit_ln(x3a, ln3[:, :], stat, qtmp[:, :], s_v, s_s)

            def ve3(e, lsa=lsa):
                e.tensor_reduce(stat[:, 5:6], ln3[:, :], AX.X, ALU.max,
                                apply_absolute_value=True)
                self.vfence(e)
                e.tensor_scalar_max(stat[:, 6:7], stat[:, 5:6], 1e-12)
                self.vfence(e)
                e.tensor_scalar_mul(lsa, stat[:, 6:7], 1.0 / 127.0)
                self.vfence(e)
                e.reciprocal(stat[:, 7:8], lsa)
                self.vfence(e)
                e.tensor_scalar_add(self.fence_t[:, :], self.fence_t[:, :],
                                    0.0).then_inc(s_v.h, 1)
            p.em("vector", ve3)
            s_v.bump(1)

            self.emit_round_i8(ln3[:, :], stat[:, 7:8], qsgn[:, :],
                               qf2[:, :], qi[:, :], qy[:, :], qm[:, :],
                               s_v, s_s, wait_v=s_v.n)
            tp_prev = s_tp.n

            def vfin(e, tp_prev=tp_prev):
                e.wait_ge(s_tp.h, tp_prev)
                e.tensor_mul(qb[:, :], qm[:, :], qsgn[:, :]).then_inc(s_v.h, 1)
            p.em("vector", vfin)
            s_v.bump(1)
            v4 = s_v.n

            def sy3(e, i=i, v4=v4, qTa=qTa):
                e.wait_ge(s_v.h, v4)
                if i >= 2:
                    e.wait_ge(s_mm.h, 16 + 2 * (i - 1))
                for k in range(8):
                    e.dma_start_transpose(qTa[:, 128 * k:128 * k + 128],
                                          qb[:, 128 * k:128 * k + 128]
                                          ).then_inc(s_tp.h, 16)
            p.em("scalar", sy3)
            s_tp.bump(16 * 8)
            tp_snap = s_tp.n

            def phase2(i=i, qTa=qTa, tp_snap=tp_snap, x3a=x3a, lsa=lsa,
                       outta=outta):
                def te(e, i=i, qTa=qTa, tp_snap=tp_snap):
                    e.wait_ge(s_tp.h, tp_snap)
                    if i == 0:
                        e.wait_ge(s_tp.h, rhs_ready)
                    if i >= 1:
                        e.wait_ge(s_v5.h, i)
                    last = None
                    for h in range(2):
                        for k in range(8):
                            last = e.matmul(
                                pb(h), qTa[:, 128 * k:128 * k + 128],
                                qrT[:, 1024 * k + 512 * h:
                                    1024 * k + 512 * h + 512],
                                start=(k == 0), stop=(k == 7))
                    last.then_inc(s_mm.h, 2)
                p.em("tensor", te)
                s_mm.bump(2)
                mm_snap = s_mm.n

                def ve5(e, i=i, x3a=x3a, lsa=lsa, outta=outta,
                        mm_snap=mm_snap):
                    e.wait_ge(s_mm.h, mm_snap)
                    if i >= 2:
                        e.wait_ge(s_st.h, 16 * (i - 1))
                    for h in range(2):
                        ins = e.scalar_tensor_tensor(
                            outta[:, 512 * h:512 * h + 512], pb(h), lsa,
                            x3a[:, 512 * h:512 * h + 512],
                            ALU.mult, ALU.add)
                    ins.then_inc(s_v5.h, 1)
                p.em("vector", ve5)
                s_v5.bump(1)

                def sy4(e, i=i, outta=outta):
                    e.wait_ge(s_v5.h, i + 1)
                    e.dma_start(out=self.d_out[128 * i:128 * i + 128, :],
                                in_=outta).then_inc(s_st.h, 16)
                p.em("sync", sy4)
                s_st.bump(16)
            pend.append(phase2)

        for f in pend:
            f()

        def syend(e):
            e.wait_ge(s_st.h, s_st.n)
        p.em("sync", syend)
        if self.debug_taps:
            for nm, t in (("qrT", qrT), ("ln3", ln3), ("qb", qb), ("qm", qm),
                          ("qsgn", qsgn), ("qy", qy), ("qf2", qf2),
                          ("x3t", x3t), ("outt", outt), ("lsc", lsc),
                          ("stat", stat)):
                self.tap(nm, t[tuple(slice(None) for _ in t.shape)],
                         wait_sems=(s_v, s_st, s_tp))

    def tap(self, name, ap, wait_sems=()):
        if not hasattr(self, "_taps"):
            self._taps = []
        shape = list(ap.shape)
        d = self._dram("tap_" + name, shape, ap.dtype, "ExternalOutput")
        self._taps.append((d, ap, tuple(wait_sems)))

    def emit_taps(self):
        if not getattr(self, "_taps", None):
            return
        p = self.prog
        s_t = self.sem("tap_st")
        taps = self._taps

        def sy(e):
            waited = set()
            for d, ap, ws in taps:
                for s in ws:
                    key = (id(s.h), s.n)
                    if key not in waited:
                        e.wait_ge(s.h, s.n)
                        waited.add(key)
            for d, ap, ws in taps:
                e.dma_start(out=d[tuple(slice(None) for _ in d.shape)], in_=ap).then_inc(s_t.h, 16)
            e.wait_ge(s_t.h, 16 * len(taps))
        p.em("sync", sy)
        s_t.bump(16 * len(taps))

    def finalize(self):
        self.emit_taps()
        with self.nc.Block() as block:
            self.prog.play(block)
        self.es.close()
        return self.nc


def build(stages=("A", "B", "C", "D")):
    k = K(stages)
    k.emit_init()
    if "A" in stages:
        k.emit_stage_A()
    if "B" in stages:
        k.dep_B = [(k.A_done[0], k.A_done[1])] if "A" in stages else []
        k.emit_stage_B()
    if "C" in stages:
        k.dep_C = [(k.B_done[0], k.B_done[1])] if "B" in stages else []
        k.emit_stage_C()
    if "D" in stages:
        k.dep_D = [(k.C_done[0], k.C_done[1])] if "C" in stages else []
        k.emit_stage_D()
    k.finalize()
    return k


_CACHE = {}


def kernel(**inputs):
    if "full" not in _CACHE:
        _CACHE["full"] = build()
    k = _CACHE["full"]
    x = np.ascontiguousarray(inputs["L0_attn_x"], dtype=np.float32)
    common = {
        "ln_g": inputs["L0_attn_ln_g"], "ln_b": inputs["L0_attn_ln_b"],
        "Wq": inputs["L0_attn_Wq"], "Wk": inputs["L0_attn_Wk"],
        "Wv": inputs["L0_attn_Wv"], "Wo": inputs["L0_attn_Wo"],
        "w_gate": inputs["L0_ffn_w_gate"], "w_up": inputs["L0_ffn_w_up"],
        "w_down": inputs["L0_ffn_w_down"],
        "Mq": inputs["L1_attn_Mq"], "Mk": inputs["L1_attn_Mk"],
        "Mv": inputs["L1_attn_Mv"], "rhs": inputs["L1_ffn_rhs"],
    }
    common = {kk: np.ascontiguousarray(v, dtype=np.float32)
              for kk, v in common.items()}
    in_maps = [dict(common, x=x[b]) for b in range(8)]
    res = run_bass_kernel_spmd(k.nc, in_maps, core_ids=list(range(8)))
    return np.stack([res.results[b]["out"] for b in range(8)])


# version 07
# speedup vs baseline: 1.2785x; 1.0131x over previous
from contextlib import ExitStack

import numpy as np

import concourse.bass as bass
import concourse.mybir as mybir
from concourse.bass_utils import run_bass_kernel_spmd
from concourse import library_config

F32 = mybir.dt.float32
BF16 = mybir.dt.bfloat16
I32 = mybir.dt.int32
AF = mybir.ActivationFunctionType
ALU = mybir.AluOpType
AX = mybir.AxisListType

TOK = 4096
D = 1024
NT = TOK // 128
FF = 2816
LN_EPS = 1e-5
A_SLABS = 8


def ns_schedule(l0, tol=1e-15, max_iter=24):
    gs = []
    l, u = float(l0), 1.0
    for _ in range(max_iter):
        g = 2.0 / (l + u)
        ylo = g * l * (2 - g * l)
        yhi = g * u * (2 - g * u)
        lo = min(ylo, yhi)
        hi = 1.0 if (g * l <= 1.0 <= g * u) else max(ylo, yhi)
        l, u = lo, hi
        gs.append(g)
        if 1 - l <= tol and u - 1 <= tol:
            break
    return gs


NS64_GAMMAS = ns_schedule(1e-3)
NS8_GAMMAS = ns_schedule(2e-3)

ENGINES = ("sync", "tensor", "vector", "scalar", "gpsimd")


class Prog:
    def __init__(self):
        self.streams = {e: [] for e in ENGINES}

    def em(self, eng, fn):
        self.streams[eng].append(fn)

    def play(self, block):
        def run(fns):
            return lambda e: [f(e) for f in fns]
        block.sync(run(self.streams["sync"]))
        block.tensor(run(self.streams["tensor"]))
        block.vector(run(self.streams["vector"]))
        block.scalar(run(self.streams["scalar"]))
        block.gpsimd(run(self.streams["gpsimd"]))


class Sem:
    def __init__(self, es, nc, name):
        self.h = es.enter_context(nc.semaphore(name))
        self.n = 0

    def bump(self, k=1):
        self.n += k
        return self.n


class K:
    def __init__(self, stages=("A", "B", "C", "D"), debug_taps=False):
        self.stages = stages
        self.debug_taps = debug_taps
        self.nc = bass.Bass("TRN2", target_bir_lowering=False,
                            detect_race_conditions=False)
        self.es = ExitStack()
        self.prog = Prog()
        self.sems = {}
        self._arena_off = None
        self._arena_end = None
        self._alloc_io()
        self._alloc_fixed()

    def sem(self, name):
        if name not in self.sems:
            self.sems[name] = Sem(self.es, self.nc, name)
        return self.sems[name]

    def _dram(self, name, shape, dtype, kind):
        return self.nc.dram_tensor(name, shape, dtype, kind=kind).ap()

    def _alloc_io(self):
        st = self.stages
        ext = lambda n, s, d=F32: self._dram(n, s, d, "ExternalInput")
        first = st[0]
        self.d_x = ext("x", [TOK, D]) if "A" in st else None
        self.d_ln_g = ext("ln_g", [D]) if "A" in st else None
        self.d_ln_b = ext("ln_b", [D]) if "A" in st else None
        for nm, shape in (("Wq", [D, D]), ("Wk", [D, D]), ("Wv", [D, D]),
                          ("Wo", [D, D])):
            setattr(self, "d_" + nm, ext(nm, shape) if "A" in st else None)
        for nm, shape in (("w_gate", [FF, D]), ("w_up", [FF, D]),
                          ("w_down", [D, FF])):
            setattr(self, "d_" + nm, ext(nm, shape) if "B" in st else None)
        for nm, shape in (("Mq", [D, 512]), ("Mk", [D, 512]), ("Mv", [D, D])):
            setattr(self, "d_" + nm, ext(nm, shape) if "C" in st else None)
        self.d_rhs = ext("rhs", [D, D]) if "D" in st else None
        self.d_out = self._dram("out", [TOK, D], F32, "ExternalOutput")
        def scratch(name, dtype, needed_by, produced_by):
            if needed_by not in st and produced_by not in st:
                return None
            if produced_by in st:
                kind = "Internal" if needed_by in st else "ExternalOutput"
            else:
                kind = "ExternalInput"
            return self._dram(name, [TOK, D], dtype, kind)
        self.d_xpl = self._dram("xpl_s", [TOK, D], F32, "Internal") if "A" in st else None
        self.d_x1 = scratch("x1_s", F32, "B", "A")
        self.d_x2 = scratch("x2_s", F32, "CD", "B")
        if self.d_x2 is None and ("C" in st or "D" in st):
            self.d_x2 = self._dram("x2_s", [TOK, D], F32,
                                   "Internal" if "B" in st else "ExternalInput")
        self.d_ln2 = self._dram("ln2_s", [D, TOK], BF16, "Internal") if "C" in st else None
        if "B" in st:
            self.d_wgq = self._dram("wgq_s", [FF, D], BF16, "Internal")
            self.d_wuq = self._dram("wuq_s", [FF, D], BF16, "Internal")
            self.d_wdq = self._dram("wdq_s", [D, FF], BF16, "Internal")
        self.d_attn = scratch("attn_s", BF16, "D", "C")

    def _alloc_fixed(self):
        nc = self.nc
        self.eye128 = nc.alloc_sbuf_tensor("eye128", [128, 128], F32)
        self.eyebf = nc.alloc_sbuf_tensor("eyebf", [128, 128], BF16)
        self.i64b = nc.alloc_sbuf_tensor("i64b", [128, 512], F32)
        self.i16b = nc.alloc_sbuf_tensor("i16b", [16, 128], F32)
        self.sel16 = nc.alloc_sbuf_tensor("sel16", [16, 2], F32)
        self.ones = nc.alloc_sbuf_tensor("ones", [128, 64], F32)
        self.gb = nc.alloc_sbuf_tensor("gb", [128, D], F32)
        self.bb = nc.alloc_sbuf_tensor("bb", [128, D], F32)
        self.gbrow = nc.alloc_sbuf_tensor("gbrow", [1, D], F32)
        self.bbrow = nc.alloc_sbuf_tensor("bbrow", [1, D], F32)
        self.onesrow = nc.alloc_sbuf_tensor("onesrow", [1, 128], F32)
        self.fence_t = nc.alloc_sbuf_tensor("fence_t", [128, 8], F32)
        self.half2 = nc.alloc_sbuf_tensor("half2", [128, 1], F32)
        self.sel2x16 = nc.alloc_sbuf_tensor("sel2x16", [2, 16], F32)
        self.sel64 = nc.alloc_sbuf_tensor("sel64", [128, 2], F32)
        self.sel2x128 = nc.alloc_sbuf_tensor("sel2x128", [2, 128], F32)
        self.pbh = [nc.alloc_psum_tensor(f"pb{i}", [128, 1024], F32)
                    for i in range(4)]
        left = nc._sbuf_addr_for_side("left")
        right = nc._sbuf_addr_for_side("right")
        self._arena_off = (left + 127) // 128 * 128
        self._arena_end = right

    def pbank(self, i, r0=0, r1=128, c0=0, c1=512):
        h = self.pbh[i // 2]
        base = 512 * (i % 2)
        return h[r0:r1, base + c0:base + c1]

    def vfence(self, e):
        e.tensor_scalar_add(self.fence_t[:, :], self.fence_t[:, :], 0.0)

    def at(self, name, shape, dtype, off):
        nbytes = shape[-1] * mybir.dt.size(dtype)
        assert self._arena_off + off + nbytes <= self._arena_end, \
            (name, self._arena_off, off, nbytes, self._arena_end)
        return self.nc.alloc_sbuf_tensor_at(name, shape, dtype,
                                            offset=self._arena_off + off)

    def emit_init(self):
        p = self.prog
        s_ld = self.sem("init_ld")
        s_gp = self.sem("init_gp")
        s_v = self.sem("init_v")
        need_gb = "A" in self.stages

        def sy(e):
            if need_gb:
                e.dma_start(out=self.gbrow[0:1, :],
                            in_=self.d_ln_g[None, :]).then_inc(s_ld.h, 16)
                e.dma_start(out=self.bbrow[0:1, :],
                            in_=self.d_ln_b[None, :]).then_inc(s_ld.h, 16)
        p.em("sync", sy)
        if need_gb:
            s_ld.bump(32)
        ld_snap = s_ld.n

        def gp(e):
            e.memset(self.onesrow[0:1, :], 1.0)
            e.iota(self.eye128[:, :], pattern=[[1, 128]], base=0,
                   channel_multiplier=-1,
                   allow_small_or_imprecise_dtypes=True).then_inc(s_gp.h, 1)
        p.em("gpsimd", gp)
        s_gp.bump(1)
        gp1 = 1
        if need_gb:
            s_mm0 = self.sem("init_mm")

            def te0(e):
                e.wait_ge(s_ld.h, ld_snap)
                e.wait_ge(s_gp.h, 1)
                last = None
                for half, (row, dst) in enumerate(
                        [(self.gbrow, 0), (self.bbrow, 1)]):
                    for c in range(2):
                        last = e.matmul(self.pbank(2 * half + c),
                                        self.onesrow[0:1, :],
                                        row[0:1, 512 * c:512 * c + 512],
                                        start=True, stop=True)
                last.then_inc(s_mm0.h, 1)
            p.em("tensor", te0)
            s_mm0.bump(1)

            def ve0(e):
                e.wait_ge(s_mm0.h, 1)
                ins = None
                for c in range(2):
                    e.tensor_copy(self.gb[:, 512 * c:512 * c + 512],
                                  self.pbank(2 * 0 + c))
                    ins = e.tensor_copy(self.bb[:, 512 * c:512 * c + 512],
                                        self.pbank(2 * 1 + c))
                ins.then_inc(s_gp.h, 2)
            p.em("vector", ve0)
            s_gp.bump(2)

        def ve(e):
            e.wait_ge(s_gp.h, gp1)
            e.memset(self.fence_t[:, :], 0.0)
            e.memset(self.half2[:, :], 2.0)
            e.tensor_single_scalar(self.eye128[:, :], self.eye128[:, :], 0.0,
                                   ALU.is_equal)
            e.tensor_copy(self.eyebf[:, :], self.eye128[:, :])
            for half in range(2):
                src = self.eye128[64 * half:64 * half + 64,
                                  64 * half:64 * half + 64]
                dst = self.i64b[64 * half:64 * half + 64, :].rearrange(
                    "p (s j) -> p s j", j=64)
                e.tensor_copy(dst, src.unsqueeze(1).broadcast_to((64, 8, 64)))
            e.memset(self.i16b[:, :], 0.0)
            e.tensor_copy(
                self.i16b[0:16, :].rearrange("p (s j) -> p s j", j=16),
                self.eye128[0:16, 0:16].unsqueeze(1).broadcast_to((16, 8, 16)))
            e.reduce_sum(self.sel16[0:16, 0:1], self.eye128[0:16, 0:8], AX.X)
            e.reduce_sum(self.sel16[0:16, 1:2], self.eye128[0:16, 8:16], AX.X)
            e.reduce_sum(self.sel64[:, 0:1], self.eye128[:, 0:64], AX.X)
            e.reduce_sum(self.sel64[:, 1:2], self.eye128[:, 64:128], AX.X)
            e.memset(self.ones[:, :], 1.0).then_inc(s_v.h, 1)
        p.em("vector", ve)
        s_v.bump(1)
        iv0 = s_v.n
        if "A" in self.stages:
            s_m0 = self.sem("init_mm2")

            def te1(e, iv0=iv0):
                e.wait_ge(s_v.h, iv0)
                e.transpose(self.pbank(6, 0, 2, 0, 16), self.sel16[0:16, 0:2],
                            self.eye128[0:16, 0:16])
                e.transpose(self.pbank(7, 0, 2, 0, 128), self.sel64[:, 0:2],
                            self.eye128[:, :]).then_inc(s_m0.h, 1)
            p.em("tensor", te1)
            s_m0.bump(1)

            def ve1(e):
                e.wait_ge(s_m0.h, 1)
                e.tensor_copy(self.sel2x16[0:2, 0:16],
                              self.pbank(6, 0, 2, 0, 16))
                e.tensor_copy(self.sel2x128[0:2, 0:128],
                              self.pbank(7, 0, 2, 0, 128)).then_inc(s_v.h, 1)
            p.em("vector", ve1)
            s_v.bump(1)
        self.init_v = s_v.n

    def emit_ln(self, x_ap, out_ap, stat, sq_tmp, sv, ss, fin_ap=None):
        p = self.prog
        n = x_ap.shape[-1]

        def v1(e):
            e.reduce_sum(stat[:, 0:1], x_ap, AX.X)
            self.vfence(e)
            e.tensor_scalar_mul(stat[:, 0:1], stat[:, 0:1], 1.0 / n)
            self.vfence(e)
            e.tensor_scalar(out_ap, x_ap, stat[:, 0:1], None,
                            ALU.subtract).then_inc(sv.h, 1)
        p.em("vector", v1)
        sv.bump(1)
        v1n = sv.n

        def s1(e):
            e.wait_ge(sv.h, v1n)
            e.activation(sq_tmp, out_ap, AF.Square,
                         accum_out=stat[:, 1:2]).then_inc(ss.h, 1)
        p.em("scalar", s1)
        ss.bump(1)
        s1n = ss.n

        def v2(e):
            e.wait_ge(ss.h, s1n)
            e.tensor_scalar(stat[:, 2:3], stat[:, 1:2], 1.0 / n, LN_EPS,
                            ALU.mult, ALU.add)
            self.vfence(e)
            e.tensor_scalar_add(self.fence_t[:, :], self.fence_t[:, :],
                                0.0).then_inc(sv.h, 1)
        p.em("vector", v2)
        sv.bump(1)
        v2n = sv.n

        def s2(e):
            e.wait_ge(sv.h, v2n)
            e.activation(stat[:, 3:4], stat[:, 2:3], AF.Sqrt).then_inc(ss.h, 1)
        p.em("scalar", s2)
        ss.bump(1)
        s2n = ss.n

        def v3(e):
            e.wait_ge(ss.h, s2n)
            self.vfence(e)
            e.reciprocal(stat[:, 3:4], stat[:, 3:4])
            self.vfence(e)
            e.tensor_mul(stat[:, 4:5], stat[:, 3:4], stat[:, 3:4])
            self.vfence(e)
            e.tensor_mul(stat[:, 4:5], stat[:, 4:5], stat[:, 2:3])
            self.vfence(e)
            e.tensor_scalar(stat[:, 4:5], stat[:, 4:5], -0.5, 1.5,
                            ALU.mult, ALU.add)
            self.vfence(e)
            e.tensor_mul(stat[:, 3:4], stat[:, 3:4], stat[:, 4:5])
            self.vfence(e)
            e.tensor_scalar(fin_ap if fin_ap is not None else out_ap,
                            out_ap, stat[:, 3:4], None,
                            ALU.mult).then_inc(sv.h, 1)
        p.em("vector", v3)
        sv.bump(1)
        return sv.n

    def emit_floor_pos(self, e, y_ap, i_tmp, f_tmp, out_ap):
        e.tensor_copy(i_tmp, y_ap)
        e.tensor_copy(f_tmp, i_tmp)
        e.tensor_tensor(out_ap, f_tmp, y_ap, ALU.is_gt)
        e.tensor_sub(out_ap, f_tmp, out_ap)

    def emit_round_i8(self, x_ap, inv_ap, sgn_tmp, f_tmp, i_tmp, y_tmp, q_out,
                      sv, ss, wait_v=None):
        p = self.prog

        def s1(e):
            if wait_v is not None:
                e.wait_ge(sv.h, wait_v)
            e.sign(sgn_tmp, x_ap).then_inc(ss.h, 1)
        p.em("scalar", s1)
        ss.bump(1)
        s1n = ss.n

        def v2(e):
            e.wait_ge(ss.h, s1n)
            e.tensor_mul(y_tmp, x_ap, sgn_tmp)
            e.tensor_scalar(y_tmp, y_tmp, inv_ap, 0.5, ALU.mult, ALU.add)
            self.emit_floor_pos(e, y_tmp, i_tmp, f_tmp, q_out)
        p.em("vector", v2)

    def emit_stage_C(self):
        p = self.prog
        at = self.at
        o = 0
        def nxt(name, shape, dtype):
            nonlocal o
            t = at(name, shape, dtype, o)
            sz = shape[-1] * mybir.dt.size(dtype)
            o += (sz + 31) // 32 * 32
            return t
        Mq_sb = nxt("c_Mq", [128, 8 * 512], BF16)
        Mk_sb = nxt("c_Mk", [128, 8 * 512], BF16)
        Mv_sb = nxt("c_Mv", [128, 8 * 1024], BF16)
        lnw = nxt("c_lnw", [128, 2 * 8 * 512], BF16)
        qT = nxt("c_qT", [128, 4096], BF16)
        kT = nxt("c_kT", [128, 4096], BF16)
        vP = nxt("c_vP", [128, 32 * 256], BF16)
        EB_off = o
        EB = nxt("c_EB", [128, 8 * 4096], BF16)
        oT = nxt("c_oT", [128, 2 * 4096], F32)
        vp = nxt("c_vp", [128, 8 * 128], BF16)
        zac = nxt("c_zac", [128, 16], F32)
        stat = nxt("c_stat", [128, 16], F32)
        oTb = at("c_oTb", [128, 4096], BF16, EB_off + 20480)
        atst = at("c_atst", [128, 8 * 128], BF16, EB_off + 28672)
        x2t = at("c_x2t", [128, 2 * 1024], F32, EB_off)
        ln2f = at("c_ln2f", [128, 1024], F32, EB_off + 8192)
        ln2b = at("c_ln2b", [128, 2 * 1024], BF16, EB_off + 12288)
        sqt = at("c_sqt", [128, 1024], F32, EB_off + 16384)
        tT = at("c_tT", [128, 2 * 1024], BF16, EB_off + 20480)

        s_ld = self.sem("C_ld")
        s_tp = self.sem("C_tp")
        s_v = self.sem("C_v")
        s_s = self.sem("C_s")
        s_mm = self.sem("C_mm")
        s_st = self.sem("C_st")
        s_gld = self.sem("C_gld")
        pb = self.pbank
        pbh = self.pbh
        deps = getattr(self, "dep_C", [])

        c0_v = {}
        c0_tr = {}
        c0_ev = {}
        for i in range(NT):
            par = i % 2
            x2a = x2t[:, par * 1024:par * 1024 + 1024]
            lb = ln2b[:, par * 1024:par * 1024 + 1024]
            tTa = tT[:, par * 1024:par * 1024 + 1024]

            def sy(e, i=i, x2a=x2a):
                if i == 0:
                    for (ds, dv) in deps:
                        e.wait_ge(ds.h, dv)
                if i >= 2:
                    e.wait_ge(s_v.h, c0_v[i - 2])
                e.dma_start(out=x2a, in_=self.d_x2[128 * i:128 * i + 128, :]
                            ).then_inc(s_ld.h, 16)
            p.em("sync", sy)
            s_ld.bump(16)
            ld_snap = s_ld.n

            def vw(e, ld_snap=ld_snap, trp=c0_tr.get(i - 2)):
                e.wait_ge(s_ld.h, ld_snap)
                if trp is not None:
                    e.wait_ge(s_mm.h, trp)
            p.em("vector", vw)
            vend = self.emit_ln(x2a, ln2f[:, :], stat, sqt[:, :], s_v, s_s,
                                fin_ap=lb)
            c0_v[i] = vend

            def tr0(e, i=i, lb=lb, vend=vend, evp=c0_ev.get(i - 2)):
                e.wait_ge(s_v.h, vend)
                if evp is not None:
                    e.wait_ge(s_s.h, evp)
                last = None
                for c in range(8):
                    last = e.matmul(pbh[2 + i % 2][:, 128 * c:128 * c + 128],
                                    lb[:, 128 * c:128 * c + 128],
                                    self.eyebf[:, :], start=True, stop=True)
                last.then_inc(s_mm.h, 1)
            p.em("tensor", tr0)
            s_mm.bump(1)
            c0_tr[i] = s_mm.n

            def ev0(e, i=i, tTa=tTa, trn=c0_tr[i]):
                e.wait_ge(s_mm.h, trn)
                if i >= 2:
                    e.wait_ge(s_st.h, 128 * (i - 1))
                e.activation(tTa, pbh[2 + i % 2][:, :],
                             AF.Copy).then_inc(s_s.h, 1)
            p.em("scalar", ev0)
            s_s.bump(1)
            c0_ev[i] = s_s.n

            def sy2(e, i=i, tTa=tTa, evn=c0_ev[i]):
                e.wait_ge(s_s.h, evn)
                for c in range(8):
                    e.dma_start(
                        out=self.d_ln2[128 * c:128 * c + 128,
                                       128 * i:128 * i + 128],
                        in_=tTa[:, 128 * c:128 * c + 128]).then_inc(s_st.h, 16)
            p.em("sync", sy2)
            s_st.bump(16 * 8)
        c0_done = s_st.n

        def gld(e):
            for (ds, dv) in deps:
                e.wait_ge(ds.h, dv)
            e.dma_start(out=Mq_sb[:, :].rearrange("p (c f) -> p c f", f=512),
                        in_=self.d_Mq.rearrange("(c p) f -> p c f", p=128)
                        ).then_inc(s_gld.h, 16)
            e.dma_start(out=Mk_sb[:, :].rearrange("p (c f) -> p c f", f=512),
                        in_=self.d_Mk.rearrange("(c p) f -> p c f", p=128)
                        ).then_inc(s_gld.h, 16)
            e.dma_start(out=Mv_sb[:, :].rearrange("p (c f) -> p c f", f=1024),
                        in_=self.d_Mv.rearrange("(c p) f -> p c f", p=128)
                        ).then_inc(s_gld.h, 16)
        p.em("gpsimd", gld)
        s_gld.bump(48)
        m_ready = s_gld.n

        cw_mm = {}
        cw_ev = {}
        pair_done = {-1: 0}
        gseq_s = {}
        out_mm = {}
        out_v = {}
        vz_v = {}
        add_v = {}
        store_n = {}

        for pr in range(4):
            for w in range(8):
                wpar = w % 2
                lw = lnw[:, wpar * 4096:wpar * 4096 + 4096]

                def syw(e, pr=pr, w=w, lw=lw):
                    if pr == 0 and w == 0:
                        e.wait_ge(s_st.h, c0_done)
                    idx = pr * 8 + w
                    if idx >= 2:
                        e.wait_ge(s_mm.h, cw_mm[idx - 2])
                    for c in range(8):
                        e.dma_start(
                            out=lw[:, 512 * c:512 * c + 512],
                            in_=self.d_ln2[128 * c:128 * c + 128,
                                           512 * w:512 * w + 512]
                            ).then_inc(s_tp.h, 16)
                p.em("sync", syw)
                s_tp.bump(16 * 8)
                tp_snap = s_tp.n

                def tew(e, pr=pr, w=w, lw=lw, tp_snap=tp_snap):
                    e.wait_ge(s_tp.h, tp_snap)
                    if pr == 0 and w == 0:
                        e.wait_ge(s_gld.h, m_ready)
                    if pr > 0 and w == 0:
                        e.wait_ge(s_st.h, pair_done[pr - 1])
                    idx = pr * 8 + w
                    if idx >= 1:
                        e.wait_ge(s_v.h, cw_ev[idx - 1])
                    for c in range(8):
                        e.matmul(pb(0), Mq_sb[:, 512 * c + 128 * pr:512 * c + 128 * pr + 128],
                                 lw[:, 512 * c:512 * c + 512],
                                 start=(c == 0), stop=(c == 7))
                    for c in range(8):
                        e.matmul(pb(1), Mk_sb[:, 512 * c + 128 * pr:512 * c + 128 * pr + 128],
                                 lw[:, 512 * c:512 * c + 512],
                                 start=(c == 0), stop=(c == 7))
                    last = None
                    for j in range(4):
                        for c in range(8):
                            last = e.matmul(
                                pb(2 + j // 2, 0, 128, 256 * (j % 2),
                                   256 * (j % 2) + 256),
                                lw[:, 512 * c + 128 * j:512 * c + 128 * j + 128],
                                Mv_sb[:, 1024 * c + 256 * pr:1024 * c + 256 * pr + 256],
                                start=(c == 0), stop=(c == 7))
                    last.then_inc(s_mm.h, 1)
                p.em("tensor", tew)
                s_mm.bump(1)
                cw_mm[pr * 8 + w] = s_mm.n
                mmw = s_mm.n

                def vew(e, w=w, mmw=mmw):
                    e.wait_ge(s_mm.h, mmw)
                    e.tensor_copy(qT[:, 512 * w:512 * w + 512], pb(0))
                    e.tensor_copy(kT[:, 512 * w:512 * w + 512], pb(1))
                    ins = e.tensor_copy(vP[:, 1024 * w:1024 * w + 1024],
                                        pbh[1][:, :])
                    ins.then_inc(s_v.h, 1)
                p.em("vector", vew)
                s_v.bump(1)
                cw_ev[pr * 8 + w] = s_v.n
            qkv_v = cw_ev[pr * 8 + 7]

            gseq = 0
            for jb in range(16):
                ebpar = jb % 2
                for t2 in range(2):
                    jt = 2 * jb + t2
                    for g in range(4):
                        myseq = gseq

                        def tes(e, jt=jt, g=g, myseq=myseq, jb=jb, t2=t2,
                                qkv_v=qkv_v, pr=pr):
                            if myseq == 0:
                                e.wait_ge(s_v.h, qkv_v)
                            if myseq >= 2:
                                e.wait_ge(s_s.h, gseq_s[(pr, myseq - 2)])
                            if t2 == 0 and g == 0 and jb >= 1:
                                e.wait_ge(s_v.h, out_v[(pr, jb - 1)])
                            last = None
                            for hh in range(2):
                                psb = pbh[(myseq % 2) + 2 * hh]
                                for n2 in range(2):
                                    last = e.matmul(
                                        psb[:, 512 * n2:512 * n2 + 512],
                                        kT[64 * hh:64 * hh + 64,
                                           128 * jt:128 * jt + 128],
                                        qT[64 * hh:64 * hh + 64,
                                           1024 * g + 512 * n2:1024 * g + 512 * n2 + 512],
                                        start=True, stop=True,
                                        tile_position=(64 * hh, 0))
                            last.then_inc(s_mm.h, 1)
                        p.em("tensor", tes)
                        s_mm.bump(1)
                        mseq = s_mm.n

                        def sce(e, g=g, myseq=myseq, jb=jb, t2=t2,
                                mseq=mseq, ebpar=ebpar, pr=pr):
                            e.wait_ge(s_mm.h, mseq)
                            if t2 == 0 and g == 0:
                                if jb >= 2:
                                    e.wait_ge(s_mm.h, out_mm[(pr, jb - 2)])
                                if jb >= 1:
                                    e.wait_ge(s_v.h, vz_v[(pr, jb - 1)])
                            last = None
                            for hh in range(2):
                                slot = ((ebpar * 2 + t2) * 2 + hh) * 4096
                                last = e.activation(
                                    EB[:, slot + 1024 * g:slot + 1024 * g + 1024],
                                    pbh[(myseq % 2) + 2 * hh][:, :], AF.Exp,
                                    scale=0.125,
                                    accum_out=zac[:, (t2 * 2 + hh) * 4 + g:
                                                  (t2 * 2 + hh) * 4 + g + 1])
                            last.then_inc(s_s.h, 1)
                        p.em("scalar", sce)
                        s_s.bump(1)
                        gseq_s[(pr, myseq)] = s_s.n
                        gseq += 1
                sexp = s_s.n

                def vz(e, jb=jb, ebpar=ebpar, sexp=sexp, pr=pr):
                    e.wait_ge(s_s.h, sexp)
                    if jb >= 2:
                        e.wait_ge(s_mm.h, out_mm[(pr, jb - 2)])
                    ins = None
                    for t2 in range(2):
                        jt = 2 * jb + t2
                        for hh in range(2):
                            zi = (t2 * 2 + hh) * 4
                            e.reduce_sum(stat[:, 0:1], zac[:, zi:zi + 4], AX.X)
                            self.vfence(e)
                            e.reciprocal(stat[:, 1:2], stat[:, 0:1])
                            self.vfence(e)
                            slot = ((ebpar * 2 + t2) * 2 + hh) * 128
                            ins = e.tensor_scalar(
                                vp[:, slot:slot + 128],
                                vP[:, 256 * jt + 128 * hh:256 * jt + 128 * hh + 128],
                                stat[:, 1:2], None, ALU.mult)
                    ins.then_inc(s_v.h, 1)
                p.em("vector", vz)
                s_v.bump(1)
                vz_v[(pr, jb)] = s_v.n
                vpn = s_v.n

                for hh in range(2):
                    for it in range(8):
                        seqi = hh * 8 + it

                        def teo(e, hh=hh, it=it, seqi=seqi, ebpar=ebpar,
                                vpn=vpn, pr=pr, jb=jb):
                            if seqi == 0:
                                e.wait_ge(s_v.h, vpn)
                            if seqi >= 2:
                                e.wait_ge(s_v.h, add_v[(pr, jb, seqi - 2)])
                            last = None
                            for t2 in range(2):
                                slotv = ((ebpar * 2 + t2) * 2 + hh) * 128
                                slote = ((ebpar * 2 + t2) * 2 + hh) * 4096
                                last = e.matmul(
                                    pbh[seqi % 2][:, 0:512],
                                    vp[:, slotv:slotv + 128],
                                    EB[:, slote + 512 * it:slote + 512 * it + 512],
                                    start=(t2 == 0), stop=(t2 == 1))
                            last.then_inc(s_mm.h, 1)
                        p.em("tensor", teo)
                        s_mm.bump(1)
                        mo = s_mm.n

                        def veo(e, jb=jb, hh=hh, it=it, seqi=seqi, mo=mo):
                            e.wait_ge(s_mm.h, mo)
                            dst = oT[:, 4096 * hh + 512 * it:
                                     4096 * hh + 512 * it + 512]
                            if jb == 0:
                                ins = e.tensor_copy(dst, pbh[seqi % 2][:, 0:512])
                            else:
                                ins = e.tensor_add(dst, dst,
                                                   pbh[seqi % 2][:, 0:512])
                            ins.then_inc(s_v.h, 1)
                        p.em("vector", veo)
                        s_v.bump(1)
                        add_v[(pr, jb, seqi)] = s_v.n
                out_mm[(pr, jb)] = s_mm.n
                out_v[(pr, jb)] = s_v.n

            for hh in range(2):
                def vcast(e, hh=hh, trp=store_n.get((pr, hh - 1, "tr"))):
                    if trp is not None:
                        e.wait_ge(s_mm.h, trp)
                    ins = e.tensor_copy(oTb[:, :],
                                        oT[:, 4096 * hh:4096 * hh + 4096])
                    ins.then_inc(s_v.h, 1)
                p.em("vector", vcast)
                s_v.bump(1)
                cv = s_v.n
                for b in range(8):
                    if b >= 2:
                        kprev = (pr, hh, b - 2)
                    elif hh == 1:
                        kprev = (pr, 0, 6 + b)
                    else:
                        kprev = (pr - 1, 1, 6 + b)

                    def trh(e, b=b, cv=cv,
                            evp=store_n.get((kprev[0], kprev[1], "ev",
                                             kprev[2]))):
                        if b == 0:
                            e.wait_ge(s_v.h, cv)
                        if evp is not None:
                            e.wait_ge(s_s.h, evp)
                        last = None
                        for t in range(4):
                            last = e.matmul(
                                pbh[2 + b % 2][:, 128 * t:128 * t + 128],
                                oTb[:, 128 * (4 * b + t):128 * (4 * b + t) + 128],
                                self.eyebf[:, :], start=True, stop=True)
                        last.then_inc(s_mm.h, 1)
                    p.em("tensor", trh)
                    s_mm.bump(1)
                    trn = s_mm.n

                    def evh(e, b=b, trn=trn,
                            stp=store_n.get((kprev[0], kprev[1], "st",
                                             kprev[2]))):
                        e.wait_ge(s_mm.h, trn)
                        if stp is not None:
                            e.wait_ge(s_st.h, stp)
                        e.activation(atst[:, 512 * (b % 2):512 * (b % 2) + 512],
                                     pbh[2 + b % 2][:, 0:512],
                                     AF.Copy).then_inc(s_s.h, 1)
                    p.em("scalar", evh)
                    s_s.bump(1)
                    store_n[(pr, hh, "ev", b)] = s_s.n

                    def sth(e, b=b, pr=pr, hh=hh, evn=s_s.n):
                        e.wait_ge(s_s.h, evn)
                        col = 128 * (2 * pr + hh)
                        for t in range(4):
                            i = 4 * b + t
                            e.dma_start(
                                out=self.d_attn[128 * i:128 * i + 128,
                                                col:col + 128],
                                in_=atst[:, 512 * (b % 2) + 128 * t:
                                         512 * (b % 2) + 128 * t + 128]
                                ).then_inc(s_st.h, 16)
                    p.em("sync", sth)
                    s_st.bump(16 * 4)
                    store_n[(pr, hh, "st", b)] = s_st.n
                store_n[(pr, hh, "tr")] = s_mm.n
            pair_done[pr] = s_st.n

        def syend(e):
            e.wait_ge(s_st.h, s_st.n)
        p.em("sync", syend)
        self.C_done = (s_st, s_st.n)
        if self.debug_taps:
            for nm, t in (("qT", qT), ("kT", kT), ("vP", vP), ("oT", oT),
                          ("zac", zac), ("vp", vp)):
                self.tap(nm, t[tuple(slice(None) for _ in t.shape)],
                         wait_sems=(s_v, s_st, s_tp, s_s))

    def emit_stage_A(self):
        p = self.prog
        at = self.at
        o = 0
        offs = {}
        def nxt(name, shape, dtype):
            nonlocal o
            offs[name] = o
            t = at(name, shape, dtype, o)
            sz = shape[-1] * mybir.dt.size(dtype)
            o += (sz + 31) // 32 * 32
            return t
        Wq_bf = nxt("a_Wq", [128, 8 * 1024], BF16)
        Wk_bf = nxt("a_Wk", [128, 8 * 1024], BF16)
        Wv_bf = nxt("a_Wv", [128, 8 * 1024], BF16)
        Wo_bf = nxt("a_Wo", [128, 8 * 1024], BF16)
        xnT = nxt("a_xnT", [128, 8 * 512], BF16)
        qTs = nxt("a_qTs", [128, 2 * 4096], BF16)
        kS = nxt("a_kS", [128, 2 * 4096], BF16)
        vS = nxt("a_vS", [128, 2 * 4096], BF16)
        Sb = nxt("a_Sb", [128, 512], F32)
        Gb = nxt("a_Gb", [128, 512], F32)
        Hb = nxt("a_Hb", [128, 512], F32)
        Xb = nxt("a_Xb", [128, 512], F32)
        T2b = nxt("a_T2b", [128, 512], F32)
        Wb = nxt("a_Wb", [128, 512], F32)
        WTb = T2b
        Sbf = nxt("a_Sbf", [128, 512], BF16)
        dcol = nxt("a_dcol", [128, 8], F32)
        Bbd = nxt("a_Bbd", [128, 128], F32)
        N16b = nxt("a_N16b", [16, 128], F32)
        X8b = nxt("a_X8b", [16, 128], F32)
        T8b = nxt("a_T8b", [16, 128], F32)
        d16 = nxt("a_d16", [16, 8], F32)
        C1b = nxt("a_C1b", [16, 512], F32)
        D1b = nxt("a_D1b", [16, 512], F32)
        BTb = at("a_BTb", [16, 1024], F32, offs["a_Gb"])
        alph = nxt("a_alph", [2, 8], F32)
        al8 = alph
        oTb = nxt("a_oTb", [128, 1024], BF16)
        xt = nxt("a_xt", [128, 2 * 1024], F32)
        ln0 = nxt("a_ln0", [128, 1024], F32)
        xnf = nxt("a_xnf", [128, 1024], F32)
        sq = nxt("a_sq", [128, 1024], F32)
        xnb = nxt("a_xnb", [128, 1024], BF16)
        xplo = nxt("a_xplo", [128, 1024], F32)
        xplt = nxt("a_xplt", [128, 4 * 1024], F32)
        x1o = nxt("a_x1o", [128, 1024], F32)
        stat = dcol

        s_ld = self.sem("A_ld")
        s_sp = self.sem("A_sp")
        s_xp = self.sem("A_xp")
        s_tp = self.sem("A_tp")
        s_v = self.sem("A_v")
        s_s = self.sem("A_s")
        s_mm = self.sem("A_mm")
        s_st = self.sem("A_st")
        s_gld = self.sem("A_gld")
        pb = self.pbank
        pbh = self.pbh
        iv = self.init_v
        s_iv = self.sem("init_v")

        G64 = [float(g) for g in NS64_GAMMAS] + [1.0]
        G8 = [float(g) for g in ns_schedule(5e-3)] + [1.0]

        def gld(e):
            for w_d, w_sb in ((self.d_Wq, Wq_bf), (self.d_Wk, Wk_bf),
                              (self.d_Wv, Wv_bf), (self.d_Wo, Wo_bf)):
                e.dma_start(out=w_sb[:, :].rearrange("p (c f) -> p c f", f=1024),
                            in_=w_d.rearrange("(c p) f -> p c f", p=128)
                            ).then_inc(s_gld.h, 16)
        p.em("gpsimd", gld)
        s_gld.bump(64)
        w_ready = s_gld.n

        def v0(e, iv=iv):
            e.wait_ge(s_iv.h, iv)
            e.memset(Sb[:, :], 0.0).then_inc(s_v.h, 1)
        p.em("vector", v0)
        s_v.bump(1)
        consts_v = s_v.n

        a0_v = {}
        a1_mm = {}
        a1_ev = {}
        scan_mm = {}
        x1_v = {}

        for s in range(A_SLABS):
            for j in range(4):
                i = 4 * s + j
                par = i % 2
                xta = xt[:, 1024 * par:1024 * par + 1024]
                xpa = xplo[:, 0:1024]

                def syx(e, i=i, xta=xta):
                    if i >= 2:
                        e.wait_ge(s_sp.h, 16 * (i - 1))
                    e.dma_start(out=xta, in_=self.d_x[128 * i:128 * i + 128, :]
                                ).then_inc(s_ld.h, 16)
                p.em("sync", syx)
                s_ld.bump(16)
                lds = s_ld.n

                def vx(e, lds=lds, i=i):
                    e.wait_ge(s_ld.h, lds)
                    if i == 0:
                        e.wait_ge(s_v.h, consts_v)
                p.em("vector", vx)
                self.emit_ln(xta, ln0[:, :], stat, sq[:, :], s_v, s_s)
                self.emit_ln(ln0[:, :], xnf[:, :], stat, sq[:, :], s_v, s_s)

                stp = s_sp.n

                def vaff(e, xta=xta, xpa=xpa, i=i, stp=stp):
                    if i >= 1:
                        e.wait_ge(s_sp.h, stp)
                    e.tensor_mul(xnf[:, :], xnf[:, :], self.gb[:, :])
                    e.tensor_add(xnb[:, :], xnf[:, :], self.bb[:, :])
                    ins = e.tensor_add(xpa, xta, ln0[:, :])
                    ins.then_inc(s_v.h, 1)
                p.em("vector", vaff)
                s_v.bump(1)
                a0_v[i] = s_v.n

                def syp(e, i=i, xpa=xpa, av=a0_v[i]):
                    e.wait_ge(s_v.h, av)
                    e.dma_start(out=self.d_xpl[128 * i:128 * i + 128, :],
                                in_=xpa).then_inc(s_sp.h, 16)
                p.em("sync", syp)
                s_sp.bump(16)

                def syt(e, j=j, av=a0_v[i], s=s, i=i):
                    e.wait_ge(s_v.h, av)
                    if s >= 1 and j == 0:
                        e.wait_ge(s_mm.h, a1_mm[s - 1])
                    for c in range(8):
                        e.dma_start_transpose(
                            xnT[:, 512 * c + 128 * j:512 * c + 128 * j + 128],
                            xnb[:, 128 * c:128 * c + 128]).then_inc(s_tp.h, 16)
                p.em("scalar", syt)
                s_tp.bump(16 * 8)
            tpA = s_tp.n
            spar = s % 2

            def teA(e, s=s, spar=spar, tpA=tpA):
                e.wait_ge(s_tp.h, tpA)
                if s == 0:
                    e.wait_ge(s_gld.h, w_ready)
                if s >= 2:
                    e.wait_ge(s_mm.h, scan_mm[4 * (s - 1) - 1 + 4])
                last = None
                for dt in range(8):
                    bk = 6 + dt % 2
                    if dt >= 2:
                        e.wait_ge(s_s.h, a1_ev.get((s, dt - 2), 0))
                    for c in range(8):
                        last = e.matmul(pb(bk),
                                        Wq_bf[:, 1024 * c + 128 * dt:1024 * c + 128 * dt + 128],
                                        xnT[:, 512 * c:512 * c + 512],
                                        start=(c == 0), stop=(c == 7))
                    last.then_inc(s_mm.h, 1)
            for dt in range(8):
                def teQ(e, s=s, dt=dt, tpA=tpA, spar=spar):
                    if dt == 0:
                        e.wait_ge(s_tp.h, tpA)
                        if s == 0:
                            e.wait_ge(s_gld.h, w_ready)
                        if s >= 2:
                            e.wait_ge(s_v.h, x1_v[4 * s - 5])
                    if dt >= 2:
                        e.wait_ge(s_s.h, a1_ev[(s, dt - 2)])
                    bk = 6 + dt % 2
                    last = None
                    for c in range(8):
                        last = e.matmul(pb(bk),
                                        Wq_bf[:, 1024 * c + 128 * dt:1024 * c + 128 * dt + 128],
                                        xnT[:, 512 * c:512 * c + 512],
                                        start=(c == 0), stop=(c == 7))
                    last.then_inc(s_mm.h, 1)
                p.em("tensor", teQ)
                s_mm.bump(1)
                mq = s_mm.n

                def scQ(e, dt=dt, spar=spar, mq=mq):
                    e.wait_ge(s_mm.h, mq)
                    bk = 6 + dt % 2
                    e.activation(qTs[:, 4096 * spar + 512 * dt:4096 * spar + 512 * dt + 512],
                                 pb(bk), AF.Copy, scale=0.125).then_inc(s_s.h, 1)
                p.em("scalar", scQ)
                s_s.bump(1)
                a1_ev[(s, dt)] = s_s.n
            for kv, (Wsb, dst) in enumerate(((Wk_bf, kS), (Wv_bf, vS))):
                for j in range(4):
                    for half in range(2):
                        idx = 8 + kv * 8 + j * 2 + half

                        def teK(e, j=j, half=half, Wsb=Wsb, spar=spar,
                                kv=kv, s=s, idx=idx):
                            bk = 6 + idx % 2
                            e.wait_ge(s_s.h, a1_ev[(s, idx - 2)])
                            last = None
                            for c in range(8):
                                last = e.matmul(
                                    pb(bk),
                                    xnT[:, 512 * c + 128 * j:512 * c + 128 * j + 128],
                                    Wsb[:, 1024 * c + 512 * half:1024 * c + 512 * half + 512],
                                    start=(c == 0), stop=(c == 7))
                            last.then_inc(s_mm.h, 1)
                        p.em("tensor", teK)
                        s_mm.bump(1)
                        mk = s_mm.n

                        def scK(e, j=j, half=half, dst=dst, spar=spar,
                                mk=mk, idx=idx):
                            e.wait_ge(s_mm.h, mk)
                            bk = 6 + idx % 2
                            e.activation(
                                dst[:, 4096 * spar + 1024 * j + 512 * half:
                                    4096 * spar + 1024 * j + 512 * half + 512],
                                pb(bk), AF.Copy).then_inc(s_s.h, 1)
                        p.em("scalar", scK)
                        s_s.bump(1)
                        a1_ev[(s, idx)] = s_s.n
            a1_mm[s] = s_mm.n
            a1_done_s = a1_ev[(s, 23)]

            for j in range(4):
                i = 4 * s + j

                def syxp(e, i=i, j=j):
                    if i >= 4:
                        e.wait_ge(s_v.h, x1_v.get(i - 4, 0))
                    e.wait_ge(s_sp.h, 16 * (i + 1))
                    e.dma_start(out=xplt[:, 1024 * (i % 4):1024 * (i % 4) + 1024],
                                in_=self.d_xpl[128 * i:128 * i + 128, :]
                                ).then_inc(s_xp.h, 16)
                p.em("sync", syxp)
                s_xp.bump(16)
                x1_v[("xp", i)] = s_xp.n

            for j in range(4):
                t = 4 * s + j
                self._emit_scan_step(
                    t, s, j, spar, a1_done_s, G64, G8,
                    dict(qTs=qTs, kS=kS, vS=vS, Sb=Sb, Gb=Gb, Hb=Hb, Xb=Xb,
                         T2b=T2b, Wb=Wb, WTb=WTb, Sbf=Sbf,
                         dcol=dcol, Bbd=Bbd, N16b=N16b, X8b=X8b, T8b=T8b,
                         d16=d16, C1b=C1b, D1b=D1b, BTb=BTb, alph=alph,
                         al8=al8, oTb=oTb, Wo_bf=Wo_bf, xplt=xplt, x1o=x1o),
                    (s_v, s_s, s_mm, s_st, s_xp), scan_mm, x1_v)

        def syend(e):
            e.wait_ge(s_st.h, s_st.n)
        p.em("sync", syend)
        self.A_done = (s_st, s_st.n)
        if self.debug_taps:
            for nm, tt in (("Sb", Sb), ("Gb", Gb), ("Xb", Xb), ("Wb", Wb),
                           ("Bbd", Bbd), ("N16b", N16b), ("X8b", X8b),
                           ("C1b", C1b), ("alph", alph), ("al8", al8),
                           ("qTs", qTs), ("kS", kS), ("vS", vS), ("oTb", oTb)):
                self.tap(nm, tt[tuple(slice(None) for _ in tt.shape)],
                         wait_sems=(s_v, s_st, s_s))

    def _emit_scan_step(self, t, s, j, spar, a1_done, G64, G8, B, sems,
                        scan_mm, x1_v):
        p = self.prog
        s_v, s_s, s_mm, s_st, s_xp = sems
        pb = self.pbank
        pbh = self.pbh
        qTs, kS, vS = B["qTs"], B["kS"], B["vS"]
        Sb, Gb, Hb, Xb, T2b = B["Sb"], B["Gb"], B["Hb"], B["Xb"], B["T2b"]
        Wb, WTb, Sbf = B["Wb"], B["WTb"], B["Sbf"]
        dcol, Bbd, N16b, X8b = B["dcol"], B["Bbd"], B["N16b"], B["X8b"]
        T8b, d16, C1b, D1b, BTb = B["T8b"], B["d16"], B["C1b"], B["D1b"], B["BTb"]
        alph, al8, oTb, Wo_bf = B["alph"], B["al8"], B["oTb"], B["Wo_bf"]
        xplt, x1o = B["xplt"], B["x1o"]

        def lane(r, c):
            return (64 * r, 64 * c)

        def Kc(h):
            return kS[64 * 0:128, 4096 * spar + 1024 * j + 64 * h:
                      4096 * spar + 1024 * j + 64 * h + 64]

        def Vc(h):
            return vS[:, 4096 * spar + 1024 * j + 64 * h:
                      4096 * spar + 1024 * j + 64 * h + 64]

        def QcT(h):
            r, c = h % 2, h // 2
            return qTs[64 * r:64 * r + 64,
                       4096 * spar + 512 * c + 128 * j:
                       4096 * spar + 512 * c + 128 * j + 128]

        def lslice(buf, h, w=64):
            r, c = h % 2, h // 2
            return buf[64 * r:64 * r + 64, w * c:w * c + w]

        def TEphase(name, fn, wait_v=None, wait_s=None, extra=None):
            def te(e, fn=fn, wait_v=wait_v, wait_s=wait_s, extra=extra):
                if wait_v is not None:
                    e.wait_ge(s_v.h, wait_v)
                if wait_s is not None:
                    e.wait_ge(s_s.h, wait_s)
                if extra is not None:
                    extra(e)
                last = fn(e)
                last.then_inc(s_mm.h, 1)
            p.em("tensor", te)
            s_mm.bump(1)
            return s_mm.n

        def p1(e):
            last = None
            for h in range(16):
                r, c = h % 2, h // 2
                e.matmul(pb(0, 64 * r, 64 * r + 64, 64 * c, 64 * c + 64),
                         Kc(h), Kc(h), start=True, stop=True,
                         tile_position=(0, 64 * r))
            for h in range(16):
                r, c = h % 2, h // 2
                last = e.matmul(pb(1, 64 * r, 64 * r + 64, 64 * c, 64 * c + 64),
                                Kc(h), Vc(h), start=True, stop=True,
                                tile_position=(0, 64 * r))
            return last
        def p1_extra(e, t=t):
            e.wait_ge(s_s.h, a1_done)
            if t >= 1:
                e.wait_ge(s_v.h, x1_v[t - 1])
        m1 = TEphase("GH", p1, extra=p1_extra)

        def v2(e, m1=m1):
            e.wait_ge(s_mm.h, m1)
            e.scalar_tensor_tensor(Gb[:, :], self.i64b[:, :], 1e-4,
                                   pb(0, 0, 128, 0, 512), ALU.mult,
                                   ALU.add).then_inc(s_v.h, 1)
        p.em("vector", v2)
        s_v.bump(1)
        v2n = s_v.n

        def s2(e, m1=m1):
            e.wait_ge(s_mm.h, m1)
            e.activation(Hb[:, :], pb(1, 0, 128, 0, 512), AF.Copy).then_inc(s_s.h, 1)
        p.em("scalar", s2)
        s_s.bump(1)
        s2n = s_s.n

        def p3(e):
            last = None
            for h in range(16):
                r, c = h % 2, h // 2
                last = e.matmul(pb(2, 64 * r, 64 * r + 64, 64 * c, 64 * c + 64),
                                lslice(Gb, h), lslice(Gb, h), start=True,
                                stop=True, tile_position=(64 * r, 64 * r))
            return last
        m3 = TEphase("G2", p3, wait_v=v2n)

        def v4(e, m3=m3):
            e.wait_ge(s_mm.h, m3)
            e.tensor_mul(T2b[:, :], pb(2, 0, 128, 0, 512), self.i64b[:, :])
            e.tensor_reduce(dcol[:, 0:8],
                            T2b[:, :].rearrange("p (c f) -> p c f", f=64),
                            AX.X, ALU.add)
            self.vfence(e)
            e.tensor_scalar_add(self.fence_t[:, :], self.fence_t[:, :],
                                0.0).then_inc(s_v.h, 1)
        p.em("vector", v4)
        s_v.bump(1)
        v4n = s_v.n

        def p5(e):
            return e.matmul(pb(3, 0, 2, 0, 8), self.sel64[:, 0:2],
                            dcol[:, 0:8], start=True, stop=True)
        m5 = TEphase("tr", p5, wait_v=v4n)

        def s6(e, m5=m5):
            e.wait_ge(s_mm.h, m5)
            e.activation(alph[0:2, 0:8], pb(3, 0, 2, 0, 8), AF.Sqrt,
                         scale=1.0101).then_inc(s_s.h, 1)
        p.em("scalar", s6)
        s_s.bump(1)
        s6n = s_s.n

        def v7(e, s6n=s6n):
            e.wait_ge(s_s.h, s6n)
            self.vfence(e)
            e.reciprocal(alph[0:2, 0:8], alph[0:2, 0:8])
            self.vfence(e)
            e.tensor_scalar_add(self.fence_t[:, :], self.fence_t[:, :],
                                0.0).then_inc(s_v.h, 1)
        p.em("vector", v7)
        s_v.bump(1)
        v7n = s_v.n

        def p8(e):
            return e.matmul(pb(3, 0, 128, 0, 512), self.sel2x128[0:2, 0:128],
                            alph[0:2, 0:8].unsqueeze(-1).broadcast_to((2, 8, 64)),
                            start=True, stop=True)
        m8 = TEphase("ab", p8, wait_v=v7n)

        def v9(e, m8=m8, g0=G64[0]):
            e.wait_ge(s_mm.h, m8)
            e.tensor_mul(Gb[:, :], Gb[:, :], pb(3, 0, 128, 0, 512))
            e.tensor_mul(Hb[:, :], Hb[:, :], pb(3, 0, 128, 0, 512))
            e.tensor_scalar_mul(Xb[:, :], Gb[:, :], -g0 * g0)
            e.scalar_tensor_tensor(Xb[:, :], self.i64b[:, :], 2.0 * g0,
                                   Xb[:, :], ALU.mult, ALU.add).then_inc(s_v.h, 1)
        p.em("vector", v9)
        s_v.bump(1)
        lastv = s_v.n
        lasts = s2n

        for g in G64[1:]:
            def pa(e):
                last = None
                for h in range(16):
                    r, c = h % 2, h // 2
                    last = e.matmul(pb(2, 64 * r, 64 * r + 64, 64 * c, 64 * c + 64),
                                    lslice(Gb, h), lslice(Xb, h), start=True,
                                    stop=True, tile_position=(64 * r, 64 * r))
                return last
            ma = TEphase("nsT", pa, wait_v=lastv, wait_s=lasts)

            def vb(e, ma=ma, g=g):
                e.wait_ge(s_mm.h, ma)
                e.tensor_scalar_mul(T2b[:, :], pb(2, 0, 128, 0, 512), -g * g)
                e.scalar_tensor_tensor(T2b[:, :], self.i64b[:, :], 2.0 * g,
                                       T2b[:, :], ALU.mult, ALU.add).then_inc(s_v.h, 1)
            p.em("vector", vb)
            s_v.bump(1)
            vbn = s_v.n

            def pc(e):
                last = None
                for h in range(16):
                    r, c = h % 2, h // 2
                    last = e.matmul(pb(3, 64 * r, 64 * r + 64, 64 * c, 64 * c + 64),
                                    lslice(Xb, h), lslice(T2b, h), start=True,
                                    stop=True, tile_position=(64 * r, 64 * r))
                return last
            mc = TEphase("nsX", pc, wait_v=vbn)

            def sd(e, mc=mc):
                e.wait_ge(s_mm.h, mc)
                e.activation(Xb[:, :], pb(3, 0, 128, 0, 512),
                             AF.Copy).then_inc(s_s.h, 1)
            p.em("scalar", sd)
            s_s.bump(1)
            lasts = s_s.n
            lastv = vbn

        def p11(e):
            last = None
            for h in range(16):
                r, c = h % 2, h // 2
                last = e.matmul(pb(2, 64 * r, 64 * r + 64, 64 * c, 64 * c + 64),
                                lslice(Xb, h), lslice(Hb, h), start=True,
                                stop=True, tile_position=(64 * r, 64 * r))
            return last
        m11 = TEphase("W", p11, wait_s=lasts)

        def s12(e, m11=m11):
            e.wait_ge(s_mm.h, m11)
            e.activation(Wb[:, :], pb(2, 0, 128, 0, 512), AF.Copy).then_inc(s_s.h, 1)
        p.em("scalar", s12)
        s_s.bump(1)
        s12n = s_s.n

        def p13(e):
            last = None
            for h in range(16):
                r, c = h % 2, h // 2
                last = e.matmul(pb(3, 64 * r, 64 * r + 64, 64 * c, 64 * c + 64),
                                lslice(Hb, h), lslice(Xb, h), start=True,
                                stop=True, tile_position=(64 * r, 64 * r))
            return last
        m13 = TEphase("WT", p13, wait_s=s12n)

        def s14(e, m13=m13):
            e.wait_ge(s_mm.h, m13)
            e.activation(WTb[:, :], pb(3, 0, 128, 0, 512), AF.Copy).then_inc(s_s.h, 1)
        p.em("scalar", s14)
        s_s.bump(1)
        s14n = s_s.n

        def p15(e):
            last = None
            for h in range(16):
                r, c = h % 2, h // 2
                last = e.matmul(pb(0, 64 * r, 64 * r + 64, 8 * c, 8 * c + 8),
                                lslice(WTb, h),
                                Wb[64 * r:64 * r + 64, 64 * c:64 * c + 8],
                                start=True, stop=True,
                                tile_position=(64 * r, 64 * r))
            return last
        m15 = TEphase("B", p15, wait_s=s14n)

        def v16(e, m15=m15):
            e.wait_ge(s_mm.h, m15)
            e.memset(Bbd[:, :], 0.0)
            for r in range(2):
                e.tensor_copy(
                    Bbd[64 * r:64 * r + 64, 8 * r:8 * r + 8].rearrange(
                        "p f -> p 1 f").broadcast_to((64, 1, 8))
                    if False else
                    Bbd[64 * r:64 * r + 64, :].rearrange(
                        "p (pr f) -> p pr f", f=16)[:, :, 8 * r:8 * r + 8],
                    pb(0, 64 * r, 64 * r + 64, 0, 64).rearrange(
                        "p (pr f) -> p pr f", f=8))
            e.tensor_scalar_add(self.fence_t[:, :], self.fence_t[:, :],
                                0.0).then_inc(s_v.h, 1)
        p.em("vector", v16)
        s_v.bump(1)
        v16n = s_v.n

        def p17(e):
            last = None
            for pr in range(8):
                last = e.matmul(pb(1, 0, 16, 16 * pr, 16 * pr + 16),
                                Bbd[:, 16 * pr:16 * pr + 16],
                                Bbd[:, 16 * pr:16 * pr + 16],
                                start=True, stop=True)
            return last
        m17 = TEphase("N16", p17, wait_v=v16n)

        def v18(e, m17=m17):
            e.wait_ge(s_mm.h, m17)
            e.tensor_copy(N16b[0:16, :], pb(1, 0, 16, 0, 128)).then_inc(s_v.h, 1)
        p.em("vector", v18)
        s_v.bump(1)
        v18a = s_v.n

        def p17b(e):
            last = None
            for pr in range(8):
                last = e.matmul(pb(0, 0, 16, 16 * pr, 16 * pr + 16),
                                N16b[0:16, 16 * pr:16 * pr + 16],
                                N16b[0:16, 16 * pr:16 * pr + 16],
                                start=True, stop=True)
            return last
        m17b = TEphase("N16sq", p17b, wait_v=v18a)

        def v18b(e, m17b=m17b):
            e.wait_ge(s_mm.h, m17b)
            e.tensor_mul(T8b[0:16, :], pb(0, 0, 16, 0, 128), self.i16b[0:16, :])
            e.tensor_reduce(d16[0:16, 0:8],
                            T8b[0:16, :].rearrange("p (c f) -> p c f", f=16),
                            AX.X, ALU.add)
            self.vfence(e)
            e.tensor_scalar_add(self.fence_t[:, :], self.fence_t[:, :],
                                0.0).then_inc(s_v.h, 1)
        p.em("vector", v18b)
        s_v.bump(1)
        v18n = s_v.n

        def p19(e):
            return e.matmul(pb(2, 0, 2, 0, 8), self.sel16[0:16, 0:2],
                            d16[0:16, 0:8], start=True, stop=True)
        m19 = TEphase("tr16", p19, wait_v=v18n)

        def s20(e, m19=m19):
            e.wait_ge(s_mm.h, m19)
            e.activation(al8[0:2, 0:8], pb(2, 0, 2, 0, 8), AF.Sqrt,
                         scale=1.0101).then_inc(s_s.h, 1)
        p.em("scalar", s20)
        s_s.bump(1)
        s20n = s_s.n

        def v21(e, s20n=s20n):
            e.wait_ge(s_s.h, s20n)
            self.vfence(e)
            e.reciprocal(al8[0:2, 0:8], al8[0:2, 0:8])
            self.vfence(e)
            e.tensor_scalar_add(self.fence_t[:, :], self.fence_t[:, :],
                                0.0).then_inc(s_v.h, 1)
        p.em("vector", v21)
        s_v.bump(1)
        v21n = s_v.n

        def p22(e):
            e.matmul(pb(2, 0, 16, 0, 128), self.sel2x16[0:2, 0:16],
                     al8[0:2, 0:8].unsqueeze(-1).broadcast_to((2, 8, 16)),
                     start=True, stop=True)
            return e.matmul(pb(3, 0, 16, 0, 512), self.sel2x16[0:2, 0:16],
                            al8[0:2, 0:8].unsqueeze(-1).broadcast_to((2, 8, 64)),
                            start=True, stop=True)
        m22 = TEphase("ab16", p22, wait_v=v21n)

        def v23(e, m22=m22, h0=G8[0]):
            e.wait_ge(s_mm.h, m22)
            e.tensor_mul(N16b[0:16, :], N16b[0:16, :], pb(2, 0, 16, 0, 128))
            e.tensor_scalar_mul(X8b[0:16, :], N16b[0:16, :], -h0 * h0)
            e.scalar_tensor_tensor(X8b[0:16, :], self.i16b[0:16, :], 2.0 * h0,
                                   X8b[0:16, :], ALU.mult, ALU.add).then_inc(s_v.h, 1)
        p.em("vector", v23)
        s_v.bump(1)
        lastv8 = s_v.n
        lasts8 = None

        for g in G8[1:]:
            def pa8(e):
                last = None
                for pr in range(8):
                    last = e.matmul(pb(0, 0, 16, 16 * pr, 16 * pr + 16),
                                    N16b[0:16, 16 * pr:16 * pr + 16],
                                    X8b[0:16, 16 * pr:16 * pr + 16],
                                    start=True, stop=True)
                return last
            ma8 = TEphase("ns8T", pa8, wait_v=lastv8, wait_s=lasts8)

            def vb8(e, ma8=ma8, g=g):
                e.wait_ge(s_mm.h, ma8)
                e.tensor_scalar_mul(T8b[0:16, :], pb(0, 0, 16, 0, 128), -g * g)
                e.scalar_tensor_tensor(T8b[0:16, :], self.i16b[0:16, :],
                                       2.0 * g, T8b[0:16, :], ALU.mult,
                                       ALU.add).then_inc(s_v.h, 1)
            p.em("vector", vb8)
            s_v.bump(1)
            vb8n = s_v.n

            def pc8(e):
                last = None
                for pr in range(8):
                    last = e.matmul(pb(1, 0, 16, 16 * pr, 16 * pr + 16),
                                    X8b[0:16, 16 * pr:16 * pr + 16],
                                    T8b[0:16, 16 * pr:16 * pr + 16],
                                    start=True, stop=True)
                return last
            mc8 = TEphase("ns8X", pc8, wait_v=vb8n)

            def sd8(e, mc8=mc8):
                e.wait_ge(s_mm.h, mc8)
                e.activation(X8b[0:16, :], pb(1, 0, 16, 0, 128),
                             AF.Copy).then_inc(s_s.h, 1)
            p.em("scalar", sd8)
            s_s.bump(1)
            lasts8 = s_s.n
            lastv8 = vb8n

        def p25(e):
            last = None
            for pr in range(8):
                last = e.matmul(pb(0, 0, 16, 64 * pr, 64 * pr + 64),
                                Bbd[:, 16 * pr:16 * pr + 16],
                                Wb[:, 64 * pr:64 * pr + 64],
                                start=True, stop=True)
            return last
        m25 = TEphase("C1", p25, wait_s=lasts8)

        def s26(e, m25=m25):
            e.wait_ge(s_mm.h, m25)
            e.activation(C1b[0:16, :], pb(0, 0, 16, 0, 512),
                         AF.Copy).then_inc(s_s.h, 1)
        p.em("scalar", s26)
        s_s.bump(1)
        s26n = s_s.n

        def v26(e, s26n=s26n):
            e.wait_ge(s_s.h, s26n)
            e.tensor_mul(C1b[0:16, :], C1b[0:16, :],
                         pb(3, 0, 16, 0, 512)).then_inc(s_v.h, 1)
        p.em("vector", v26)
        s_v.bump(1)
        v26n = s_v.n

        def p27(e):
            last = None
            for pr in range(8):
                last = e.matmul(pb(1, 0, 16, 64 * pr, 64 * pr + 64),
                                X8b[0:16, 16 * pr:16 * pr + 16],
                                C1b[0:16, 64 * pr:64 * pr + 64],
                                start=True, stop=True)
            return last
        m27 = TEphase("D1", p27, wait_v=v26n)

        def s28(e, m27=m27):
            e.wait_ge(s_mm.h, m27)
            e.activation(D1b[0:16, :], pb(1, 0, 16, 0, 512),
                         AF.Copy).then_inc(s_s.h, 1)
        p.em("scalar", s28)
        s_s.bump(1)
        s28n = s_s.n

        def p29(e):
            last = None
            for pr in range(8):
                last = e.transpose(pbh[1][0:16, 128 * pr:128 * pr + 128],
                                   Bbd[:, 16 * pr:16 * pr + 16],
                                   self.eye128[:, :])
            return last
        m29 = TEphase("BT", p29, wait_s=s28n)

        def s30(e, m29=m29):
            e.wait_ge(s_mm.h, m29)
            e.activation(BTb[0:16, :], pbh[1][0:16, 0:1024],
                         AF.Copy).then_inc(s_s.h, 1)
        p.em("scalar", s30)
        s_s.bump(1)
        s30n = s_s.n

        def p31(e):
            last = None
            for pr in range(8):
                last = e.matmul(pb(0, 0, 128, 64 * pr, 64 * pr + 64),
                                BTb[0:16, 128 * pr:128 * pr + 128],
                                D1b[0:16, 64 * pr:64 * pr + 64],
                                start=True, stop=True)
            return last
        m31 = TEphase("Wr", p31, wait_s=s30n)

        def v32(e, m31=m31):
            e.wait_ge(s_mm.h, m31)
            e.scalar_tensor_tensor(Sb[:, :], Sb[:, :], 0.9,
                                   pb(0, 0, 128, 0, 512), ALU.mult, ALU.add)
            e.tensor_copy(Sbf[:, :], Sb[:, :]).then_inc(s_v.h, 1)
        p.em("vector", v32)
        s_v.bump(1)
        v32n = s_v.n

        def p33(e):
            last = None
            for h in range(16):
                r, c = h % 2, h // 2
                last = e.matmul(pbh[1][64 * r:64 * r + 64, 128 * c:128 * c + 128],
                                lslice(Sbf, h), QcT(h), start=True, stop=True,
                                tile_position=(64 * r, 64 * r))
            return last
        m33 = TEphase("oT", p33, wait_v=v32n)

        def s34(e, m33=m33):
            e.wait_ge(s_mm.h, m33)
            e.activation(oTb[:, :], pbh[1][:, 0:1024], AF.Copy).then_inc(s_s.h, 1)
        p.em("scalar", s34)
        s_s.bump(1)
        s34n = s_s.n

        def p35(e):
            last = None
            for d2 in range(2):
                for c in range(8):
                    last = e.matmul(pb(4 + d2),
                                    oTb[:, 128 * c:128 * c + 128],
                                    Wo_bf[:, 1024 * c + 512 * d2:
                                          1024 * c + 512 * d2 + 512],
                                    start=(c == 0), stop=(c == 7))
            return last
        m35 = TEphase("Wo", p35, wait_s=s34n)
        scan_mm[t] = s_mm.n

        par = t % 2
        xpv = x1_v[("xp", t)]
        stsnap = s_st.n

        def v36(e, m35=m35, par=par, xpv=xpv, t=t, stsnap=stsnap):
            e.wait_ge(s_mm.h, m35)
            e.wait_ge(s_xp.h, xpv)
            if t >= 1:
                e.wait_ge(s_st.h, stsnap)
            for d2 in range(2):
                ins = e.tensor_add(
                    x1o[:, 512 * d2:512 * d2 + 512],
                    xplt[:, 1024 * (t % 4) + 512 * d2:
                         1024 * (t % 4) + 512 * d2 + 512],
                    pb(4 + d2))
            ins.then_inc(s_v.h, 1)
        p.em("vector", v36)
        s_v.bump(1)
        x1_v[t] = s_v.n

        def sy36(e, t=t, par=par, xv=x1_v[t]):
            e.wait_ge(s_v.h, xv)
            e.dma_start(out=self.d_x1[128 * t:128 * t + 128, :],
                        in_=x1o[:, 0:1024]).then_inc(s_st.h, 16)
        p.em("sync", sy36)
        s_st.bump(16)

    def emit_fp4(self, x_ap, out_bf, T, sv, ss, nblk):
        p = self.prog
        N = nblk * 16
        qam, qsc, qinv = T["qam"], T["qsc"], T["qinv"]
        qv = T["qv"][:, 0:N]
        qu2 = T["qu2"][:, 0:N]
        qf2 = T["qf2"][:, 0:N]
        qc1 = T["qc1"][:, 0:N]
        qr1 = T["qr1"][:, 0:N]
        qi = T["qi"][:, 0:N]
        qsg = T["qsg"][:, 0:N]
        am = qam[:, 0:nblk]
        sc = qsc[:, 0:nblk]
        inv = qinv[:, 0:nblk]
        xb = x_ap.rearrange("p (b k) -> p b k", k=16)
        invb = inv.unsqueeze(-1).broadcast_to((128, nblk, 16))
        scb = sc.unsqueeze(-1).broadcast_to((128, nblk, 16))

        def d1(e):
            e.tensor_reduce(am, xb, AX.X, ALU.max, apply_absolute_value=True)
            self.vfence(e)
            e.tensor_scalar(sc, am, 1.0 / 6.0, 1e-12, ALU.mult, ALU.max)
            self.vfence(e)
            e.reciprocal(inv, sc)
            self.vfence(e)
            e.tensor_tensor(qv.rearrange("p (b k) -> p b k", k=16), xb, invb,
                            ALU.mult).then_inc(sv.h, 1)
        p.em("vector", d1)
        sv.bump(1)
        v1 = sv.n

        def a1(e):
            e.wait_ge(sv.h, v1)
            e.activation(qu2, qv, AF.Abs, scale=2.0)
            e.sign(qsg, qv).then_inc(ss.h, 1)
        p.em("scalar", a1)
        ss.bump(1)
        s1 = ss.n

        def d2(e):
            e.wait_ge(ss.h, s1)
            e.tensor_scalar_add(qi, qu2, 0.5)
            e.tensor_copy(qf2, qi)
            e.scalar_tensor_tensor(qv, qf2, -0.5, qu2, ALU.add, ALU.is_gt)
            e.tensor_sub(qf2, qf2, qv)
            e.tensor_single_scalar(qc1, qu2, 5.0, ALU.is_gt).then_inc(sv.h, 1)
        p.em("vector", d2)
        sv.bump(1)
        v2 = sv.n

        def a2(e):
            e.wait_ge(sv.h, v2)
            e.activation(qr1, qf2, AF.Relu, scale=-0.5,
                         bias=self.half2[:, 0:1]).then_inc(ss.h, 1)
        p.em("scalar", a2)
        ss.bump(1)
        s2 = ss.n

        def d3(e):
            e.wait_ge(ss.h, s2)
            e.tensor_sub(qc1, qc1, qr1)
            e.scalar_tensor_tensor(qc1, qu2, 7.0, qc1, ALU.is_gt, ALU.add)
            e.tensor_scalar(qr1, qu2, 10.0, 2.0, ALU.is_gt, ALU.mult)
            e.tensor_add(qc1, qc1, qr1)
            e.tensor_tensor(qr1.rearrange("p (b k) -> p b k", k=16),
                            qsg.rearrange("p (b k) -> p b k", k=16), scb,
                            ALU.mult)
            e.scalar_tensor_tensor(out_bf, qc1, 2.0, qr1, ALU.add,
                                   ALU.mult).then_inc(sv.h, 1)
        p.em("vector", d3)
        sv.bump(1)
        return sv.n

    def emit_fp4_dve(self, x_ap, out_bf, T, sv, nblk):
        p = self.prog
        N = nblk * 16
        qam, qsc, qinv = T["qam"], T["qsc"], T["qinv"]
        qv = T["qv"][:, 0:N]
        qu2 = T["qu2"][:, 0:N]
        qf2 = T["qf2"][:, 0:N]
        qc1 = T["qc1"][:, 0:N]
        qr1 = T["qr1"][:, 0:N]
        qi = T["qi"][:, 0:N]
        am = qam[:, 0:nblk]
        sc = qsc[:, 0:nblk]
        inv = qinv[:, 0:nblk]
        xb = x_ap.rearrange("p (b k) -> p b k", k=16)
        invb = inv.unsqueeze(-1).broadcast_to((128, nblk, 16))
        scb = sc.unsqueeze(-1).broadcast_to((128, nblk, 16))

        def dq(e):
            e.tensor_reduce(am, xb, AX.X, ALU.max, apply_absolute_value=True)
            self.vfence(e)
            e.tensor_scalar(sc, am, 1.0 / 6.0, 1e-12, ALU.mult, ALU.max)
            self.vfence(e)
            e.reciprocal(inv, sc)
            self.vfence(e)
            e.tensor_tensor(qv.rearrange("p (b k) -> p b k", k=16), xb, invb,
                            ALU.mult)
            e.tensor_scalar_mul(qf2, qv, -1.0)
            e.tensor_tensor(qu2, qv, qf2, ALU.max)
            e.tensor_scalar(qr1, qu2, 2.0, 0.5, ALU.mult, ALU.add)
            e.tensor_copy(qi, qr1)
            e.tensor_copy(qf2, qi)
            e.tensor_tensor(qc1, qf2, qr1, ALU.is_gt)
            e.tensor_sub(qf2, qf2, qc1)
            e.tensor_scalar(qc1, qf2, 0.5, 2.0, ALU.mult, ALU.min)
            e.scalar_tensor_tensor(qc1, qu2, 2.5, qc1, ALU.is_gt, ALU.add)
            e.scalar_tensor_tensor(qc1, qu2, 3.5, qc1, ALU.is_gt, ALU.add)
            e.tensor_scalar(qr1, qu2, 5.0, 2.0, ALU.is_gt, ALU.mult)
            e.tensor_add(qc1, qc1, qr1)
            e.tensor_single_scalar(qr1, qv, 0.0, ALU.is_gt)
            e.tensor_scalar(qr1, qr1, 2.0, -1.0, ALU.mult, ALU.add)
            e.tensor_tensor(qf2.rearrange("p (b k) -> p b k", k=16),
                            qr1.rearrange("p (b k) -> p b k", k=16), scb,
                            ALU.mult)
            e.tensor_tensor(out_bf, qc1, qf2, ALU.mult).then_inc(sv.h, 1)
        p.em("vector", dq)
        sv.bump(1)
        return sv.n

    def emit_stage_B(self):
        p = self.prog
        at = self.at
        o = 0
        def nxt(name, shape, dtype):
            nonlocal o
            t = at(name, shape, dtype, o)
            sz = shape[-1] * mybir.dt.size(dtype)
            o += (sz + 31) // 32 * 32
            return t
        wgqT = nxt("b_wgqT", [128, 8 * FF], BF16)
        wuqT = nxt("b_wuqT", [128, 8 * FF], BF16)
        wdqT = nxt("b_wdqT", [128, 22 * 1024], BF16)
        T = {}
        T["qv"] = nxt("b_qv", [128, 512], F32)
        T["qu2"] = nxt("b_qu2", [128, 512], F32)
        T["qf2"] = nxt("b_qf2", [128, 512], F32)
        T["qc1"] = nxt("b_qc1", [128, 512], F32)
        T["qr1"] = nxt("b_qr1", [128, 512], F32)
        T["qi"] = nxt("b_qi", [128, 512], I32)
        T["qsg"] = nxt("b_qsg", [128, 512], BF16)
        T["qam"] = nxt("b_qam", [128, 32], F32)
        T["qsc"] = nxt("b_qsc", [128, 32], F32)
        T["qinv"] = nxt("b_qinv", [128, 32], F32)
        hq = nxt("b_hq", [128, FF], BF16)
        hqT = nxt("b_hqT", [128, 22 * 128], BF16)
        hbuf = T["qf2"]
        qsil = nxt("b_qsil", [128, 512], F32)
        x1t = nxt("b_x1t", [128, 2 * 1024], F32)
        ln1 = nxt("b_ln1", [128, 1024], F32)
        sq = nxt("b_sq", [128, 1024], F32)
        xq = nxt("b_xq", [128, 1024], BF16)
        xqT = nxt("b_xqT", [128, 8 * 128], BF16)
        x2o = nxt("b_x2o", [128, 1024], F32)
        stat = nxt("b_stat", [128, 16], F32)
        wld = ln1
        wqb = xq

        s_ld = self.sem("B_ld")
        s_tp = self.sem("B_tp")
        s_tq = self.sem("B_tq")
        s_v = self.sem("B_v")
        s_s = self.sem("B_s")
        s_mm = self.sem("B_mm")
        s_st = self.sem("B_st")
        s_pld = self.sem("B_pld")
        s_pst = self.sem("B_pst")
        s_vd = self.sem("B_vd")
        pb = self.pbank
        deps = getattr(self, "dep_B", [])

        subs = []
        for d in range(8):
            for g in range(6):
                subs.append((self.d_w_down, self.d_wdq, d, 512 * g,
                             512 if g < 5 else 256, "d"))
        n_down = len(subs)
        for m in range(22):
            for wn in ("g", "u"):
                src = self.d_w_gate if wn == "g" else self.d_w_up
                dst = self.d_wgq if wn == "g" else self.d_wuq
                for g in range(2):
                    subs.append((src, dst, m, 512 * g, 512, wn))

        vqk = {}
        pstk = {}
        U = {}
        wdq_tq = 0
        for k, (src, dst, r, off, gw, wn) in enumerate(subs):
            lb = 512 * (k % 2)

            def gl(e, src=src, r=r, off=off, gw=gw, k=k, lb=lb,
                   vprev=vqk.get(k - 2)):
                if k == 0:
                    for (ds, dv) in deps:
                        e.wait_ge(ds.h, dv)
                if vprev is not None:
                    e.wait_ge(s_v.h, vprev)
                e.dma_start(out=wld[:, lb:lb + gw],
                            in_=src[128 * r:128 * r + 128, off:off + gw]
                            ).then_inc(s_pld.h, 16)
            p.em("gpsimd", gl)
            s_pld.bump(16)
            pl = s_pld.n

            def vpre(e, pl=pl, sprev=pstk.get(k - 2)):
                e.wait_ge(s_pld.h, pl)
                if sprev is not None:
                    e.wait_ge(s_pst.h, sprev)
            p.em("vector", vpre)
            vq = self.emit_fp4_dve(wld[:, lb:lb + gw], wqb[:, lb:lb + gw],
                                   T, s_v, gw // 16)
            vqk[k] = vq

            def gs(e, dst=dst, r=r, off=off, gw=gw, lb=lb, vq=vq):
                e.wait_ge(s_v.h, vq)
                e.dma_start(out=dst[128 * r:128 * r + 128, off:off + gw],
                            in_=wqb[:, lb:lb + gw]).then_inc(s_pst.h, 16)
            p.em("sync", gs)
            s_pst.bump(16)
            pstk[k] = s_pst.n

            blks = tuple(range(off // 128, (off + gw) // 128))
            if wn == "d":
                q = "sync" if r % 2 == 0 else "scalar"
                sem = s_tp if q == "sync" else s_tq

                def tpd(e, r=r, blks=blks, ps=pstk[k], sem=sem):
                    e.wait_ge(s_pst.h, ps)
                    for m in blks:
                        e.dma_start_transpose(
                            wdqT[:, 1024 * m + 128 * r:1024 * m + 128 * r + 128],
                            self.d_wdq[128 * r:128 * r + 128,
                                       128 * m:128 * m + 128]
                            ).then_inc(sem.h, 16)
                p.em(q, tpd)
                sem.bump(16 * len(blks))
            else:
                q = "sync" if wn == "g" else "scalar"
                sem = s_tp if wn == "g" else s_tq
                wT = wgqT if wn == "g" else wuqT
                dq = self.d_wgq if wn == "g" else self.d_wuq

                def tpg(e, m=r, blks=blks, ps=pstk[k], wT=wT, dq=dq, sem=sem):
                    e.wait_ge(s_pst.h, ps)
                    for c in blks:
                        e.dma_start_transpose(
                            wT[:, FF * c + 128 * m:FF * c + 128 * m + 128],
                            dq[128 * m:128 * m + 128, 128 * c:128 * c + 128]
                            ).then_inc(sem.h, 16)
                p.em(q, tpg)
                sem.bump(16 * len(blks))
                if wn == "u" and off == 512:
                    for gm in range(6):
                        if (gm < 5 and r == 4 * gm + 3) or (gm == 5 and r == 21):
                            U[gm] = s_tq.n
            if k == n_down - 1:
                wdq_tq = s_tq.n

        GW = [512, 512, 512, 512, 512, 256]
        GOF = [0, 512, 1024, 1536, 2048, 2560]
        BT = [(0, 4, 6), (4, 4, 7), (8, 4, 6), (12, 4, 7), (16, 4, 6), (20, 2, 7)]
        tpq = {}
        gu_mm = {}
        gu_v = {}
        vh_v = {}
        tr_n = {}
        evac = {}
        md = {}
        vdns = {}
        trd = {}
        pend = []
        for i in range(NT):
            while len(pend) > 1:
                pend.pop(0)()
            bx = x1t[:, 1024 * (i % 2):1024 * (i % 2) + 1024]
            bxq = xq[:, :]
            bhq = hq[:, :]
            bx2 = x2o[:, :]

            def syx(e, i=i, bx=bx):
                if i == 0:
                    for (ds, dv) in deps:
                        e.wait_ge(ds.h, dv)
                if i >= 2:
                    e.wait_ge(s_vd.h, i - 1)
                e.dma_start(out=bx, in_=self.d_x1[128 * i:128 * i + 128, :]
                            ).then_inc(s_ld.h, 16)
            p.em("sync", syx)
            s_ld.bump(16)
            lds = s_ld.n

            def vx(e, lds=lds):
                e.wait_ge(s_ld.h, lds)
            p.em("vector", vx)
            self.emit_ln(bx, ln1[:, :], stat, sq[:, :], s_v, s_s)

            def vx2(e, i=i, tp=tpq.get(i - 1), pfin=pstk[len(subs) - 1]):
                if i == 0:
                    e.wait_ge(s_pst.h, pfin)
                if tp is not None:
                    e.wait_ge(s_tp.h, tp)
            p.em("vector", vx2)
            vq = 0
            for g in range(2):
                vq = self.emit_fp4(ln1[:, 512 * g:512 * g + 512],
                                   bxq[:, 512 * g:512 * g + 512], T, s_v, s_s, 32)

            def syq(e, i=i, vq=vq, bxq=bxq, mmp=gu_mm.get((i - 1, 5))):
                e.wait_ge(s_v.h, vq)
                if mmp is not None:
                    e.wait_ge(s_mm.h, mmp)
                for k in range(8):
                    e.dma_start_transpose(xqT[:, 128 * k:128 * k + 128],
                                          bxq[:, 128 * k:128 * k + 128]
                                          ).then_inc(s_tp.h, 16)
            p.em("sync", syq)
            s_tp.bump(16 * 8)
            tpq[i] = s_tp.n

            for g in range(6):
                gw, gof = GW[g], GOF[g]
                bga = pb(0 + 2 * (g % 2), 0, 128, 0, gw)
                bup = pb(1 + 2 * (g % 2), 0, 128, 0, gw)
                key = (i, g - 2) if g >= 2 else (i - 1, g + 4)

                def teg(e, i=i, g=g, gof=gof, gw=gw, bga=bga, bup=bup,
                        tp=tpq[i], pv=gu_v.get(key)):
                    if g == 0:
                        e.wait_ge(s_tp.h, tp)
                    if i == 0:
                        e.wait_ge(s_tq.h, U[g])
                    if pv is not None:
                        e.wait_ge(s_v.h, pv)
                    last = None
                    for c in range(8):
                        e.matmul(bga, xqT[:, 128 * c:128 * c + 128],
                                 wgqT[:, FF * c + gof:FF * c + gof + gw],
                                 start=(c == 0), stop=(c == 7))
                    for c in range(8):
                        last = e.matmul(bup, xqT[:, 128 * c:128 * c + 128],
                                        wuqT[:, FF * c + gof:FF * c + gof + gw],
                                        start=(c == 0), stop=(c == 7))
                    last.then_inc(s_mm.h, 1)
                p.em("tensor", teg)
                s_mm.bump(1)
                gu_mm[(i, g)] = s_mm.n
                mg = s_mm.n

                vkey = (i, g - 1) if g >= 1 else (i - 1, 5)

                def sgl(e, mg=mg, gw=gw, bga=bga, vp=vh_v.get(vkey)):
                    e.wait_ge(s_mm.h, mg)
                    if vp is not None:
                        e.wait_ge(s_v.h, vp)
                    e.activation(qsil[:, 0:gw], bga, AF.Silu).then_inc(s_s.h, 1)
                p.em("scalar", sgl)
                s_s.bump(1)
                sg = s_s.n

                def vh(e, sg=sg, gw=gw, g=g, bup=bup, trp=trd.get(i - 1)):
                    e.wait_ge(s_s.h, sg)
                    if g == 0 and trp is not None:
                        e.wait_ge(s_mm.h, trp)
                    e.tensor_mul(hbuf[:, 0:gw], qsil[:, 0:gw],
                                 bup).then_inc(s_v.h, 1)
                p.em("vector", vh)
                s_v.bump(1)
                vh_v[(i, g)] = s_v.n
                hq_v = self.emit_fp4(hbuf[:, 0:gw], bhq[:, gof:gof + gw],
                                     T, s_v, s_s, gw // 16)
                gu_v[(i, g)] = hq_v

            for b, (m0, nm, bank) in enumerate(BT):
                ev_guard = evac.get((i, b - 2)) if b >= 2 else \
                    evac.get((i - 1, 4 + b))

                def tr(e, b=b, m0=m0, nm=nm, bank=bank, bhq=bhq,
                       hv=gu_v[(i, 5)], eg=ev_guard):
                    if b == 0:
                        e.wait_ge(s_v.h, hv)
                    if eg is not None:
                        e.wait_ge(s_s.h, eg)
                    last = None
                    for t in range(nm):
                        last = e.matmul(
                            pb(bank, 0, 128, 128 * t, 128 * t + 128),
                            bhq[:, 128 * (m0 + t):128 * (m0 + t) + 128],
                            self.eyebf[:, :], start=True, stop=True)
                    last.then_inc(s_mm.h, 1)
                p.em("tensor", tr)
                s_mm.bump(1)
                tr_n[(i, b)] = s_mm.n

                def ev(e, b=b, m0=m0, nm=nm, bank=bank, tn=tr_n[(i, b)]):
                    e.wait_ge(s_mm.h, tn)
                    e.activation(hqT[:, 128 * m0:128 * (m0 + nm)],
                                 pb(bank, 0, 128, 0, 128 * nm),
                                 AF.Copy).then_inc(s_s.h, 1)
                p.em("scalar", ev)
                s_s.bump(1)
                evac[(i, b)] = s_s.n
            trd[i] = s_mm.n

            def ted(e, i=i, evl=evac[(i, 5)]):
                e.wait_ge(s_s.h, evl)
                if i == 0:
                    e.wait_ge(s_tq.h, wdq_tq)
                if i >= 1:
                    e.wait_ge(s_vd.h, i)
                last = None
                for h in range(2):
                    for m in range(22):
                        last = e.matmul(pb(4 + h),
                                        hqT[:, 128 * m:128 * m + 128],
                                        wdqT[:, 1024 * m + 512 * h:
                                             1024 * m + 512 * h + 512],
                                        start=(m == 0), stop=(m == 21))
                last.then_inc(s_mm.h, 1)
            p.em("tensor", ted)
            s_mm.bump(1)
            md[i] = s_mm.n

            def phase2(i=i, md=md[i], bx=bx, bx2=bx2):
                def vd(e, i=i, md=md, bx=bx, bx2=bx2):
                    e.wait_ge(s_mm.h, md)
                    if i >= 1:
                        e.wait_ge(s_st.h, 16 * i)
                    for h in range(2):
                        ins = e.tensor_add(bx2[:, 512 * h:512 * h + 512],
                                           bx[:, 512 * h:512 * h + 512],
                                           pb(4 + h))
                    ins.then_inc(s_vd.h, 1)
                p.em("vector", vd)
                s_vd.bump(1)

                def sy2(e, i=i, bx2=bx2):
                    e.wait_ge(s_vd.h, i + 1)
                    e.dma_start(out=self.d_x2[128 * i:128 * i + 128, :],
                                in_=bx2).then_inc(s_st.h, 16)
                p.em("sync", sy2)
                s_st.bump(16)
            pend.append(phase2)

        for f in pend:
            f()

        def syend(e):
            e.wait_ge(s_st.h, s_st.n)
        p.em("sync", syend)
        self.B_done = (s_st, s_st.n)

    def emit_stage_D(self):
        p = self.prog
        at = self.at
        o = 0
        def nxt(name, shape, dtype):
            nonlocal o
            t = at(name, shape, dtype, o)
            sz = shape[-1] * mybir.dt.size(dtype)
            o += (sz + 31) // 32 * 32
            return t
        qrT = nxt("d_qrT", [128, 8 * 1024], BF16)
        rload = nxt("d_rload", [128, 2 * 1024], F32)
        rhsT = nxt("d_rhsT", [128, 1024], F32)
        rq = nxt("d_rq", [128, 1024], BF16)
        rtmp = nxt("d_rtmp", [128, 1024], F32)
        rsgn = nxt("d_rsgn", [128, 1024], F32)
        ri = nxt("d_ri", [128, 1024], I32)
        rf2 = nxt("d_rf2", [128, 1024], F32)
        ry = nxt("d_ry", [128, 1024], F32)
        rqm = nxt("d_rqm", [128, 1024], F32)
        rstat = nxt("d_rstat", [128, 16], F32)
        x2t = nxt("d_x2t", [128, 2 * 1024], F32)
        att = nxt("d_att", [128, 2 * 1024], BF16)
        x3t = nxt("d_x3t", [128, 2 * 1024], F32)
        ln3 = nxt("d_ln3", [128, 1024], F32)
        stat = nxt("d_stat", [128, 16], F32)
        lsc = nxt("d_lsc", [128, 4], F32)
        qsgn = nxt("d_qsgn", [128, 1024], F32)
        qtmp = nxt("d_qtmp", [128, 1024], F32)
        qi = nxt("d_qi", [128, 1024], I32)
        qf2 = nxt("d_qf2", [128, 1024], F32)
        qy = nxt("d_qy", [128, 1024], F32)
        qm = nxt("d_qm", [128, 1024], F32)
        qb = nxt("d_qb", [128, 1024], BF16)
        qT = nxt("d_qT", [128, 2 * 1024], BF16)
        outt = nxt("d_outt", [128, 2 * 1024], F32)

        s_ld = self.sem("D_ld")
        s_tp = self.sem("D_tp")
        s_v = self.sem("D_v")
        s_s = self.sem("D_s")
        s_mm = self.sem("D_mm")
        s_st = self.sem("D_st")
        s_pe = self.sem("D_pe")
        pb = self.pbank
        init_v = self.init_v
        s_iv = self.sem("init_v")

        for c in range(8):
            def sy(e, c=c):
                if c == 0:
                    for (ds, dv) in getattr(self, "dep_D", []):
                        e.wait_ge(ds.h, dv)
                if c >= 2:
                    e.wait_ge(s_pe.h, 2 * (c - 1))
                e.dma_start(
                    out=rload[:, (c % 2) * 1024:(c % 2) * 1024 + 1024]
                    .rearrange("p (k f) -> p k f", f=128),
                    in_=self.d_rhs[:, 128 * c:128 * c + 128]
                    .rearrange("(k p) f -> p k f", p=128)).then_inc(s_ld.h, 16)
            p.em("sync", sy)
            s_ld.bump(16)
            ld_snap = s_ld.n

            def te1(e, c=c, ld_snap=ld_snap):
                e.wait_ge(s_ld.h, ld_snap)
                e.wait_ge(s_iv.h, init_v)
                if c > 0:
                    e.wait_ge(s_pe.h, 2 * c)
                buf = rload[:, (c % 2) * 1024:(c % 2) * 1024 + 1024]
                last = None
                for k in range(4):
                    last = e.transpose(pb(k, 0, 128, 0, 128),
                                       buf[:, 128 * k:128 * k + 128],
                                       self.eye128[:, :])
                last.then_inc(s_mm.h, 1)
            p.em("tensor", te1)
            s_mm.bump(1)
            mm1 = s_mm.n

            def te2(e, c=c):
                e.wait_ge(s_pe.h, 2 * c + 1)
                buf = rload[:, (c % 2) * 1024:(c % 2) * 1024 + 1024]
                last = None
                for k in range(4):
                    last = e.transpose(pb(4 + k, 0, 128, 0, 128),
                                       buf[:, 128 * (4 + k):128 * (4 + k) + 128],
                                       self.eye128[:, :])
                last.then_inc(s_mm.h, 1)
            p.em("tensor", te2)
            s_mm.bump(1)
            mm2 = s_mm.n

            def ve(e, mm1=mm1, mm2=mm2):
                e.wait_ge(s_mm.h, mm1)
                for k in range(4):
                    ins = e.tensor_copy(rhsT[:, 128 * k:128 * k + 128],
                                        pb(k, 0, 128, 0, 128))
                ins.then_inc(s_pe.h, 1)
                e.wait_ge(s_mm.h, mm2)
                for k in range(4):
                    ins = e.tensor_copy(rhsT[:, 128 * (4 + k):128 * (4 + k) + 128],
                                        pb(4 + k, 0, 128, 0, 128))
                ins.then_inc(s_pe.h, 1)
                e.tensor_reduce(rstat[:, 0:1], rhsT[:, :], AX.X, ALU.max,
                                apply_absolute_value=True)
                self.vfence(e)
                e.tensor_scalar_max(rstat[:, 1:2], rstat[:, 0:1], 1e-12)
                self.vfence(e)
                e.tensor_scalar_mul(rstat[:, 2:3], rstat[:, 1:2], 1.0 / 127.0)
                self.vfence(e)
                e.reciprocal(rstat[:, 3:4], rstat[:, 2:3])
                self.vfence(e)
                e.tensor_scalar_add(self.fence_t[:, :], self.fence_t[:, :],
                                    0.0).then_inc(s_v.h, 1)
            p.em("vector", ve)
            s_pe.bump(2)
            s_v.bump(1)

            self.emit_round_i8(rhsT[:, :], rstat[:, 3:4], rsgn[:, :],
                               rf2[:, :], ri[:, :], ry[:, :], rqm[:, :],
                               s_v, s_s, wait_v=s_v.n)
            tp_prev = s_tp.n

            def vfold(e, tp_prev=tp_prev):
                e.wait_ge(s_tp.h, tp_prev)
                e.tensor_scalar(rsgn[:, :], rsgn[:, :], rstat[:, 2:3], None,
                                ALU.mult)
                e.tensor_mul(rq[:, :], rqm[:, :], rsgn[:, :]).then_inc(s_v.h, 1)
            p.em("vector", vfold)
            s_v.bump(1)
            vf = s_v.n

            def sy2(e, c=c, vf=vf):
                e.wait_ge(s_v.h, vf)
                for k in range(8):
                    e.dma_start_transpose(
                        qrT[:, 1024 * k + 128 * c:1024 * k + 128 * c + 128],
                        rq[:, 128 * k:128 * k + 128]).then_inc(s_tp.h, 16)
            p.em("scalar", sy2)
            s_tp.bump(16 * 8)

        rhs_ready = s_tp.n
        s_v5 = self.sem("D_v5")

        pend = []
        for i in range(NT):
            while len(pend) > 1:
                pend.pop(0)()
            par = i % 2
            x2a = x2t[:, par * 1024:par * 1024 + 1024]
            atta = att[:, par * 1024:par * 1024 + 1024]
            x3a = x3t[:, par * 1024:par * 1024 + 1024]
            lsa = lsc[:, par:par + 1]
            qTa = qT[:, par * 1024:par * 1024 + 1024]
            outta = outt[:, par * 1024:par * 1024 + 1024]

            def sy(e, i=i, x2a=x2a, atta=atta):
                if i == 0:
                    for (ds, dv) in getattr(self, "dep_D", []):
                        e.wait_ge(ds.h, dv)
                if i >= 2:
                    e.wait_ge(s_st.h, 16 * (i - 1))
                e.dma_start(out=x2a, in_=self.d_x2[128 * i:128 * i + 128, :]
                            ).then_inc(s_ld.h, 16)
                e.dma_start(out=atta, in_=self.d_attn[128 * i:128 * i + 128, :]
                            ).then_inc(s_ld.h, 16)
            p.em("sync", sy)
            s_ld.bump(32)
            ld_snap = s_ld.n

            def ve(e, x2a=x2a, atta=atta, x3a=x3a, ld_snap=ld_snap):
                e.wait_ge(s_ld.h, ld_snap)
                e.tensor_add(x3a, x2a, atta)
            p.em("vector", ve)

            self.emit_ln(x3a, ln3[:, :], stat, qtmp[:, :], s_v, s_s)

            def ve3(e, lsa=lsa):
                e.tensor_reduce(stat[:, 5:6], ln3[:, :], AX.X, ALU.max,
                                apply_absolute_value=True)
                self.vfence(e)
                e.tensor_scalar_max(stat[:, 6:7], stat[:, 5:6], 1e-12)
                self.vfence(e)
                e.tensor_scalar_mul(lsa, stat[:, 6:7], 1.0 / 127.0)
                self.vfence(e)
                e.reciprocal(stat[:, 7:8], lsa)
                self.vfence(e)
                e.tensor_scalar_add(self.fence_t[:, :], self.fence_t[:, :],
                                    0.0).then_inc(s_v.h, 1)
            p.em("vector", ve3)
            s_v.bump(1)

            self.emit_round_i8(ln3[:, :], stat[:, 7:8], qsgn[:, :],
                               qf2[:, :], qi[:, :], qy[:, :], qm[:, :],
                               s_v, s_s, wait_v=s_v.n)
            tp_prev = s_tp.n

            def vfin(e, tp_prev=tp_prev):
                e.wait_ge(s_tp.h, tp_prev)
                e.tensor_mul(qb[:, :], qm[:, :], qsgn[:, :]).then_inc(s_v.h, 1)
            p.em("vector", vfin)
            s_v.bump(1)
            v4 = s_v.n

            def sy3(e, i=i, v4=v4, qTa=qTa):
                e.wait_ge(s_v.h, v4)
                if i >= 2:
                    e.wait_ge(s_mm.h, 16 + 2 * (i - 1))
                for k in range(8):
                    e.dma_start_transpose(qTa[:, 128 * k:128 * k + 128],
                                          qb[:, 128 * k:128 * k + 128]
                                          ).then_inc(s_tp.h, 16)
            p.em("scalar", sy3)
            s_tp.bump(16 * 8)
            tp_snap = s_tp.n

            def phase2(i=i, qTa=qTa, tp_snap=tp_snap, x3a=x3a, lsa=lsa,
                       outta=outta):
                def te(e, i=i, qTa=qTa, tp_snap=tp_snap):
                    e.wait_ge(s_tp.h, tp_snap)
                    if i == 0:
                        e.wait_ge(s_tp.h, rhs_ready)
                    if i >= 1:
                        e.wait_ge(s_v5.h, i)
                    last = None
                    for h in range(2):
                        for k in range(8):
                            last = e.matmul(
                                pb(h), qTa[:, 128 * k:128 * k + 128],
                                qrT[:, 1024 * k + 512 * h:
                                    1024 * k + 512 * h + 512],
                                start=(k == 0), stop=(k == 7))
                    last.then_inc(s_mm.h, 2)
                p.em("tensor", te)
                s_mm.bump(2)
                mm_snap = s_mm.n

                def ve5(e, i=i, x3a=x3a, lsa=lsa, outta=outta,
                        mm_snap=mm_snap):
                    e.wait_ge(s_mm.h, mm_snap)
                    if i >= 2:
                        e.wait_ge(s_st.h, 16 * (i - 1))
                    for h in range(2):
                        ins = e.scalar_tensor_tensor(
                            outta[:, 512 * h:512 * h + 512], pb(h), lsa,
                            x3a[:, 512 * h:512 * h + 512],
                            ALU.mult, ALU.add)
                    ins.then_inc(s_v5.h, 1)
                p.em("vector", ve5)
                s_v5.bump(1)

                def sy4(e, i=i, outta=outta):
                    e.wait_ge(s_v5.h, i + 1)
                    e.dma_start(out=self.d_out[128 * i:128 * i + 128, :],
                                in_=outta).then_inc(s_st.h, 16)
                p.em("sync", sy4)
                s_st.bump(16)
            pend.append(phase2)

        for f in pend:
            f()

        def syend(e):
            e.wait_ge(s_st.h, s_st.n)
        p.em("sync", syend)
        if self.debug_taps:
            for nm, t in (("qrT", qrT), ("ln3", ln3), ("qb", qb), ("qm", qm),
                          ("qsgn", qsgn), ("qy", qy), ("qf2", qf2),
                          ("x3t", x3t), ("outt", outt), ("lsc", lsc),
                          ("stat", stat)):
                self.tap(nm, t[tuple(slice(None) for _ in t.shape)],
                         wait_sems=(s_v, s_st, s_tp))

    def tap(self, name, ap, wait_sems=()):
        if not hasattr(self, "_taps"):
            self._taps = []
        shape = list(ap.shape)
        d = self._dram("tap_" + name, shape, ap.dtype, "ExternalOutput")
        self._taps.append((d, ap, tuple(wait_sems)))

    def emit_taps(self):
        if not getattr(self, "_taps", None):
            return
        p = self.prog
        s_t = self.sem("tap_st")
        taps = self._taps

        def sy(e):
            waited = set()
            for d, ap, ws in taps:
                for s in ws:
                    key = (id(s.h), s.n)
                    if key not in waited:
                        e.wait_ge(s.h, s.n)
                        waited.add(key)
            for d, ap, ws in taps:
                e.dma_start(out=d[tuple(slice(None) for _ in d.shape)], in_=ap).then_inc(s_t.h, 16)
            e.wait_ge(s_t.h, 16 * len(taps))
        p.em("sync", sy)
        s_t.bump(16 * len(taps))

    def finalize(self):
        self.emit_taps()
        with self.nc.Block() as block:
            self.prog.play(block)
        self.es.close()
        return self.nc


def build(stages=("A", "B", "C", "D")):
    k = K(stages)
    k.emit_init()
    if "A" in stages:
        k.emit_stage_A()
    if "B" in stages:
        k.dep_B = [(k.A_done[0], k.A_done[1])] if "A" in stages else []
        k.emit_stage_B()
    if "C" in stages:
        k.dep_C = [(k.B_done[0], k.B_done[1])] if "B" in stages else []
        k.emit_stage_C()
    if "D" in stages:
        k.dep_D = [(k.C_done[0], k.C_done[1])] if "C" in stages else []
        k.emit_stage_D()
    k.finalize()
    return k


_CACHE = {}


def kernel(**inputs):
    if "full" not in _CACHE:
        _CACHE["full"] = build()
    k = _CACHE["full"]
    x = np.ascontiguousarray(inputs["L0_attn_x"], dtype=np.float32)
    common = {
        "ln_g": inputs["L0_attn_ln_g"], "ln_b": inputs["L0_attn_ln_b"],
        "Wq": inputs["L0_attn_Wq"], "Wk": inputs["L0_attn_Wk"],
        "Wv": inputs["L0_attn_Wv"], "Wo": inputs["L0_attn_Wo"],
        "w_gate": inputs["L0_ffn_w_gate"], "w_up": inputs["L0_ffn_w_up"],
        "w_down": inputs["L0_ffn_w_down"],
        "Mq": inputs["L1_attn_Mq"], "Mk": inputs["L1_attn_Mk"],
        "Mv": inputs["L1_attn_Mv"], "rhs": inputs["L1_ffn_rhs"],
    }
    common = {kk: np.ascontiguousarray(v, dtype=np.float32)
              for kk, v in common.items()}
    in_maps = [dict(common, x=x[b]) for b in range(8)]
    res = run_bass_kernel_spmd(k.nc, in_maps, core_ids=list(range(8)))
    return np.stack([res.results[b]["out"] for b in range(8)])
